# version 02
# baseline (speedup 1.0000x reference)
import numpy as np
import ml_dtypes
import concourse.bass as bass
import concourse.mybir as mybir
from concourse.bass_utils import run_bass_kernel_spmd
from contextlib import ExitStack

F32 = mybir.dt.float32
BF16 = mybir.dt.bfloat16
AF = mybir.ActivationFunctionType
ALU = mybir.AluOpType
AX = mybir.AxisListType
BF = ml_dtypes.bfloat16

T = 4096
D = 1024
DI = 2048
NS = 16
DTR = 64
NG = DI // 128
CH = 512
NCHK = T // CH
EPS = 1e-5
NT = T // 128
ENGS = ("sync", "tensor", "vector", "scalar", "gpsimd")
COMPUTE_CTR = {"sync": "q_sync", "tensor": "c_te", "vector": "c_ve",
               "scalar": "c_se", "gpsimd": "q_gp"}


class Ctr:
    def __init__(self, handle):
        self.h = handle
        self.n = 0


ARENA_BYTES = 196608


class Alloc:
    def __init__(self, nc, limit=None):
        arena = nc.alloc_sbuf_tensor("manual_arena", [128, ARENA_BYTES],
                                     mybir.dt.uint8)
        base = nc.lookup_mloc(arena).addr
        self.nc, self.cur, self.uid = nc, base, 0
        self.limit = base + ARENA_BYTES

    def tile(self, name, shape, dtype, align=64):
        self.cur = (self.cur + align - 1) // align * align
        nb = int(np.prod(shape[1:])) * mybir.dt.size(dtype)
        t = self.nc.alloc_sbuf_tensor_at(f"{name}_{self.uid}", list(shape),
                                         dtype, offset=self.cur)
        self.uid += 1
        self.cur += nb
        assert self.cur <= self.limit, f"SBUF overflow at {name}: {self.cur}"
        return t.ap()

    def mark(self):
        return self.cur

    def reset(self, pos):
        self.cur = pos


class B:
    def __init__(self):
        self.nc = bass.Bass("TRN2", target_bir_lowering=False)
        self.ctx = ExitStack()
        self.al = Alloc(self.nc)
        self.sems = {}
        self.psum = [self.nc.alloc_psum_tensor(f"pb{i}", [128, 512], F32).ap()
                     for i in range(8)]
        self.q = {e: [] for e in ENGS}
        self.lastw = {}
        self.readers = {}
        self.watermark = {e: {} for e in ENGS}

    def ctr(self, name):
        if name not in self.sems:
            self.sems[name] = Ctr(self.ctx.enter_context(self.nc.semaphore(name)))
        return self.sems[name]

    def din(self, name, shape, dtype):
        return self.nc.dram_tensor(name, list(shape), dtype,
                                   kind="ExternalInput").ap()

    def dout(self, name, shape, dtype):
        return self.nc.dram_tensor(name, list(shape), dtype,
                                   kind="ExternalOutput").ap()

    def dint(self, name, shape, dtype):
        return self.nc.dram_tensor(name, list(shape), dtype,
                                   kind="Internal").ap()

    def raw(self, eng, fn):
        self.q[eng].append(fn)

    def _wait(self, eng, ctrname, val):
        if val <= 0:
            return
        wm = self.watermark[eng]
        if wm.get(ctrname, 0) >= val:
            return
        wm[ctrname] = val
        h = self.ctr(ctrname).h
        self.q[eng].append(lambda E, h=h, v=val: E.wait_ge(h, v))

    def op(self, eng, fn, reads=(), writes=(), dma=False):
        need = {}
        def _need(c, v):
            if need.get(c, 0) < v:
                need[c] = v
        for r in reads:
            lw = self.lastw.get(r)
            if lw:
                _need(lw[0], lw[1])
        for w in writes:
            lw = self.lastw.get(w)
            if lw:
                _need(lw[0], lw[1])
            for rc, rv in self.readers.get(w, ()):
                _need(rc, rv)
        for c, v in need.items():
            self._wait(eng, c, v)
        cname = COMPUTE_CTR[eng] if not dma else {
            "sync": "q_sync", "gpsimd": "q_gp", "scalar": "q_scl"}[eng]
        c = self.ctr(cname)
        k = 16 if dma else 1
        c.n += k
        self.q[eng].append(lambda E, fn=fn, h=c.h, k=k: fn(E).then_inc(h, k))
        mile = (cname, c.n)
        if not dma and eng in ("vector", "tensor"):
            self.watermark[eng][cname] = c.n
        for r in reads:
            lst = self.readers.setdefault(r, [])
            lst[:] = [m for m in lst if m[0] != cname]
            lst.append(mile)
        for w in writes:
            self.lastw[w] = mile
            self.readers[w] = []
        return mile

    def dma(self, eng, out, in_, reads=(), writes=(), **kw):
        return self.op(eng, lambda E, o=out, i=in_, kw=kw:
                       E.dma_start(out=o, in_=i, **kw),
                       reads=reads, writes=writes, dma=True)

    def wait_mile(self, eng, mile):
        self._wait(eng, mile[0], mile[1])

    def barrier(self, dummy):
        bar = self.ctr("bar")
        ms = []
        self.q["tensor"].append(lambda E, h=bar.h: E.nop().then_inc(h, 1))
        self.q["vector"].append(lambda E, h=bar.h, d=dummy:
                                E.memset(d[:, 0:2], 0.0).then_inc(h, 1))
        self.q["scalar"].append(lambda E, h=bar.h, d=dummy:
                                E.copy(out=d[:, 2:4], in_=d[:, 0:2]).then_inc(h, 1))
        self.q["gpsimd"].append(lambda E, h=bar.h, d=dummy:
                                E.memset(d[:, 4:6], 0.0).then_inc(h, 1))
        bar.n += 4
        tgt = bar.n
        dma_targets = [(n, self.sems[n].n) for n in ("q_sync", "q_gp", "q_scl")
                       if n in self.sems]
        for e in ENGS:
            self.q[e].append(lambda E, h=bar.h, v=tgt: E.wait_ge(h, v))
            self.watermark[e]["bar"] = tgt
            for n, v in dma_targets:
                self._wait(e, n, v)
        self.lastw = {}
        self.readers = {}

    def emit(self):
        blk = self.ctx.enter_context(self.nc.Block())
        for e in ENGS:
            fns = self.q[e]
            if fns:
                getattr(blk, e)(lambda E, fns=fns: [f(E) for f in fns])



def emit_ln_phase(b, io, *, first, make_xn2, final):
    al = b.al
    mark = al.mark()
    G = 4
    xb = [al.tile("ln_x", [128, D], F32) for _ in range(G)]
    if not first:
        sb = [al.tile("ln_s", [128, D], F32) for _ in range(G)]
        ab = [al.tile("ln_a", [128, D], F32) for _ in range(G)]
        gb = [al.tile("ln_g", [128, D], F32) for _ in range(G)]
        x1 = [al.tile("ln_x1", [128, D], F32) for _ in range(G)]
    sq = [al.tile("ln_sq", [128, D], F32) for _ in range(G)]
    xn = [al.tile("ln_xn", [128, D], BF16) for _ in range(G)]
    co = [al.tile("ln_co", [128, 16], F32) for _ in range(G)]
    if make_xn2:
        xn2 = [al.tile("ln_xn2", [128, D], BF16) for _ in range(G)]
        t2 = [al.tile("ln_t2", [128, D], F32) for _ in range(G)]
        co2 = [al.tile("ln_co2", [128, 16], F32) for _ in range(G)]
        lnw = al.tile("lnw_rep", [128, D], F32)
        lnb = al.tile("lnb_rep", [128, D], F32)
        b.dma("sync", lnw, io["ln_w_rep"], writes=("lnw",))
        b.dma("sync", lnb, io["ln_b_rep"], writes=("lnb",))

    def ln_chain(tiles, src_of, src_key, out_of, out_key, cob, sqb, affine, pfx):
        K = lambda nm, i: (pfx + nm, i % G)
        for i in tiles:
            c = cob[i % G]
            b.op("vector", lambda E, s=src_of(i), c=c:
                 E.tensor_reduce(out=c[:, 0:1], in_=s, axis=AX.X, op=ALU.add),
                 reads=(src_key(i),), writes=(K("c0", i),))
        for i in tiles:
            c = cob[i % G]
            b.op("scalar", lambda E, c=c:
                 E.activation(out=c[:, 1:2], in_=c[:, 0:1], func=AF.Copy,
                              scale=1.0 / D),
                 reads=(K("c0", i),), writes=(K("mu", i),))
            b.op("scalar", lambda E, s=src_of(i), q=sqb[i % G]:
                 E.activation(out=q, in_=s, func=AF.Square),
                 reads=(src_key(i),), writes=(K("sq", i),))
        for i in tiles:
            c = cob[i % G]
            b.op("vector", lambda E, q=sqb[i % G], c=c:
                 E.tensor_reduce(out=c[:, 2:3], in_=q, axis=AX.X, op=ALU.add),
                 reads=(K("sq", i),), writes=(K("c2", i),))
        for i in tiles:
            c = cob[i % G]
            b.op("scalar", lambda E, c=c:
                 E.activation(out=c[:, 3:4], in_=c[:, 2:3], func=AF.Copy,
                              scale=1.0 / D),
                 reads=(K("c2", i),), writes=(K("e2", i),))
            b.op("scalar", lambda E, c=c:
                 E.activation(out=c[:, 4:5], in_=c[:, 1:2], func=AF.Square),
                 reads=(K("mu", i),), writes=(K("m2", i),))
        for i in tiles:
            c = cob[i % G]
            b.op("vector", lambda E, c=c:
                 E.tensor_tensor(out=c[:, 5:6], in0=c[:, 3:4], in1=c[:, 4:5],
                                 op=ALU.subtract),
                 reads=(K("e2", i), K("m2", i)), writes=(K("var", i),))
        for i in tiles:
            c = cob[i % G]
            b.op("scalar", lambda E, c=c:
                 E.activation(out=c[:, 7:8], in_=c[:, 5:6], func=AF.Copy,
                              bias=EPS),
                 reads=(K("var", i),), writes=(K("veb", i),))
        for i in tiles:
            c = cob[i % G]
            b.op("vector", lambda E, c=c:
                 E.reciprocal(out=c[:, 8:9], in_=c[:, 7:8]),
                 reads=(K("veb", i),), writes=(K("rcp", i),))
        for i in tiles:
            c = cob[i % G]
            b.op("scalar", lambda E, c=c:
                 E.activation(out=c[:, 9:10], in_=c[:, 8:9], func=AF.Sqrt),
                 reads=(K("rcp", i),), writes=(K("rstd", i),))
        for i in tiles:
            c = cob[i % G]
            if affine is None:
                b.op("vector", lambda E, s=src_of(i), o=out_of(i), c=c:
                     E.tensor_scalar(out=o, in0=s, scalar1=c[:, 1:2],
                                     scalar2=c[:, 9:10], op0=ALU.subtract,
                                     op1=ALU.mult),
                     reads=(K("mu", i), K("rstd", i), src_key(i)),
                     writes=(out_key(i),))
            else:
                g_r, b_r = affine
                b.op("vector", lambda E, s=src_of(i), c=c, t=t2[i % G]:
                     E.tensor_scalar(out=t, in0=s, scalar1=c[:, 1:2],
                                     scalar2=c[:, 9:10], op0=ALU.subtract,
                                     op1=ALU.mult),
                     reads=(K("mu", i), K("rstd", i), src_key(i)),
                     writes=(K("t2", i),))
                b.op("vector", lambda E, t=t2[i % G], q=sqb[i % G]:
                     E.tensor_tensor(out=q, in0=t, in1=g_r, op=ALU.mult),
                     reads=(K("t2", i), "lnw"), writes=(K("sq", i),))
                b.op("vector", lambda E, q=sqb[i % G], o=out_of(i):
                     E.tensor_tensor(out=o, in0=q, in1=b_r, op=ALU.add),
                     reads=(K("sq", i), "lnb"), writes=(out_key(i),))

    for g0 in range(0, NT, G):
        tiles = list(range(g0, min(g0 + G, NT)))
        for i in tiles:
            s = i % G
            b.dma("sync", xb[s], io["x_in"][i * 128:(i + 1) * 128, :],
                  writes=(("xb", s),))
            if not first:
                b.dma("sync", sb[s], io["s_in"][i * 128:(i + 1) * 128, :],
                      writes=(("sb", s),))
                b.dma("sync", ab[s], io["acc_in"][i * 128:(i + 1) * 128, :],
                      writes=(("ab", s),))
        if not first:
            for i in tiles:
                s = i % G
                b.op("scalar", lambda E, s=s:
                     E.activation(out=gb[s], in_=ab[s], func=AF.Silu),
                     reads=(("ab", s),), writes=(("gb", s),))
            for i in tiles:
                s = i % G
                b.op("vector", lambda E, s=s:
                     E.tensor_tensor(out=sq[s], in0=sb[s], in1=gb[s], op=ALU.mult),
                     reads=(("sb", s), ("gb", s)), writes=(("Lsq", s),))
                b.op("vector", lambda E, s=s:
                     E.tensor_tensor(out=x1[s], in0=xb[s], in1=sq[s], op=ALU.add),
                     reads=(("Lsq", s), ("xb", s)), writes=(("x1", s),))
            for i in tiles:
                s = i % G
                b.dma("sync", io["x_out"][i * 128:(i + 1) * 128, :], x1[s],
                      reads=(("x1", s),))
            src_of = lambda i: x1[i % G]
            src_key = lambda i: ("x1", i % G)
        else:
            src_of = lambda i: xb[i % G]
            src_key = lambda i: ("xb", i % G)
        if not final:
            ln_chain(tiles, src_of, src_key, lambda i: xn[i % G],
                     lambda i: ("xn", i % G), co, sq, None, "A")
            for i in tiles:
                b.dma("sync", io["xn_out"][i * 128:(i + 1) * 128, :], xn[i % G],
                      reads=(("xn", i % G),))
            if make_xn2:
                ln_chain(tiles, lambda i: xn[i % G], lambda i: ("xn", i % G),
                         lambda i: xn2[i % G], lambda i: ("xn2", i % G),
                         co2, sq, (lnw, lnb), "B")
                for i in tiles:
                    b.dma("sync", io["xn2_out"][i * 128:(i + 1) * 128, :],
                          xn2[i % G], reads=(("xn2", i % G),))
    al.reset(mark)



def emit_ffn(b, io, *, layer):
    al = b.al
    mark = al.mark()
    E_ = 16 if layer == 0 else 64
    QT = 1024
    NQ = T // QT
    QTT = QT // 128
    NCH2 = QT // 512

    xnfm = al.tile("xnfm", [128, 8 * QT], BF16)
    acc = [al.tile("acc", [128, QTT * D], F32) for _ in range(2)]
    hsb = al.tile("hsb", [128, 8 * QT], BF16)
    sgb = al.tile("sgb", [128, 512], BF16)
    uub = al.tile("uub", [128, 512], BF16)
    tmpd = al.tile("tmpd", [128, 512], F32)
    comb = al.tile("comb", [128, QTT * E_], F32)
    combb = al.tile("combb", [128, QTT * E_], F32)
    rsc = al.tile("rsc", [128, 5 * E_ + 96], F32)
    gbr = al.tile("gbr", [128, E_], F32)
    gwt = al.tile("gwt", [128, 8 * E_], BF16)
    dummy = al.tile("ffdum", [128, 16], F32)
    if layer == 0:
        wg = [al.tile("w13t", [128, 8 * 1024], BF16) for _ in range(2)]
        wd = [al.tile("w2t", [128, 4 * 1024], BF16) for _ in range(2)]
    else:
        weg = [al.tile("egt", [128, 8 * 512], BF16) for _ in range(2)]
        weu = [al.tile("eut", [128, 8 * 512], BF16) for _ in range(2)]
        wed = [al.tile("edt", [128, 4 * 1024], BF16) for _ in range(2)]

    scr = io["scr"]
    pm = io["prep_marks"]

    b.dma("sync", gbr, io["gate_b_rep"], writes=("gbr",))
    b.wait_mile("sync", pm["gate_w"])
    for kc in range(8):
        b.dma("sync", gwt[:, kc * E_:(kc + 1) * E_],
              scr["gate_w"][0:E_, kc * 128:(kc + 1) * 128], transpose=True,
              writes=("gwt",))

    def load_shared():
        if layer == 0:
            b.wait_mile("sync", pm["w13_s"])
            for kc in range(8):
                b.dma("sync", wg[0][:, kc * 1024:(kc + 1) * 1024],
                      scr["w13_s"][0:1024, kc * 128:(kc + 1) * 128],
                      transpose=True, writes=("wslot0",))
            b.wait_mile("sync", pm["w2_s"])
            for kc in range(4):
                b.dma("sync", wd[0][:, kc * 1024:(kc + 1) * 1024],
                      scr["w2_s"][0:1024, kc * 128:(kc + 1) * 128],
                      transpose=True, writes=("wslot0",))
        else:
            pass

    def load_shared1(half, slot):
        w = f"wslot{slot}"
        i0 = half * 512
        b.wait_mile("sync", pm["sg"])
        for kc in range(8):
            b.dma("sync", weg[slot][:, kc * 512:(kc + 1) * 512],
                  scr["sg"][i0:i0 + 512, kc * 128:(kc + 1) * 128],
                  transpose=True, writes=(w,))
        b.wait_mile("sync", pm["su"])
        for kc in range(8):
            b.dma("sync", weu[slot][:, kc * 512:(kc + 1) * 512],
                  scr["su"][i0:i0 + 512, kc * 128:(kc + 1) * 128],
                  transpose=True, writes=(w,))
        b.wait_mile("sync", pm["sd"])
        for kc in range(4):
            b.dma("sync", wed[slot][:, kc * 1024:(kc + 1) * 1024],
                  scr["sd"][0:1024, i0 + kc * 128:i0 + (kc + 1) * 128],
                  transpose=True, writes=(w,))

    def load_expert_slot(e, slot):
        w = f"wslot{slot}"
        if layer == 0:
            b.wait_mile("sync", pm["w13"][e])
            for kc in range(8):
                b.dma("sync", wg[slot][:, kc * 1024:(kc + 1) * 1024],
                      scr["w13"][e, 0:1024, kc * 128:(kc + 1) * 128],
                      transpose=True, writes=(w,))
            b.wait_mile("sync", pm["w2"][e])
            for kc in range(4):
                b.dma("sync", wd[slot][:, kc * 1024:(kc + 1) * 1024],
                      scr["w2"][e, 0:1024, kc * 128:(kc + 1) * 128],
                      transpose=True, writes=(w,))
        else:
            b.wait_mile("sync", pm["eg"][e])
            for kc in range(8):
                b.dma("sync", weg[slot][:, kc * 512:(kc + 1) * 512],
                      scr["eg"][e, 0:512, kc * 128:(kc + 1) * 128],
                      transpose=True, writes=(w,))
            b.wait_mile("sync", pm["eu"][e])
            for kc in range(8):
                b.dma("sync", weu[slot][:, kc * 512:(kc + 1) * 512],
                      scr["eu"][e, 0:512, kc * 128:(kc + 1) * 128],
                      transpose=True, writes=(w,))
            b.wait_mile("sync", pm["ed"][e])
            for kc in range(4):
                b.dma("sync", wed[slot][:, kc * 1024:(kc + 1) * 1024],
                      scr["ed"][e, 0:1024, kc * 128:(kc + 1) * 128],
                      transpose=True, writes=(w,))

    def routing(q):
        for tt in range(QTT):
            bank = b.psum[tt % 2]
            bres = f"pbank{tt % 2}"
            mm = b.op("tensor", lambda E, bank=bank, tt=tt: [
                E.matmul(bank[:, 0:E_],
                         xnfm[:, kc * QT + tt * 128: kc * QT + (tt + 1) * 128],
                         gwt[:, kc * E_:(kc + 1) * E_],
                         start=(kc == 0), stop=(kc == 7))
                for kc in range(8)][-1],
                reads=("xnfm", "gwt"), writes=(bres,))
            sc_t = rsc[:, 0:E_]
            b.op("scalar", lambda E, bank=bank, o=sc_t:
                 E.activation(out=o, in_=bank[:, 0:E_], func=AF.Sigmoid),
                 reads=(bres,), writes=(("sc",),))
            bi = rsc[:, E_:2 * E_]
            b.op("vector", lambda E, o=bi, a=sc_t:
                 E.tensor_tensor(out=o, in0=a, in1=gbr, op=ALU.add),
                 reads=(("sc",), "gbr", bres), writes=(("bi",),))
            bi2 = rsc[:, 2 * E_:3 * E_]
            b.op("scalar", lambda E, o=bi2, a=bi: E.copy(out=o, in_=a),
                 reads=(("bi",),), writes=(("bi2",),))
            x8 = rsc[:, 4 * E_:4 * E_ + 8]
            if layer == 0:
                b.op("vector", lambda E, a=bi2, o=x8: E.max(out=o, in_=a),
                     reads=(("bi2",),), writes=(("x8",),))
                th = rsc[:, 4 * E_ + 8:4 * E_ + 9]
                b.op("scalar", lambda E, o=th, a=x8: E.copy(out=o, in_=a[:, 3:4]),
                     reads=(("x8",),), writes=(("th",),))
                sel = rsc[:, 3 * E_:4 * E_]
                b.op("vector", lambda E, o=sel, a=bi2, t=th:
                     E.tensor_scalar(out=o, in0=a, scalar1=t, scalar2=None,
                                     op0=ALU.is_ge),
                     reads=(("bi2",), ("th",)), writes=(("sel",),))
                selb = rsc[:, 4 * E_ + 80:5 * E_ + 80]
                b.op("scalar", lambda E, o=selb, a=sel: E.copy(out=o, in_=a),
                     reads=(("sel",),), writes=(("selb",),))
                w_ = rsc[:, 4 * E_ + 16:5 * E_ + 16]
                b.op("vector", lambda E, o=w_, a=sc_t, s=selb:
                     E.tensor_tensor(out=o, in0=a, in1=s, op=ALU.mult),
                     reads=(("selb",), ("sc",)), writes=(("w",),))
                w2_ = rsc[:, 5 * E_ + 16:6 * E_ + 16]
                b.op("scalar", lambda E, o=w2_, a=w_: E.copy(out=o, in_=a),
                     reads=(("w",),), writes=(("w2",),))
                sm = rsc[:, 4 * E_ + 9:4 * E_ + 10]
                b.op("vector", lambda E, o=sm, a=w2_:
                     E.tensor_reduce(out=o, in_=a, axis=AX.X, op=ALU.add),
                     reads=(("w2",),), writes=(("sm",),))
                sm2 = rsc[:, 4 * E_ + 10:4 * E_ + 11]
                b.op("scalar", lambda E, o=sm2, a=sm: E.copy(out=o, in_=a),
                     reads=(("sm",),), writes=(("sm2",),))
                rc = rsc[:, 4 * E_ + 11:4 * E_ + 12]
                b.op("vector", lambda E, o=rc, a=sm2: E.reciprocal(out=o, in_=a),
                     reads=(("sm2",),), writes=(("rc",),))
                rc2 = rsc[:, 4 * E_ + 12:4 * E_ + 13]
                b.op("scalar", lambda E, o=rc2, a=rc:
                     E.activation(out=o, in_=a, func=AF.Copy, scale=2.5),
                     reads=(("rc",),), writes=(("rc2",),))
                b.op("vector", lambda E, o=comb[:, tt * E_:(tt + 1) * E_], a=w_,
                     r=rc2:
                     E.tensor_scalar(out=o, in0=a, scalar1=r, scalar2=None,
                                     op0=ALU.mult),
                     reads=(("w",), ("rc2",)), writes=(("comb",),))
                if tt == 0 and q == 0 and "dbg_out" in io:
                    b.dma("sync", io["dbg_out"], rsc,
                          reads=(("comb",), ("sc",), ("bi",), ("bi2",),
                                 ("sel",), ("x8",), ("w",), ("w2",)))
            else:
                gm = rsc[:, 4 * E_ + 16:4 * E_ + 24]
                b.op("vector", lambda E, a=bi2, o=gm:
                     E.tensor_reduce(out=o,
                                     in_=a.rearrange("p (g e) -> p g e", e=8),
                                     axis=AX.X, op=ALU.max),
                     reads=(("bi2",),), writes=(("gm",),))
                gm2 = rsc[:, 4 * E_ + 24:4 * E_ + 32]
                b.op("scalar", lambda E, o=gm2, a=gm: E.copy(out=o, in_=a),
                     reads=(("gm",),), writes=(("gm2",),))
                b.op("vector", lambda E, a=gm2, o=x8: E.max(out=o, in_=a),
                     reads=(("gm2",),), writes=(("x8",),))
                thg = rsc[:, 4 * E_ + 8:4 * E_ + 9]
                b.op("scalar", lambda E, o=thg, a=x8: E.copy(out=o, in_=a[:, 3:4]),
                     reads=(("x8",),), writes=(("thg",),))
                gmask = rsc[:, 4 * E_ + 32:4 * E_ + 40]
                b.op("vector", lambda E, o=gmask, a=gm2, t=thg:
                     E.tensor_scalar(out=o, in0=a, scalar1=t, scalar2=None,
                                     op0=ALU.is_ge),
                     reads=(("gm2",), ("thg",)), writes=(("gmk",),))
                gmask2 = rsc[:, 4 * E_ + 40:4 * E_ + 48]
                b.op("scalar", lambda E, o=gmask2, a=gmask: E.copy(out=o, in_=a),
                     reads=(("gmk",),), writes=(("gmk2",),))
                masked = rsc[:, 3 * E_:4 * E_]
                b.op("vector", lambda E, o=masked, a=bi2, s=gmask2: [
                     E.tensor_scalar(out=o[:, g * 8:(g + 1) * 8],
                                     in0=a[:, g * 8:(g + 1) * 8],
                                     scalar1=s[:, g:g + 1], scalar2=None,
                                     op0=ALU.mult) for g in range(8)][-1],
                     reads=(("bi2",), ("gmk2",)), writes=(("msk",),))
                masked2 = comb[:, tt * E_:(tt + 1) * E_]
                b.op("scalar", lambda E, o=masked2, a=masked: E.copy(out=o, in_=a),
                     reads=(("msk",),), writes=(("msk2",),))
                b.op("vector", lambda E, a=masked2, o=x8: E.max(out=o, in_=a),
                     reads=(("msk2",),), writes=(("x8",),))
                th6 = rsc[:, 4 * E_ + 9:4 * E_ + 10]
                b.op("scalar", lambda E, o=th6, a=x8: E.copy(out=o, in_=a[:, 5:6]),
                     reads=(("x8",),), writes=(("th6",),))
                sel = rsc[:, 0:E_]
                b.op("vector", lambda E, o=sel, a=masked2, t=th6:
                     E.tensor_scalar(out=o, in0=a, scalar1=t, scalar2=None,
                                     op0=ALU.is_ge),
                     reads=(("msk2",), ("th6",)), writes=(("sel",),))
                selb = rsc[:, 4 * E_ + 80:5 * E_ + 80]
                b.op("scalar", lambda E, o=selb, a=sel: E.copy(out=o, in_=a),
                     reads=(("sel",),), writes=(("selb",),))
                w_ = rsc[:, E_:2 * E_]
                b.op("vector", lambda E, o=w_, a=masked2, s=selb:
                     E.tensor_tensor(out=o, in0=a, in1=s, op=ALU.mult),
                     reads=(("selb",), ("msk2",)), writes=(("w",),))
                w2_ = rsc[:, 2 * E_:3 * E_]
                b.op("scalar", lambda E, o=w2_, a=w_: E.copy(out=o, in_=a),
                     reads=(("w",),), writes=(("w2",),))
                sm = rsc[:, 4 * E_ + 10:4 * E_ + 11]
                b.op("vector", lambda E, o=sm, a=w2_:
                     E.tensor_reduce(out=o, in_=a, axis=AX.X, op=ALU.add),
                     reads=(("w2",),), writes=(("sm",),))
                sm2 = rsc[:, 4 * E_ + 11:4 * E_ + 12]
                b.op("scalar", lambda E, o=sm2, a=sm:
                     E.activation(out=o, in_=a, func=AF.Copy, bias=1e-8),
                     reads=(("sm",),), writes=(("sm2",),))
                rc = rsc[:, 4 * E_ + 12:4 * E_ + 13]
                b.op("vector", lambda E, o=rc, a=sm2: E.reciprocal(out=o, in_=a),
                     reads=(("sm2",),), writes=(("rc",),))
                rc2 = rsc[:, 4 * E_ + 13:4 * E_ + 14]
                b.op("scalar", lambda E, o=rc2, a=rc: E.copy(out=o, in_=a),
                     reads=(("rc",),), writes=(("rc2",),))
                b.op("vector", lambda E, o=combb[:, tt * E_:(tt + 1) * E_],
                     a=w_, r=rc2:
                     E.tensor_scalar(out=o, in0=a, scalar1=r, scalar2=None,
                                     op0=ALU.mult),
                     reads=(("w",), ("rc2",)), writes=(("combv",),))
        src_, dst_ = (comb, combb) if layer == 0 else (combb, comb)
        b.op("scalar", lambda E, s=src_, d=dst_:
             E.copy(out=d[:, 0:QTT * E_], in_=s[:, 0:QTT * E_]),
             reads=(("comb",) if layer == 0 else ("combv",),),
             writes=("combf",))
        return dst_

    def expert_compute(kind, slot, acc_src, acc_dst, first_contrib, e_idx,
                       combf):
        if kind in ("exp0", "sh0"):
            n_inter = 512
            gm_ = lambda kc, msl: wg[slot][:, kc * 1024 + msl.start:
                                           kc * 1024 + msl.stop]
            um_ = lambda kc, msl: wg[slot][:, kc * 1024 + 512 + msl.start:
                                           kc * 1024 + 512 + msl.stop]
            dm_ = lambda kc2, nc_: wd[slot][:, kc2 * 1024 + nc_ * 512:
                                            kc2 * 1024 + (nc_ + 1) * 512]
            wres = (f"wslot{slot}",)
        elif kind == "exp1":
            n_inter = 512
            gm_ = lambda kc, msl: weg[slot][:, kc * 512 + msl.start:
                                            kc * 512 + msl.stop]
            um_ = lambda kc, msl: weu[slot][:, kc * 512 + msl.start:
                                            kc * 512 + msl.stop]
            dm_ = lambda kc2, nc_: wed[slot][:, kc2 * 1024 + nc_ * 512:
                                             kc2 * 1024 + (nc_ + 1) * 512]
            wres = (f"wslot{slot}",)
        NMT = n_inter // 128
        for mt in range(NMT):
            msl = slice(mt * 128, (mt + 1) * 128)
            for nch in range(NCH2):
                gb_ = b.psum[(2 * (mt * NCH2 + nch)) % 4]
                ub_ = b.psum[(2 * (mt * NCH2 + nch) + 1) % 4]
                gbr_ = f"pbank{(2 * (mt * NCH2 + nch)) % 4}"
                ubr_ = f"pbank{(2 * (mt * NCH2 + nch) + 1) % 4}"
                b.op("tensor", lambda E, gb_=gb_, msl=msl, nch=nch, gm_=gm_: [
                     E.matmul(gb_[:, 0:512], gm_(kc, msl),
                              xnfm[:, kc * QT + nch * 512:kc * QT + (nch + 1) * 512],
                              start=(kc == 0), stop=(kc == 7))
                     for kc in range(8)][-1],
                     reads=wres + ("xnfm",), writes=(gbr_,))
                b.op("tensor", lambda E, ub_=ub_, msl=msl, nch=nch, um_=um_: [
                     E.matmul(ub_[:, 0:512], um_(kc, msl),
                              xnfm[:, kc * QT + nch * 512:kc * QT + (nch + 1) * 512],
                              start=(kc == 0), stop=(kc == 7))
                     for kc in range(8)][-1],
                     reads=wres + ("xnfm",), writes=(ubr_,))
                b.op("scalar", lambda E, gb_=gb_:
                     E.activation(out=sgb, in_=gb_[:, 0:512], func=AF.Silu),
                     reads=(gbr_,), writes=("sgb",))
                b.op("vector", lambda E, ub_=ub_:
                     E.tensor_copy(out=uub, in_=ub_[:, 0:512]),
                     reads=(ubr_,), writes=("uub",))
                hsl = slice(mt * QT + nch * 512, mt * QT + (nch + 1) * 512)
                b.op("vector", lambda E, hsl=hsl:
                     E.tensor_tensor(out=hsb[:, hsl], in0=sgb, in1=uub,
                                     op=ALU.mult),
                     reads=("sgb", "uub"), writes=(("h", mt, nch),))
        NK2 = n_inter // 128
        hreads = tuple(("h", mt, nch) for mt in range(NMT) for nch in range(NCH2))
        for tt in range(QTT):
            for nc_ in range(2):
                bank = b.psum[4 + (tt * 2 + nc_) % 4]
                bres = f"pbank{4 + (tt * 2 + nc_) % 4}"
                b.op("tensor", lambda E, bank=bank, tt=tt, nc_=nc_, NK2=NK2,
                     dm_=dm_: [
                     E.matmul(bank[:, 0:512],
                              hsb[:, kc2 * QT + tt * 128:kc2 * QT + (tt + 1) * 128],
                              dm_(kc2, nc_),
                              start=(kc2 == 0), stop=(kc2 == NK2 - 1))
                     for kc2 in range(NK2)][-1],
                     reads=hreads + wres, writes=(bres,))
                dsl = slice(tt * D + nc_ * 512, tt * D + (nc_ + 1) * 512)
                if first_contrib:
                    b.op("vector", lambda E, bank=bank, dsl=dsl, a=acc_dst:
                         E.tensor_copy(out=a[:, dsl], in_=bank[:, 0:512]),
                         reads=(bres,), writes=(("acc", id(acc_dst), tt, nc_),))
                elif e_idx is None:
                    b.op("vector", lambda E, bank=bank, dsl=dsl, s=acc_src,
                         a=acc_dst:
                         E.tensor_tensor(out=a[:, dsl], in0=s[:, dsl],
                                         in1=bank[:, 0:512], op=ALU.add),
                         reads=(bres, ("acc", id(acc_src), tt, nc_)),
                         writes=(("acc", id(acc_dst), tt, nc_),))
                else:
                    ccol = combf[:, tt * E_ + e_idx:tt * E_ + e_idx + 1]
                    b.op("vector", lambda E, bank=bank, c=ccol:
                         E.tensor_scalar(out=tmpd, in0=bank[:, 0:512], scalar1=c,
                                         scalar2=None, op0=ALU.mult),
                         reads=(bres, "combf"), writes=("tmpd",))
                    b.op("vector", lambda E, dsl=dsl, s=acc_src, a=acc_dst:
                         E.tensor_tensor(out=a[:, dsl], in0=s[:, dsl], in1=tmpd,
                                         op=ALU.add),
                         reads=("tmpd", ("acc", id(acc_src), tt, nc_)),
                         writes=(("acc", id(acc_dst), tt, nc_),))

    if layer == 0:
        load_shared()
    for q in range(NQ):
        for kc in range(8):
            b.dma("sync", xnfm[:, kc * QT:(kc + 1) * QT],
                  io["xn_in"][q * QT:(q + 1) * QT, kc * 128:(kc + 1) * 128],
                  transpose=True, writes=("xnfm",))
        combf = routing(q)
        if "comb_out" in io:
            for tt in range(QTT):
                b.dma("sync", io["comb_out"][(q * QTT + tt) * 128:
                                             (q * QTT + tt + 1) * 128, :],
                      combf[:, tt * E_:(tt + 1) * E_], reads=("combf",))
        nexp = io.get("nexp", E_)
        if layer == 0:
            expert_compute("sh0", 0, None, acc[0], True, None, combf)
            nsh = 0
        else:
            load_shared1(0, 0)
            expert_compute("exp1", 0, None, acc[0], True, None, combf)
            load_shared1(1, 1)
            expert_compute("exp1", 1, acc[0], acc[1], False, None, combf)
            nsh = 1
        for e in range(nexp):
            slot = (e + nsh + 1) % 2
            load_expert_slot(e, slot)
            expert_compute("exp0" if layer == 0 else "exp1", slot,
                           acc[(e + nsh) % 2], acc[(e + nsh + 1) % 2], False, e,
                           combf)
        final_acc = acc[(nexp + nsh) % 2]
        for tt in range(QTT):
            b.dma("sync", io["acc_out"][(q * QTT + tt) * 128:
                                        (q * QTT + tt + 1) * 128, :],
                  final_acc[:, tt * D:(tt + 1) * D],
                  reads=(("acc", id(final_acc), tt, 0),
                         ("acc", id(final_acc), tt, 1)))
        if layer == 0:
            if q < NQ - 1:
                load_shared()
    al.reset(mark)



def emit_attn1(b, io):
    al = b.al
    mark = al.mark()
    C2 = 512
    NC2 = T // C2
    CT = C2 // 128

    projT = al.tile("projT", [128, 8 * 1024], BF16)
    f1sl = [al.tile("f1sl", [128, 8 * 128], BF16) for _ in range(2)]
    f2sl = al.tile("f2sl", [128, 32 * 256], BF16)
    g1r = al.tile("g1r", [128, D], BF16)
    g2r = al.tile("g2r", [128, D], BF16)
    pbr = al.tile("pbr", [128, D], BF16)
    f2br = al.tile("f2br", [128, D], BF16)
    l1g = al.tile("l1g", [128, D], BF16)
    l1b = al.tile("l1b", [128, D], BF16)
    l2g = al.tile("l2g", [128, D], BF16)
    l2b = al.tile("l2b", [128, D], BF16)
    lhg = al.tile("lhg", [128, 32], F32)
    lhb = al.tile("lhb", [128, 32], F32)
    f1b = al.tile("f1b", [128, 32], F32)
    ones1 = al.tile("ones1", [1, 128], BF16)
    onecol = al.tile("onecol", [128, 1], BF16)
    xnt = [al.tile("at_xn", [128, D], BF16) for _ in range(4)]
    y1t = [al.tile("at_y1", [128, D], BF16) for _ in range(2)]
    t1t = [al.tile("at_t1", [128, D], F32) for _ in range(4)]
    zt = [al.tile("at_z", [128, D], BF16) for _ in range(2)]
    s1t = [al.tile("at_s1", [128, D], F32) for _ in range(4)]
    co = [al.tile("at_co", [128, 16], F32) for _ in range(4)]
    co2 = [al.tile("at_co2", [128, 16], F32) for _ in range(4)]
    sq = [al.tile("at_sq", [128, D], F32) for _ in range(2)]
    t2a = [al.tile("at_t2", [128, D], F32) for _ in range(2)]
    y1fm = al.tile("y1fm", [128, 8 * C2], BF16)
    zfm = al.tile("zfm", [128, 8 * C2], BF16)
    hfm = al.tile("hfm", [128, 32 * C2], BF16)
    hsqt = al.tile("hsqt", [128, C2], BF16)
    rowb = al.tile("rowb", [1, 3 * C2], F32)
    rowbb = al.tile("rowbb", [1, 2 * C2], BF16)
    d1t = al.tile("at_d1", [128, C2], F32)
    d2t = al.tile("at_d2", [128, C2], F32)
    tmpv = al.tile("at_tmpv", [128, 512], F32)
    tmpw = al.tile("at_tmpw", [128, 512], F32)

    scr = io["scr"]
    pm = io["prep_marks"]

    for dst, nm in ((lhg, "lnh_g_t"), (lhb, "lnh_b_t"), (f1b, "fc1_b_t")):
        b.dma("sync", dst, io[nm], writes=("consts",))
    b.dma("sync", ones1, io["ones_row"], writes=("consts",))
    b.dma("sync", onecol, io["ones_col"], writes=("consts",))
    for dst, nm in ((g1r, "g1_rep"), (g2r, "g2_rep"), (pbr, "pb_rep"),
                    (f2br, "f2b_rep"), (l1g, "l1g_rep"), (l1b, "l1b_rep"),
                    (l2g, "l2g_rep"), (l2b, "l2b_rep")):
        b.dma("gpsimd", dst, io[nm], writes=("consts",))

    b.wait_mile("sync", pm["proj_w"])
    for kc in range(8):
        b.dma("sync", projT[:, kc * 1024:(kc + 1) * 1024],
              scr["proj_w"][0:1024, kc * 128:(kc + 1) * 128], transpose=True,
              writes=("projT",))

    def ln_tile(i, src, src_key, out, out_key, cob, gg, bb, pfx):
        c = cob
        K = lambda nm: (pfx + nm, i)
        sqb, t2b = sq[i % 2], t2a[i % 2]
        b.op("vector", lambda E: E.tensor_reduce(out=c[:, 0:1], in_=src,
                                                 axis=AX.X, op=ALU.add),
             reads=(src_key,), writes=(K("c0"),))
        b.op("scalar", lambda E: E.activation(out=c[:, 1:2], in_=c[:, 0:1],
                                              func=AF.Copy, scale=1.0 / D),
             reads=(K("c0"),), writes=(K("mu"),))
        b.op("scalar", lambda E: E.activation(out=sqb, in_=src, func=AF.Square),
             reads=(src_key,), writes=(("SQb", i % 2),))
        b.op("vector", lambda E: E.tensor_reduce(out=c[:, 2:3], in_=sqb,
                                                 axis=AX.X, op=ALU.add),
             reads=(("SQb", i % 2),), writes=(K("c2"),))
        b.op("scalar", lambda E: E.activation(out=c[:, 3:4], in_=c[:, 2:3],
                                              func=AF.Copy, scale=1.0 / D),
             reads=(K("c2"),), writes=(K("e2"),))
        b.op("scalar", lambda E: E.activation(out=c[:, 4:5], in_=c[:, 1:2],
                                              func=AF.Square),
             reads=(K("mu"),), writes=(K("m2"),))
        b.op("vector", lambda E: E.tensor_tensor(out=c[:, 5:6], in0=c[:, 3:4],
                                                 in1=c[:, 4:5], op=ALU.subtract),
             reads=(K("e2"), K("m2")), writes=(K("var"),))
        b.op("scalar", lambda E: E.activation(out=c[:, 7:8], in_=c[:, 5:6],
                                              func=AF.Copy, bias=EPS),
             reads=(K("var"),), writes=(K("veb"),))
        b.op("vector", lambda E: E.reciprocal(out=c[:, 8:9], in_=c[:, 7:8]),
             reads=(K("veb"),), writes=(K("rcp"),))
        b.op("scalar", lambda E: E.activation(out=c[:, 9:10], in_=c[:, 8:9],
                                              func=AF.Sqrt),
             reads=(K("rcp"),), writes=(K("rstd"),))
        b.op("vector", lambda E: E.tensor_scalar(out=t2b, in0=src,
                                                 scalar1=c[:, 1:2],
                                                 scalar2=c[:, 9:10],
                                                 op0=ALU.subtract, op1=ALU.mult),
             reads=(K("mu"), K("rstd"), src_key), writes=(("T2b", i % 2),))
        b.op("vector", lambda E: E.tensor_tensor(out=sqb, in0=t2b, in1=gg,
                                                 op=ALU.mult),
             reads=(("T2b", i % 2), "consts"), writes=(("SQb", i % 2),))
        b.op("vector", lambda E: E.tensor_tensor(out=out, in0=sqb, in1=bb,
                                                 op=ALU.add),
             reads=(("SQb", i % 2), "consts"), writes=(out_key,))

    for ch in range(NC2):
        t0 = ch * C2
        for i in range(CT):
            b.dma("sync", xnt[i], io["xn_in"][t0 + i * 128:t0 + (i + 1) * 128, :],
                  writes=(("axn", i),))
        for i in range(CT):
            ln_tile(i, xnt[i], ("axn", i), y1t[i % 2], ("ay1", i % 2), co[i],
                    l1g, l1b, "p")
            b.dma("sync", io["y1tm"][t0 + i * 128:t0 + (i + 1) * 128, :],
                  y1t[i % 2], reads=(("ay1", i % 2),), writes=(("y1tm", i),))
        for kc in range(8):
            b.dma("sync", y1fm[:, kc * C2:(kc + 1) * C2],
                  io["y1tm"][t0:t0 + C2, kc * 128:(kc + 1) * 128], transpose=True,
                  reads=tuple(("y1tm", i) for i in range(CT)), writes=("y1fm",))
        for i in range(CT):
            for nc_ in range(2):
                bank = b.psum[(i * 2 + nc_) % 4]
                bres = f"pk{(i * 2 + nc_) % 4}"
                b.op("tensor", lambda E, bank=bank, i=i, nc_=nc_: [
                     E.matmul(bank[:, 0:512],
                              y1fm[:, kc * C2 + i * 128:kc * C2 + (i + 1) * 128],
                              projT[:, kc * 1024 + nc_ * 512:
                                    kc * 1024 + (nc_ + 1) * 512],
                              start=(kc == 0), stop=(kc == 7))
                     for kc in range(8)][-1],
                     reads=("y1fm", "projT"), writes=(bres,))
                dsl = slice(nc_ * 512, (nc_ + 1) * 512)
                b.op("vector", lambda E, bank=bank, dsl=dsl:
                     E.tensor_tensor(out=tmpv, in0=bank[:, 0:512],
                                     in1=pbr[:, dsl], op=ALU.add),
                     reads=(bres, "consts"), writes=(("tmpv",),))
                b.op("vector", lambda E, dsl=dsl:
                     E.tensor_tensor(out=tmpw, in0=tmpv, in1=g1r[:, dsl],
                                     op=ALU.mult),
                     reads=(("tmpv",), "consts"), writes=(("tmpw",),))
                b.op("vector", lambda E, i=i, dsl=dsl:
                     E.tensor_tensor(out=t1t[i][:, dsl], in0=tmpw,
                                     in1=xnt[i][:, dsl], op=ALU.add),
                     reads=(("tmpw",), ("axn", i)), writes=(("t1", i),))
        for i in range(CT):
            ln_tile(i, t1t[i], ("t1", i), zt[i % 2], ("az", i % 2), co2[i],
                    l2g, l2b, "q")
            b.dma("sync", io["ztm"][t0 + i * 128:t0 + (i + 1) * 128, :], zt[i % 2],
                  reads=(("az", i % 2),), writes=(("ztm", i),))
        for kc in range(8):
            b.dma("sync", zfm[:, kc * C2:(kc + 1) * C2],
                  io["ztm"][t0:t0 + C2, kc * 128:(kc + 1) * 128], transpose=True,
                  reads=tuple(("ztm", i) for i in range(CT)), writes=("zfm",))
        for mt in range(32):
            sl = mt % 2
            b.wait_mile("sync", pm["fc1_w"])
            for kc in range(8):
                b.dma("sync", f1sl[sl][:, kc * 128:(kc + 1) * 128],
                      scr["fc1_w"][mt * 128:(mt + 1) * 128,
                                   kc * 128:(kc + 1) * 128],
                      transpose=True, writes=((f"f1sl{sl}",),))
            bank = b.psum[mt % 4]
            bres = f"pk{mt % 4}"
            b.op("tensor", lambda E, bank=bank, sl=sl: [
                 E.matmul(bank[:, 0:C2], f1sl[sl][:, kc * 128:(kc + 1) * 128],
                          zfm[:, kc * C2:(kc + 1) * C2],
                          start=(kc == 0), stop=(kc == 7))
                 for kc in range(8)][-1],
                 reads=("zfm", (f"f1sl{sl}",)), writes=(bres,))
            b.op("scalar", lambda E, bank=bank, mt=mt:
                 E.activation(out=hfm[:, mt * C2:(mt + 1) * C2],
                              in_=bank[:, 0:C2], func=AF.Gelu,
                              bias=f1b[:, mt:mt + 1]),
                 reads=(bres, "consts"), writes=(("hf", mt),))
        hreads = tuple(("hf", mt) for mt in range(32))
        b.op("tensor", lambda E: [
             E.matmul(b.psum[4][0:1, 0:C2], onecol[:, 0:1],
                      hfm[:, mt * C2:(mt + 1) * C2],
                      start=(mt == 0), stop=(mt == 31))
             for mt in range(32)][-1],
             reads=hreads + ("consts",), writes=("pk4",))
        for mt in range(32):
            b.op("scalar", lambda E, mt=mt:
                 E.activation(out=hsqt, in_=hfm[:, mt * C2:(mt + 1) * C2],
                              func=AF.Square),
                 reads=(("hf", mt),), writes=("hsqt",))
            b.op("tensor", lambda E, mt=mt:
                 E.matmul(b.psum[5][0:1, 0:C2], onecol[:, 0:1], hsqt,
                          start=(mt == 0), stop=(mt == 31)),
                 reads=("hsqt", "consts"),
                 writes=("pk5",) if mt in (0, 31) else ())
        b.op("scalar", lambda E: E.activation(out=rowb[0:1, 0:C2],
                                              in_=b.psum[4][0:1, 0:C2],
                                              func=AF.Copy, scale=1.0 / 4096),
             reads=("pk4",), writes=("rmu",))
        b.op("vector", lambda E: E.tensor_copy(out=rowbb[0:1, 0:C2],
                                               in_=rowb[0:1, 0:C2]),
             reads=("rmu",), writes=("rbmu",))
        b.op("scalar", lambda E: E.activation(out=rowb[0:1, C2:2 * C2],
                                              in_=b.psum[5][0:1, 0:C2],
                                              func=AF.Copy, scale=1.0 / 4096),
             reads=("pk5",), writes=("re2",))
        b.op("scalar", lambda E: E.activation(out=rowb[0:1, 2 * C2:3 * C2],
                                              in_=rowb[0:1, 0:C2],
                                              func=AF.Square),
             reads=("rmu", "rbmu"), writes=("rm2",))
        b.op("vector", lambda E: E.tensor_tensor(out=rowb[0:1, 0:C2],
                                                 in0=rowb[0:1, C2:2 * C2],
                                                 in1=rowb[0:1, 2 * C2:3 * C2],
                                                 op=ALU.subtract),
             reads=("re2", "rm2", "rbmu"), writes=("rvar",))
        b.op("scalar", lambda E: E.activation(out=rowb[0:1, C2:2 * C2],
                                              in_=rowb[0:1, 0:C2],
                                              func=AF.Copy, bias=EPS),
             reads=("rvar",), writes=("rveb",))
        b.op("vector", lambda E: E.reciprocal(out=rowb[0:1, 0:C2],
                                              in_=rowb[0:1, C2:2 * C2]),
             reads=("rveb",), writes=("rrcp",))
        b.op("scalar", lambda E: E.activation(out=rowb[0:1, 2 * C2:3 * C2],
                                              in_=rowb[0:1, 0:C2],
                                              func=AF.Sqrt),
             reads=("rrcp",), writes=("rrstd",))
        b.op("vector", lambda E: E.tensor_copy(out=rowbb[0:1, C2:2 * C2],
                                               in_=rowb[0:1, 2 * C2:3 * C2]),
             reads=("rrstd",), writes=("rbrstd",))
        b.op("tensor", lambda E: E.matmul(b.psum[6][:, 0:C2], ones1[0:1, 0:128],
                                          rowbb[0:1, 0:C2], start=True,
                                          stop=True),
             reads=("rbmu", "consts"), writes=("pk6",))
        b.op("tensor", lambda E: E.matmul(b.psum[7][:, 0:C2], ones1[0:1, 0:128],
                                          rowbb[0:1, C2:2 * C2], start=True,
                                          stop=True),
             reads=("rbrstd", "consts"), writes=("pk7",))
        for mt in range(32):
            b.op("vector", lambda E, mt=mt:
                 E.tensor_tensor(out=d1t, in0=hfm[:, mt * C2:(mt + 1) * C2],
                                 in1=b.psum[6][:, 0:C2], op=ALU.subtract),
                 reads=(("hf", mt), "pk6"), writes=("d1t",))
            b.op("vector", lambda E:
                 E.tensor_tensor(out=d2t, in0=d1t, in1=b.psum[7][:, 0:C2],
                                 op=ALU.mult),
                 reads=("d1t", "pk7"), writes=("d2t",))
            b.op("scalar", lambda E, mt=mt:
                 E.activation(out=hfm[:, mt * C2:(mt + 1) * C2], in_=d2t,
                              func=AF.Identity, scale=lhg[:, mt:mt + 1],
                              bias=lhb[:, mt:mt + 1]),
                 reads=("d2t", "consts"), writes=(("hf", mt),))
        hreads2 = tuple(("hf", mt) for mt in range(32))
        for osl in range(4):
            b.wait_mile("sync", pm["fc2_w"])
            for kc in range(32):
                b.dma("sync", f2sl[:, kc * 256:(kc + 1) * 256],
                      scr["fc2_w"][osl * 256:(osl + 1) * 256,
                                   kc * 128:(kc + 1) * 128],
                      transpose=True, writes=("f2sl",))
            osll = slice(osl * 256, (osl + 1) * 256)
            for tt in range(CT):
                bank = b.psum[tt % 4]
                bres = f"pk{tt % 4}"
                b.op("tensor", lambda E, bank=bank, tt=tt: [
                     E.matmul(bank[:, 0:256],
                              hfm[:, kc * C2 + tt * 128:kc * C2 + (tt + 1) * 128],
                              f2sl[:, kc * 256:(kc + 1) * 256],
                              start=(kc == 0), stop=(kc == 31))
                     for kc in range(32)][-1],
                     reads=hreads2 + ("f2sl",), writes=(bres,))
                b.op("vector", lambda E, bank=bank, osll=osll:
                     E.tensor_tensor(out=tmpv[:, 0:256], in0=bank[:, 0:256],
                                     in1=f2br[:, osll], op=ALU.add),
                     reads=(bres, "consts"), writes=(("tmpv",),))
                b.op("vector", lambda E, osll=osll:
                     E.tensor_tensor(out=tmpw[:, 0:256], in0=tmpv[:, 0:256],
                                     in1=g2r[:, osll], op=ALU.mult),
                     reads=(("tmpv",), "consts"), writes=(("tmpw",),))
                b.op("vector", lambda E, tt=tt, osll=osll:
                     E.tensor_tensor(out=s1t[tt][:, osll], in0=tmpw[:, 0:256],
                                     in1=t1t[tt][:, osll], op=ALU.add),
                     reads=(("tmpw",), ("t1", tt)), writes=(("s1", tt),))
        for i in range(CT):
            b.dma("sync", io["s_out"][t0 + i * 128:t0 + (i + 1) * 128, :], s1t[i],
                  reads=(("s1", i),))
    al.reset(mark)



MCH = 256
NMCH = T // MCH

def emit_mamba(b, io):
    al = b.al
    mark = al.mark()
    CHm = MCH
    CT2 = CHm // 128

    ipw = al.tile("ipw", [128, 8 * 4096], BF16)
    xpw = al.tile("xpw", [128, 16 * 96], BF16)
    dtw = al.tile("dtw", [64, DI], BF16)
    opwh = al.tile("opwh", [128, 16 * 512], BF16)
    xn2f = [al.tile("xn2f", [128, 8 * CHm], BF16)]
    A_sb = al.tile("A_sb", [128, 256], F32)
    A_tmp = al.tile("A_tmp", [128, 256], F32)
    cw = al.tile("cw", [128, 64], F32)
    cb = al.tile("cb", [128, 16], F32)
    dtb = al.tile("dtb", [128, 16], F32)
    dsk = al.tile("dsk", [128, 16], F32)
    selm = al.tile("selm", [16, 16 * 128], BF16)
    xi = al.tile("xi", [128, 16 * (CHm + 3)], BF16)
    zs = al.tile("zs", [128, 16 * CHm], BF16)
    xc = al.tile("xc", [128, 16 * CHm], BF16)
    dt = al.tile("dt", [128, 16 * CHm], F32)
    dug = al.tile("dug", [128, CHm], BF16)
    Bsb = al.tile("Bsb", [128, 16 * CHm], BF16)
    Csb = al.tile("Csb", [128, 16 * CHm], BF16)
    yg = al.tile("ygat", [128, 16 * CHm], BF16)
    hst = al.tile("hst", [128, 256], F32)
    hsl = [al.tile("hsl", [128, CHm], BF16) for _ in range(4)]
    asl = [al.tile("asl", [128, CHm], BF16) for _ in range(2)]
    bsl = [al.tile("bsl", [128, CHm], BF16) for _ in range(2)]
    ymb = al.tile("ymb", [128, CHm], BF16)
    ypp = [al.tile("ypp", [128, CHm], F32) for _ in range(2)]
    spt = [al.tile("spt", [128, CHm], F32) for _ in range(2)]
    dtr = al.tile("dtr", [64, CHm], BF16)
    bcbB = al.tile("bcbB", [16, CHm], BF16)
    bcbC = al.tile("bcbC", [16, CHm], BF16)
    xn0t = [al.tile("m_xn0", [128, D], BF16) for _ in range(2)]
    s0t = [al.tile("m_s0", [128, D], F32) for _ in range(2)]

    for kc in range(8):
        b.dma("gpsimd", ipw[:, kc * 4096:(kc + 1) * 4096],
              io["w_inproj"][kc * 128:(kc + 1) * 128, :], writes=("ipw",))
    for kc in range(16):
        b.dma("gpsimd", xpw[:, kc * 96:(kc + 1) * 96],
              io["w_xproj"][kc * 128:(kc + 1) * 128, :], writes=("xpw",))
    b.dma("gpsimd", dtw, io["w_dtproj"], writes=("dtw",))
    b.dma("gpsimd", selm, io["selmat"][0:16, 0:16 * 128], writes=("selm",))
    for dst, nm in ((A_sb, "alog_t"), (cw, "cw_t"), (cb, "cb_t"),
                    (dtb, "dtb_t"), (dsk, "dsk_t")):
        b.dma("sync", dst, io[nm], writes=("mconsts",))
    b.op("scalar", lambda E: E.activation(out=A_tmp, in_=A_sb, func=AF.Exp),
         reads=("mconsts",), writes=("A1",))
    b.op("vector", lambda E: E.tensor_scalar(out=A_sb, in0=A_tmp, scalar1=-1.0,
                                             scalar2=None, op0=ALU.mult),
         reads=("A1", "mconsts"), writes=("A2",))
    b.op("scalar", lambda E: E.copy(out=A_tmp, in_=A_sb),
         reads=("A2",), writes=("A_sb",))
    b.op("vector", lambda E: E.memset(hst[:, :], 0.0), writes=("hstz",))
    for g in range(16):
        b.op("vector", lambda E, g=g: E.memset(xi[:, g * (CHm + 3):
                                               g * (CHm + 3) + 3], 0.0),
             writes=(("xh", g),))

    for c in range(NMCH):
        t0 = c * CHm
        xs = 0
        for kc in range(8):
            b.dma("sync", xn2f[xs][:, kc * CHm:(kc + 1) * CHm],
                  io["xn2_in"][t0:t0 + CHm, kc * 128:(kc + 1) * 128],
                  transpose=True, writes=((f"xn2f{xs}",),))
        for mt in range(32):
            bank = b.psum[mt % 4]
            bres = f"pm{mt % 4}"
            b.op("tensor", lambda E, bank=bank, mt=mt, xs=xs: [
                 E.matmul(bank[:, 0:CHm],
                          ipw[:, kc * 4096 + mt * 128:kc * 4096 + (mt + 1) * 128],
                          xn2f[xs][:, kc * CHm:(kc + 1) * CHm],
                          start=(kc == 0), stop=(kc == 7))
                 for kc in range(8)][-1],
                 reads=("ipw", (f"xn2f{xs}",)), writes=(bres,))
            if mt < 16:
                g = mt
                b.op("vector", lambda E, bank=bank, g=g:
                     E.tensor_copy(out=xi[:, g * (CHm + 3) + 3:
                                          (g + 1) * (CHm + 3)],
                                   in_=bank[:, 0:CHm]),
                     reads=(bres,), writes=((f"xi", g),))
            else:
                g = mt - 16
                b.op("scalar", lambda E, bank=bank, g=g:
                     E.activation(out=zs[:, g * CHm:(g + 1) * CHm],
                                  in_=bank[:, 0:CHm], func=AF.Silu),
                     reads=(bres,), writes=((f"zs", g),))
        for g in range(16):
            x0 = g * (CHm + 3)
            for k in range(2):
                b.op("scalar", lambda E, g=g, k=k, x0=x0:
                     E.activation(out=spt[k], in_=xi[:, x0 + k:x0 + k + CHm],
                                  func=AF.Copy, scale=cw[:, g * 4 + k:g * 4 + k + 1]),
                     reads=((f"xi", g), ("xh", g), "mconsts"),
                     writes=((f"tap{k}",),))
            b.op("vector", lambda E:
                 E.tensor_tensor(out=ypp[0], in0=spt[0], in1=spt[1], op=ALU.add),
                 reads=(("tap0",), ("tap1",)), writes=(("cv01",),))
            for k in range(2, 4):
                b.op("scalar", lambda E, g=g, k=k, x0=x0:
                     E.activation(out=spt[k - 2], in_=xi[:, x0 + k:x0 + k + CHm],
                                  func=AF.Copy, scale=cw[:, g * 4 + k:g * 4 + k + 1]),
                     reads=((f"xi", g), ("xh", g), "mconsts"),
                     writes=((f"tap{k}",),))
            b.op("vector", lambda E:
                 E.tensor_tensor(out=ypp[1], in0=spt[0], in1=spt[1], op=ALU.add),
                 reads=(("tap2",), ("tap3",)), writes=(("cv23",),))
            b.op("vector", lambda E:
                 E.tensor_tensor(out=spt[0], in0=ypp[0], in1=ypp[1], op=ALU.add),
                 reads=(("cv01",), ("cv23",)), writes=(("cvs",),))
            b.op("scalar", lambda E, g=g:
                 E.activation(out=xc[:, g * CHm:(g + 1) * CHm], in_=spt[0],
                              func=AF.Silu, bias=cb[:, g:g + 1]),
                 reads=(("cvs",), "mconsts"), writes=((f"xc", g),))
            b.op("vector", lambda E, g=g, x0=x0:
                 E.tensor_copy(out=xi[:, x0:x0 + 3],
                               in_=xi[:, x0 + CHm:x0 + CHm + 3]),
                 reads=((f"xi", g),), writes=(("xh", g),))
        xc_reads = tuple((f"xc", g) for g in range(16))
        for gi, (c0, c1, M) in enumerate(((0, 64, 64), (64, 80, 16),
                                          (80, 96, 16))):
            bank = b.psum[4 + gi]
            bres = f"pm{4 + gi}"
            b.op("tensor", lambda E, bank=bank, c0=c0, c1=c1, M=M: [
                 E.matmul(bank[0:M, 0:CHm], xpw[:, kc * 96 + c0:kc * 96 + c1],
                          xc[:, kc * CHm:(kc + 1) * CHm],
                          start=(kc == 0), stop=(kc == 15))
                 for kc in range(16)][-1],
                 reads=("xpw",) + xc_reads, writes=(bres,))
        b.op("vector", lambda E: E.tensor_copy(out=dtr, in_=b.psum[4][0:64, 0:CHm]),
             reads=("pm4",), writes=("dtr",))
        b.op("vector", lambda E: E.tensor_copy(out=bcbB[0:16, :],
                                               in_=b.psum[5][0:16, 0:CHm]),
             reads=("pm5",), writes=("bcb",))
        b.op("vector", lambda E: E.tensor_copy(out=bcbC[0:16, :],
                                               in_=b.psum[6][0:16, 0:CHm]),
             reads=("pm6",), writes=("bcb2",))
        for g in range(16):
            bank = b.psum[4 + (g % 2)]
            bres = f"pm{4 + (g % 2)}"
            b.op("tensor", lambda E, bank=bank, g=g:
                 E.matmul(bank[:, 0:CHm], dtw[0:64, g * 128:(g + 1) * 128], dtr,
                          start=True, stop=True),
                 reads=("dtw", "dtr"), writes=(bres,))
            b.op("scalar", lambda E, bank=bank, g=g:
                 E.activation(out=spt[0], in_=bank[:, 0:CHm], func=AF.Identity,
                              bias=dtb[:, g:g + 1]),
                 reads=(bres, "mconsts"), writes=(("sp0",),))
            b.op("scalar", lambda E:
                 E.activation(out=spt[1], in_=spt[0], func=AF.Abs),
                 reads=(("sp0",),), writes=(("sp1",),))
            b.op("scalar", lambda E:
                 E.activation(out=ypp[0], in_=spt[1], func=AF.Exp, scale=-1.0),
                 reads=(("sp1",),), writes=(("sp2",),))
            b.op("scalar", lambda E:
                 E.activation(out=spt[1], in_=ypp[0], func=AF.Ln, bias=1.0),
                 reads=(("sp2",),), writes=(("sp1b",),))
            b.op("scalar", lambda E:
                 E.activation(out=ypp[1], in_=spt[0], func=AF.Relu),
                 reads=(("sp0",),), writes=(("sp3",),))
            b.op("vector", lambda E, g=g:
                 E.tensor_tensor(out=dt[:, g * CHm:(g + 1) * CHm], in0=ypp[1],
                                 in1=spt[1], op=ALU.add),
                 reads=(("sp3",), ("sp1b",)), writes=((f"dt", g),))
        for s in range(NS):
            bank = b.psum[6 + (s % 2)]
            bres = f"pm{6 + (s % 2)}"
            b.op("tensor", lambda E, bank=bank, s=s:
                 E.matmul(bank[:, 0:CHm], selm[0:16, s * 128:(s + 1) * 128],
                          bcbB[0:16, 0:CHm], start=True, stop=True),
                 reads=("selm", "bcb"), writes=(bres,))
            b.op("vector", lambda E, bank=bank, s=s:
                 E.tensor_copy(out=Bsb[:, s * CHm:(s + 1) * CHm],
                               in_=bank[:, 0:CHm]),
                 reads=(bres,), writes=((f"Bs", s),))
            bank2 = b.psum[6 + ((s + 1) % 2)]
            bres2 = f"pm{6 + ((s + 1) % 2)}"
            b.op("tensor", lambda E, bank=bank2, s=s:
                 E.matmul(bank[:, 0:CHm], selm[0:16, s * 128:(s + 1) * 128],
                          bcbC[0:16, 0:CHm], start=True, stop=True),
                 reads=("selm", "bcb2"), writes=(bres2,))
            b.op("vector", lambda E, bank=bank2, s=s:
                 E.tensor_copy(out=Csb[:, s * CHm:(s + 1) * CHm],
                               in_=bank[:, 0:CHm]),
                 reads=(bres2,), writes=((f"Cs", s),))
        for g in range(16):
            b.op("vector", lambda E, g=g:
                 E.tensor_tensor(out=dug, in0=dt[:, g * CHm:(g + 1) * CHm],
                                 in1=xc[:, g * CHm:(g + 1) * CHm], op=ALU.mult),
                 reads=((f"dt", g), (f"xc", g)), writes=(("dug",),))
            b.op("vector", lambda E: E.memset(ypp[0][:, :], 0.0),
                 writes=(("yp", 0),))
            def sched(s, g=g):
                if s < NS:
                    b.op("gpsimd", lambda E, s=s:
                         E.tensor_tensor(out=bsl[s % 2], in0=dug,
                                         in1=Bsb[:, s * CHm:(s + 1) * CHm],
                                         op=ALU.mult),
                         reads=(("dug",), (f"Bs", s)), writes=((f"bsl", s % 2),))
                    b.op("scalar", lambda E, s=s, g=g:
                         E.activation(out=asl[s % 2],
                                      in_=dt[:, g * CHm:(g + 1) * CHm],
                                      func=AF.Exp,
                                      scale=A_sb[:, g * NS + s:g * NS + s + 1]),
                         reads=((f"dt", g), "A2"), writes=((f"asl", s % 2),))
                    b.op("vector", lambda E, s=s, g=g:
                         E.tensor_tensor_scan(out=hsl[s % 4], data0=asl[s % 2],
                                              data1=bsl[s % 2],
                                              initial=hst[:, g * NS + s:
                                                          g * NS + s + 1],
                                              op0=ALU.mult, op1=ALU.add),
                         reads=((f"asl", s % 2), (f"bsl", s % 2),
                                ("hst", g, s)),
                         writes=((f"hsl", s % 4),))
                sp = s - 2
                if 0 <= sp < NS:
                    b.op("vector", lambda E, sp=sp:
                         E.tensor_tensor(out=ymb, in0=hsl[sp % 4],
                                         in1=Csb[:, sp * CHm:(sp + 1) * CHm],
                                         op=ALU.mult),
                         reads=((f"hsl", sp % 4), (f"Cs", sp)), writes=("ymb",))
                    b.op("vector", lambda E, sp=sp:
                         E.tensor_tensor(out=ypp[(sp + 1) % 2], in0=ypp[sp % 2],
                                         in1=ymb, op=ALU.add),
                         reads=("ymb", ("yp", sp % 2)),
                         writes=(("yp", (sp + 1) % 2),))
                    b.op("vector", lambda E, sp=sp, g=g:
                         E.tensor_copy(out=hst[:, g * NS + sp:g * NS + sp + 1],
                                       in_=hsl[sp % 4][:, CHm - 1:CHm]),
                         reads=((f"hsl", sp % 4),), writes=(("hst", g, sp),))
            for s in range(NS + 2):
                sched(s)
            b.op("vector", lambda E, g=g:
                 E.tensor_scalar(out=spt[0], in0=xc[:, g * CHm:(g + 1) * CHm],
                                 scalar1=dsk[:, g:g + 1], scalar2=None,
                                 op0=ALU.mult),
                 reads=((f"xc", g), "mconsts"), writes=(("sp0",),))
            b.op("vector", lambda E, g=g:
                 E.tensor_tensor(out=spt[1], in0=ypp[0], in1=spt[0], op=ALU.add),
                 reads=(("yp", 0), ("sp0",)), writes=(("sp1",),))
            b.op("vector", lambda E, g=g:
                 E.tensor_tensor(out=yg[:, g * CHm:(g + 1) * CHm], in0=spt[1],
                                 in1=zs[:, g * CHm:(g + 1) * CHm], op=ALU.mult),
                 reads=(("sp1",), (f"zs", g)), writes=((f"yg", g),))
        yg_reads = tuple((f"yg", g) for g in range(16))
        for nc_ in range(2):
            for kc2 in range(16):
                b.dma("gpsimd", opwh[:, kc2 * 512:(kc2 + 1) * 512],
                      io["w_outproj"][kc2 * 128:(kc2 + 1) * 128,
                                      nc_ * 512:(nc_ + 1) * 512],
                      writes=("opwh",))
            for tt in range(CT2):
                if nc_ == 0:
                    b.dma("sync", xn0t[tt],
                          io["xn0_in"][t0 + tt * 128:t0 + (tt + 1) * 128, :],
                          writes=((f"xn0t", tt),))
                bank = b.psum[nc_ * 2 + tt]
                bres = f"pm{nc_ * 2 + tt}"
                b.op("tensor", lambda E, bank=bank, tt=tt: [
                     E.matmul(bank[:, 0:512],
                              yg[:, kc2 * CHm + tt * 128:
                                 kc2 * CHm + (tt + 1) * 128],
                              opwh[:, kc2 * 512:(kc2 + 1) * 512],
                              start=(kc2 == 0), stop=(kc2 == 15))
                     for kc2 in range(16)][-1],
                     reads=yg_reads + ("opwh",), writes=(bres,))
                b.op("vector", lambda E, bank=bank, tt=tt, nc_=nc_:
                     E.tensor_tensor(out=s0t[tt][:, nc_ * 512:(nc_ + 1) * 512],
                                     in0=bank[:, 0:512],
                                     in1=xn0t[tt][:, nc_ * 512:(nc_ + 1) * 512],
                                     op=ALU.add),
                     reads=(bres, (f"xn0t", tt)), writes=((f"s0t", tt, nc_),))
        for tt in range(CT2):
            b.dma("sync", io["s_out"][t0 + tt * 128:t0 + (tt + 1) * 128, :],
                  s0t[tt], reads=((f"s0t", tt, 0), (f"s0t", tt, 1)))
    al.reset(mark)



def _finalize(b):
    for cn in ("q_sync", "q_gp", "q_scl"):
        if cn in b.sems:
            b._wait("sync", cn, b.sems[cn].n)
    b.emit()


def _build_ln0():
    b = B()
    io = {
        "x_in": b.din("x_in", [T, D], F32),
        "xn_out": b.dout("xn0", [T, D], BF16),
        "xn2_out": b.dout("xn2", [T, D], BF16),
        "ln_w_rep": b.din("ln_w_rep", [128, D], F32),
        "ln_b_rep": b.din("ln_b_rep", [128, D], F32),
    }
    emit_ln_phase(b, io, first=True, make_xn2=True, final=False)
    _finalize(b)
    return b


def _build_upd(layer):
    b = B()
    io = {
        "x_in": b.din("x_in", [T, D], F32),
        "s_in": b.din("s_in", [T, D], F32),
        "acc_in": b.din("acc_in", [T, D], F32),
        "x_out": b.dout("x_out", [T, D], F32),
    }
    if layer == 0:
        io["xn_out"] = b.dout("xn1", [T, D], BF16)
        emit_ln_phase(b, io, first=False, make_xn2=False, final=False)
    else:
        emit_ln_phase(b, io, first=False, make_xn2=False, final=True)
    _finalize(b)
    return b


def _build_mamba():
    b = B()
    io = {
        "xn2_in": b.din("xn2_in", [T, D], BF16),
        "xn0_in": b.din("xn0_in", [T, D], BF16),
        "s_out": b.dout("s_out", [T, D], F32),
        "w_inproj": b.din("w_inproj", [D, 2 * DI], F32),
        "w_xproj": b.din("w_xproj", [DI, 96], F32),
        "w_dtproj": b.din("w_dtproj", [DTR, DI], F32),
        "w_outproj": b.din("w_outproj", [DI, D], F32),
        "alog_t": b.din("alog_t", [128, 256], F32),
        "cw_t": b.din("cw_t", [128, 64], F32),
        "cb_t": b.din("cb_t", [128, 16], F32),
        "dtb_t": b.din("dtb_t", [128, 16], F32),
        "dsk_t": b.din("dsk_t", [128, 16], F32),
        "selmat": b.din("selmat", [32, 32 * 128], BF16),
    }
    emit_mamba(b, io)
    _finalize(b)
    return b


def _build_ffn(layer):
    b = B()
    io = {
        "xn_in": b.din("xn_in", [T, D], BF16),
        "acc_out": b.dout("acc_out", [T, D], F32),
    }
    scr = {}
    pm = {}
    E_ = 16 if layer == 0 else 64
    io["gate_b_rep"] = b.din("gate_b_rep", [128, E_], F32)
    if layer == 0:
        w13 = b.din("w13", [16, 1024, 1024], F32)
        w2 = b.din("w2", [16, 1024, 512], F32)
        w13s = b.din("w13s", [1024, 1024], F32)
        w2s = b.din("w2s", [1024, 512], F32)
        gw = b.din("gw", [16, 1024], F32)
        scr["w13"] = b.dint("scr_w13", [16, 1024, 1024], BF16)
        scr["w2"] = b.dint("scr_w2", [16, 1024, 512], BF16)
        scr["w13_s"] = b.dint("scr_w13s", [1024, 1024], BF16)
        scr["w2_s"] = b.dint("scr_w2s", [1024, 512], BF16)
        scr["gate_w"] = b.dint("scr_gw", [16, 1024], BF16)
        pm["gate_w"] = b.dma("gpsimd", scr["gate_w"], gw)
        pm["w13_s"] = b.dma("gpsimd", scr["w13_s"], w13s)
        pm["w2_s"] = b.dma("gpsimd", scr["w2_s"], w2s)
        pm["w13"] = [b.dma("gpsimd", scr["w13"][e], w13[e]) for e in range(16)]
        pm["w2"] = [b.dma("gpsimd", scr["w2"][e], w2[e]) for e in range(16)]
    else:
        eg = b.din("eg", [64, 512, 1024], F32)
        eu = b.din("eu", [64, 512, 1024], F32)
        ed = b.din("ed", [64, 1024, 512], F32)
        sg = b.din("sg", [1024, 1024], F32)
        su = b.din("su", [1024, 1024], F32)
        sd = b.din("sd", [1024, 1024], F32)
        gw = b.din("gw", [64, 1024], F32)
        scr["eg"] = b.dint("scr_eg", [64, 512, 1024], BF16)
        scr["eu"] = b.dint("scr_eu", [64, 512, 1024], BF16)
        scr["ed"] = b.dint("scr_ed", [64, 1024, 512], BF16)
        scr["sg"] = b.dint("scr_sg", [1024, 1024], BF16)
        scr["su"] = b.dint("scr_su", [1024, 1024], BF16)
        scr["sd"] = b.dint("scr_sd", [1024, 1024], BF16)
        scr["gate_w"] = b.dint("scr_gw64", [64, 1024], BF16)
        pm["gate_w"] = b.dma("gpsimd", scr["gate_w"], gw)
        pm["sg"] = b.dma("gpsimd", scr["sg"], sg)
        pm["su"] = b.dma("gpsimd", scr["su"], su)
        pm["sd"] = b.dma("gpsimd", scr["sd"], sd)
        pm["eg"] = [b.dma("gpsimd", scr["eg"][e], eg[e]) for e in range(64)]
        pm["eu"] = [b.dma("gpsimd", scr["eu"][e], eu[e]) for e in range(64)]
        pm["ed"] = [b.dma("gpsimd", scr["ed"][e], ed[e]) for e in range(64)]
    io["scr"] = scr
    io["prep_marks"] = pm
    emit_ffn(b, io, layer=layer)
    _finalize(b)
    return b


def _build_attn1():
    b = B()
    io = {
        "xn_in": b.din("xn_in", [T, D], BF16),
        "s_out": b.dout("s_out", [T, D], F32),
        "g1_rep": b.din("g1_rep", [128, D], F32),
        "g2_rep": b.din("g2_rep", [128, D], F32),
        "l1g_rep": b.din("l1g_rep", [128, D], F32),
        "l1b_rep": b.din("l1b_rep", [128, D], F32),
        "l2g_rep": b.din("l2g_rep", [128, D], F32),
        "l2b_rep": b.din("l2b_rep", [128, D], F32),
        "lnh_g_t": b.din("lnh_g_t", [128, 32], F32),
        "lnh_b_t": b.din("lnh_b_t", [128, 32], F32),
        "fc1_b_t": b.din("fc1_b_t", [128, 32], F32),
        "pb_rep": b.din("pb_rep", [128, D], F32),
        "f2b_rep": b.din("f2b_rep", [128, D], F32),
        "ones_row": b.din("ones_row", [1, 128], BF16),
        "ones_col": b.din("ones_col", [128, 1], BF16),
        "y1tm": b.dint("y1tm", [T, D], BF16),
        "ztm": b.dint("ztm", [T, D], BF16),
    }
    pw = b.din("proj_w", [1024, 1024], F32)
    f1w = b.din("fc1_w", [4096, 1024], F32)
    f2w = b.din("fc2_w", [1024, 4096], F32)
    scr = {
        "proj_w": b.dint("scr_proj", [1024, 1024], BF16),
        "fc1_w": b.dint("scr_fc1", [4096, 1024], BF16),
        "fc2_w": b.dint("scr_fc2", [1024, 4096], BF16),
    }
    pm = {
        "proj_w": b.dma("gpsimd", scr["proj_w"], pw),
        "fc1_w": b.dma("gpsimd", scr["fc1_w"], f1w),
        "fc2_w": b.dma("gpsimd", scr["fc2_w"], f2w),
    }
    io["scr"] = scr
    io["prep_marks"] = pm
    emit_attn1(b, io)
    _finalize(b)
    return b



def _rep(v):
    v = np.ascontiguousarray(np.asarray(v, np.float32))
    return np.ascontiguousarray(np.broadcast_to(v[None, :], (128, v.shape[0])))


def _t16(v, k):
    return np.ascontiguousarray(np.asarray(v, np.float32).reshape(k, 128).T)


def _run(b, in_maps):
    res = run_bass_kernel_spmd(b.nc, in_maps, core_ids=list(range(len(in_maps))))
    return res.results


def _kernel_multi(inp):
    NCORES = 8
    x = np.ascontiguousarray(np.asarray(inp["L0_attn_x"], np.float32))

    lnw_rep = _rep(inp["L0_attn_ln_w"])
    lnb_rep = _rep(inp["L0_attn_ln_b"])

    alog_t = np.ascontiguousarray(
        np.asarray(inp["L0_attn_A_log"], np.float32)
        .reshape(16, 128, 16).transpose(1, 0, 2).reshape(128, 256))
    cw_t = np.ascontiguousarray(
        np.asarray(inp["L0_attn_conv_w"], np.float32)
        .reshape(16, 128, 4).transpose(1, 0, 2).reshape(128, 64))
    cb_t = _t16(inp["L0_attn_conv_b"], 16)
    dtb_t = _t16(inp["L0_attn_dt_proj_b"], 16)
    dsk_t = _t16(inp["L0_attn_D_skip"], 16)
    selmat = np.zeros((32, 32 * 128), dtype=BF)
    for j in range(32):
        selmat[j, j * 128:(j + 1) * 128] = 1

    b0 = _build_ln0()
    r0 = _run(b0, [{"x_in": x[i], "ln_w_rep": lnw_rep, "ln_b_rep": lnb_rep}
                   for i in range(NCORES)])
    xn0 = [r["xn0"] for r in r0]
    xn2 = [r["xn2"] for r in r0]

    b1 = _build_mamba()
    r1 = _run(b1, [{
        "xn2_in": xn2[i], "xn0_in": xn0[i],
        "w_inproj": inp["L0_attn_in_proj_w"],
        "w_xproj": inp["L0_attn_x_proj_w"],
        "w_dtproj": inp["L0_attn_dt_proj_w"],
        "w_outproj": inp["L0_attn_out_proj_w"],
        "alog_t": alog_t, "cw_t": cw_t, "cb_t": cb_t, "dtb_t": dtb_t,
        "dsk_t": dsk_t, "selmat": selmat,
    } for i in range(NCORES)])
    s0 = [r["s_out"] for r in r1]

    b2 = _build_ffn(0)
    r2 = _run(b2, [{
        "xn_in": xn0[i],
        "gate_b_rep": _rep(inp["L0_ffn_gate_b"]),
        "w13": inp["L0_ffn_w13"], "w2": inp["L0_ffn_w2"],
        "w13s": inp["L0_ffn_w13_s"], "w2s": inp["L0_ffn_w2_s"],
        "gw": inp["L0_ffn_gate_w"],
    } for i in range(NCORES)])
    acc0 = [r["acc_out"] for r in r2]

    b3 = _build_upd(0)
    r3 = _run(b3, [{"x_in": x[i], "s_in": s0[i], "acc_in": acc0[i]}
                   for i in range(NCORES)])
    x1 = [r["x_out"] for r in r3]
    xn1 = [r["xn1"] for r in r3]

    b4 = _build_attn1()
    r4 = _run(b4, [{
        "xn_in": xn1[i],
        "g1_rep": _rep(inp["L1_attn_gamma1"]), "g2_rep": _rep(inp["L1_attn_gamma2"]),
        "l1g_rep": _rep(inp["L1_attn_ln1_g"]), "l1b_rep": _rep(inp["L1_attn_ln1_b"]),
        "l2g_rep": _rep(inp["L1_attn_ln2_g"]), "l2b_rep": _rep(inp["L1_attn_ln2_b"]),
        "lnh_g_t": _t16(inp["L1_attn_lnh_g"], 32),
        "lnh_b_t": _t16(inp["L1_attn_lnh_b"], 32),
        "fc1_b_t": _t16(inp["L1_attn_fc1_b"], 32),
        "pb_rep": _rep(inp["L1_attn_proj_b"]),
        "f2b_rep": _rep(inp["L1_attn_fc2_b"]),
        "ones_row": np.ones((1, 128), dtype=BF),
        "ones_col": np.ones((128, 1), dtype=BF),
        "proj_w": inp["L1_attn_proj_w"],
        "fc1_w": inp["L1_attn_fc1_w"],
        "fc2_w": inp["L1_attn_fc2_w"],
    } for i in range(NCORES)])
    s1 = [r["s_out"] for r in r4]

    b5 = _build_ffn(1)
    r5 = _run(b5, [{
        "xn_in": xn1[i],
        "gate_b_rep": _rep(inp["L1_ffn_gate_bias"]),
        "eg": inp["L1_ffn_eg_w"], "eu": inp["L1_ffn_eu_w"],
        "ed": inp["L1_ffn_ed_w"],
        "sg": inp["L1_ffn_sg_w"], "su": inp["L1_ffn_su_w"],
        "sd": inp["L1_ffn_sd_w"],
        "gw": inp["L1_ffn_gate_w"],
    } for i in range(NCORES)])
    acc1 = [r["acc_out"] for r in r5]

    b6 = _build_upd(1)
    r6 = _run(b6, [{"x_in": x1[i], "s_in": s1[i], "acc_in": acc1[i]}
                   for i in range(NCORES)])
    out = np.stack([np.asarray(r["x_out"], np.float32) for r in r6], axis=0)
    return out



def _build_fused():
    b = B()
    al = b.al
    dummy = al.tile("bar_dummy", [128, 16], F32)

    x_in = b.din("x_in", [T, D], F32)
    out = b.dout("out", [T, D], F32)
    xn0 = b.dint("xn0", [T, D], BF16)
    xn2 = b.dint("xn2", [T, D], BF16)
    s0 = b.dint("s0", [T, D], F32)
    acc0 = b.dint("acc0", [T, D], F32)
    x1 = b.dint("x1", [T, D], F32)
    xn1 = b.dint("xn1", [T, D], BF16)
    s1 = b.dint("s1", [T, D], F32)
    acc1 = b.dint("acc1", [T, D], F32)

    io = {"x_in": x_in, "xn_out": xn0, "xn2_out": xn2,
          "ln_w_rep": b.din("ln_w_rep", [128, D], F32),
          "ln_b_rep": b.din("ln_b_rep", [128, D], F32)}
    emit_ln_phase(b, io, first=True, make_xn2=True, final=False)
    b.barrier(dummy)

    io = {"xn2_in": xn2, "xn0_in": xn0, "s_out": s0,
          "w_inproj": b.din("w_inproj", [D, 2 * DI], F32),
          "w_xproj": b.din("w_xproj", [DI, 96], F32),
          "w_dtproj": b.din("w_dtproj", [DTR, DI], F32),
          "w_outproj": b.din("w_outproj", [DI, D], F32),
          "alog_t": b.din("alog_t", [128, 256], F32),
          "cw_t": b.din("cw_t", [128, 64], F32),
          "cb_t": b.din("cb_t", [128, 16], F32),
          "dtb_t": b.din("dtb_t", [128, 16], F32),
          "dsk_t": b.din("dsk_t", [128, 16], F32),
          "selmat": b.din("selmat", [32, 32 * 128], BF16)}
    emit_mamba(b, io)
    b.barrier(dummy)

    io = {"xn_in": xn0, "acc_out": acc0,
          "gate_b_rep": b.din("gate_b_rep0", [128, 16], F32)}
    scr = {"w13": b.dint("scr_w13", [16, 1024, 1024], BF16),
           "w2": b.dint("scr_w2", [16, 1024, 512], BF16),
           "w13_s": b.dint("scr_w13s", [1024, 1024], BF16),
           "w2_s": b.dint("scr_w2s", [1024, 512], BF16),
           "gate_w": b.dint("scr_gw0", [16, 1024], BF16)}
    w13 = b.din("w13", [16, 1024, 1024], F32)
    w2 = b.din("w2", [16, 1024, 512], F32)
    pm = {"gate_w": b.dma("gpsimd", scr["gate_w"], b.din("gw0", [16, 1024], F32)),
          "w13_s": b.dma("gpsimd", scr["w13_s"], b.din("w13s", [1024, 1024], F32)),
          "w2_s": b.dma("gpsimd", scr["w2_s"], b.din("w2s", [1024, 512], F32)),
          "w13": [b.dma("gpsimd", scr["w13"][e], w13[e]) for e in range(16)],
          "w2": [b.dma("gpsimd", scr["w2"][e], w2[e]) for e in range(16)]}
    io["scr"] = scr
    io["prep_marks"] = pm
    emit_ffn(b, io, layer=0)
    b.barrier(dummy)

    io = {"x_in": x_in, "s_in": s0, "acc_in": acc0, "x_out": x1,
          "xn_out": xn1}
    emit_ln_phase(b, io, first=False, make_xn2=False, final=False)
    b.barrier(dummy)

    io = {"xn_in": xn1, "s_out": s1,
          "g1_rep": b.din("g1_rep", [128, D], F32),
          "g2_rep": b.din("g2_rep", [128, D], F32),
          "l1g_rep": b.din("l1g_rep", [128, D], F32),
          "l1b_rep": b.din("l1b_rep", [128, D], F32),
          "l2g_rep": b.din("l2g_rep", [128, D], F32),
          "l2b_rep": b.din("l2b_rep", [128, D], F32),
          "lnh_g_t": b.din("lnh_g_t", [128, 32], F32),
          "lnh_b_t": b.din("lnh_b_t", [128, 32], F32),
          "fc1_b_t": b.din("fc1_b_t", [128, 32], F32),
          "pb_rep": b.din("pb_rep", [128, D], F32),
          "f2b_rep": b.din("f2b_rep", [128, D], F32),
          "ones_row": b.din("ones_row", [1, 128], BF16),
          "ones_col": b.din("ones_col", [128, 1], BF16),
          "y1tm": b.dint("y1tm", [T, D], BF16),
          "ztm": b.dint("ztm", [T, D], BF16)}
    scr = {"proj_w": b.dint("scr_proj", [1024, 1024], BF16),
           "fc1_w": b.dint("scr_fc1", [4096, 1024], BF16),
           "fc2_w": b.dint("scr_fc2", [1024, 4096], BF16)}
    pm = {"proj_w": b.dma("gpsimd", scr["proj_w"], b.din("proj_w", [1024, 1024], F32)),
          "fc1_w": b.dma("gpsimd", scr["fc1_w"], b.din("fc1_w", [4096, 1024], F32)),
          "fc2_w": b.dma("gpsimd", scr["fc2_w"], b.din("fc2_w", [1024, 4096], F32))}
    io["scr"] = scr
    io["prep_marks"] = pm
    emit_attn1(b, io)
    b.barrier(dummy)

    io = {"xn_in": xn1, "acc_out": acc1,
          "gate_b_rep": b.din("gate_b_rep1", [128, 64], F32)}
    scr = {"eg": b.dint("scr_eg", [64, 512, 1024], BF16),
           "eu": b.dint("scr_eu", [64, 512, 1024], BF16),
           "ed": b.dint("scr_ed", [64, 1024, 512], BF16),
           "sg": b.dint("scr_sg", [1024, 1024], BF16),
           "su": b.dint("scr_su", [1024, 1024], BF16),
           "sd": b.dint("scr_sd", [1024, 1024], BF16),
           "gate_w": b.dint("scr_gw1", [64, 1024], BF16)}
    eg = b.din("eg", [64, 512, 1024], F32)
    eu = b.din("eu", [64, 512, 1024], F32)
    ed = b.din("ed", [64, 1024, 512], F32)
    pm = {"gate_w": b.dma("gpsimd", scr["gate_w"], b.din("gw1", [64, 1024], F32)),
          "sg": b.dma("gpsimd", scr["sg"], b.din("sg", [1024, 1024], F32)),
          "su": b.dma("gpsimd", scr["su"], b.din("su", [1024, 1024], F32)),
          "sd": b.dma("gpsimd", scr["sd"], b.din("sd", [1024, 1024], F32)),
          "eg": [b.dma("gpsimd", scr["eg"][e], eg[e]) for e in range(64)],
          "eu": [b.dma("gpsimd", scr["eu"][e], eu[e]) for e in range(64)],
          "ed": [b.dma("gpsimd", scr["ed"][e], ed[e]) for e in range(64)]}
    io["scr"] = scr
    io["prep_marks"] = pm
    emit_ffn(b, io, layer=1)
    b.barrier(dummy)

    io = {"x_in": x1, "s_in": s1, "acc_in": acc1, "x_out": out}
    emit_ln_phase(b, io, first=False, make_xn2=False, final=True)
    _finalize(b)
    return b


def _kernel_fused(inp):
    x = np.ascontiguousarray(np.asarray(inp["L0_attn_x"], np.float32))
    alog_t = np.ascontiguousarray(
        np.asarray(inp["L0_attn_A_log"], np.float32)
        .reshape(16, 128, 16).transpose(1, 0, 2).reshape(128, 256))
    cw_t = np.ascontiguousarray(
        np.asarray(inp["L0_attn_conv_w"], np.float32)
        .reshape(16, 128, 4).transpose(1, 0, 2).reshape(128, 64))
    selmat = np.zeros((32, 32 * 128), dtype=BF)
    for j in range(32):
        selmat[j, j * 128:(j + 1) * 128] = 1
    common = {
        "ln_w_rep": _rep(inp["L0_attn_ln_w"]), "ln_b_rep": _rep(inp["L0_attn_ln_b"]),
        "w_inproj": inp["L0_attn_in_proj_w"], "w_xproj": inp["L0_attn_x_proj_w"],
        "w_dtproj": inp["L0_attn_dt_proj_w"], "w_outproj": inp["L0_attn_out_proj_w"],
        "alog_t": alog_t, "cw_t": cw_t, "cb_t": _t16(inp["L0_attn_conv_b"], 16),
        "dtb_t": _t16(inp["L0_attn_dt_proj_b"], 16),
        "dsk_t": _t16(inp["L0_attn_D_skip"], 16), "selmat": selmat,
        "gate_b_rep0": _rep(inp["L0_ffn_gate_b"]),
        "w13": inp["L0_ffn_w13"], "w2": inp["L0_ffn_w2"],
        "w13s": inp["L0_ffn_w13_s"], "w2s": inp["L0_ffn_w2_s"],
        "gw0": inp["L0_ffn_gate_w"],
        "g1_rep": _rep(inp["L1_attn_gamma1"]), "g2_rep": _rep(inp["L1_attn_gamma2"]),
        "l1g_rep": _rep(inp["L1_attn_ln1_g"]), "l1b_rep": _rep(inp["L1_attn_ln1_b"]),
        "l2g_rep": _rep(inp["L1_attn_ln2_g"]), "l2b_rep": _rep(inp["L1_attn_ln2_b"]),
        "lnh_g_t": _t16(inp["L1_attn_lnh_g"], 32),
        "lnh_b_t": _t16(inp["L1_attn_lnh_b"], 32),
        "fc1_b_t": _t16(inp["L1_attn_fc1_b"], 32),
        "pb_rep": _rep(inp["L1_attn_proj_b"]), "f2b_rep": _rep(inp["L1_attn_fc2_b"]),
        "ones_row": np.ones((1, 128), dtype=BF),
        "ones_col": np.ones((128, 1), dtype=BF),
        "proj_w": inp["L1_attn_proj_w"], "fc1_w": inp["L1_attn_fc1_w"],
        "fc2_w": inp["L1_attn_fc2_w"],
        "gate_b_rep1": _rep(inp["L1_ffn_gate_bias"]),
        "eg": inp["L1_ffn_eg_w"], "eu": inp["L1_ffn_eu_w"], "ed": inp["L1_ffn_ed_w"],
        "sg": inp["L1_ffn_sg_w"], "su": inp["L1_ffn_su_w"], "sd": inp["L1_ffn_sd_w"],
        "gw1": inp["L1_ffn_gate_w"],
    }
    b = _build_fused()
    res = _run(b, [dict(common, x_in=x[i]) for i in range(8)])
    return np.stack([np.asarray(r["out"], np.float32) for r in res], axis=0)


KERNEL_MODE = "fused"


def kernel(**inp):
    if KERNEL_MODE == "fused":
        return _kernel_fused(inp)
    return _kernel_multi(inp)


# version 03
# speedup vs baseline: 1.0205x; 1.0205x over previous
import numpy as np
import ml_dtypes
import concourse.bass as bass
import concourse.mybir as mybir
from concourse.bass_utils import run_bass_kernel_spmd
from contextlib import ExitStack

F32 = mybir.dt.float32
BF16 = mybir.dt.bfloat16
AF = mybir.ActivationFunctionType
ALU = mybir.AluOpType
AX = mybir.AxisListType
BF = ml_dtypes.bfloat16

T = 4096
D = 1024
DI = 2048
NS = 16
DTR = 64
NG = DI // 128
CH = 512
NCHK = T // CH
EPS = 1e-5
NT = T // 128
ENGS = ("sync", "tensor", "vector", "scalar", "gpsimd")
COMPUTE_CTR = {"sync": "q_sync", "tensor": "c_te", "vector": "c_ve",
               "scalar": "c_se", "gpsimd": "q_gp"}


class Ctr:
    def __init__(self, handle):
        self.h = handle
        self.n = 0


ARENA_BYTES = 196608


class Alloc:
    def __init__(self, nc, limit=None):
        arena = nc.alloc_sbuf_tensor("manual_arena", [128, ARENA_BYTES],
                                     mybir.dt.uint8)
        base = nc.lookup_mloc(arena).addr
        self.nc, self.cur, self.uid = nc, base, 0
        self.limit = base + ARENA_BYTES

    def tile(self, name, shape, dtype, align=64):
        self.cur = (self.cur + align - 1) // align * align
        nb = int(np.prod(shape[1:])) * mybir.dt.size(dtype)
        t = self.nc.alloc_sbuf_tensor_at(f"{name}_{self.uid}", list(shape),
                                         dtype, offset=self.cur)
        self.uid += 1
        self.cur += nb
        assert self.cur <= self.limit, f"SBUF overflow at {name}: {self.cur}"
        return t.ap()

    def mark(self):
        return self.cur

    def reset(self, pos):
        self.cur = pos


class B:
    def __init__(self):
        self.nc = bass.Bass("TRN2", target_bir_lowering=False)
        self.ctx = ExitStack()
        self.al = Alloc(self.nc)
        self.sems = {}
        self.psum = [self.nc.alloc_psum_tensor(f"pb{i}", [128, 512], F32).ap()
                     for i in range(8)]
        self.q = {e: [] for e in ENGS}
        self.lastw = {}
        self.readers = {}
        self.watermark = {e: {} for e in ENGS}

    def ctr(self, name):
        if name not in self.sems:
            self.sems[name] = Ctr(self.ctx.enter_context(self.nc.semaphore(name)))
        return self.sems[name]

    def din(self, name, shape, dtype):
        return self.nc.dram_tensor(name, list(shape), dtype,
                                   kind="ExternalInput").ap()

    def dout(self, name, shape, dtype):
        return self.nc.dram_tensor(name, list(shape), dtype,
                                   kind="ExternalOutput").ap()

    def dint(self, name, shape, dtype):
        return self.nc.dram_tensor(name, list(shape), dtype,
                                   kind="Internal").ap()

    def raw(self, eng, fn):
        self.q[eng].append(fn)

    def _wait(self, eng, ctrname, val):
        if val <= 0:
            return
        wm = self.watermark[eng]
        if wm.get(ctrname, 0) >= val:
            return
        wm[ctrname] = val
        h = self.ctr(ctrname).h
        self.q[eng].append(lambda E, h=h, v=val: E.wait_ge(h, v))

    def op(self, eng, fn, reads=(), writes=(), dma=False):
        need = {}
        def _need(c, v):
            if need.get(c, 0) < v:
                need[c] = v
        for r in reads:
            lw = self.lastw.get(r)
            if lw:
                _need(lw[0], lw[1])
        for w in writes:
            lw = self.lastw.get(w)
            if lw:
                _need(lw[0], lw[1])
            for rc, rv in self.readers.get(w, ()):
                _need(rc, rv)
        for c, v in need.items():
            self._wait(eng, c, v)
        cname = COMPUTE_CTR[eng] if not dma else {
            "sync": "q_sync", "gpsimd": "q_gp", "scalar": "q_scl"}[eng]
        c = self.ctr(cname)
        k = 16 if dma else 1
        c.n += k
        self.q[eng].append(lambda E, fn=fn, h=c.h, k=k: fn(E).then_inc(h, k))
        mile = (cname, c.n)
        if not dma and eng in ("vector", "tensor"):
            self.watermark[eng][cname] = c.n
        for r in reads:
            lst = self.readers.setdefault(r, [])
            lst[:] = [m for m in lst if m[0] != cname]
            lst.append(mile)
        for w in writes:
            self.lastw[w] = mile
            self.readers[w] = []
        return mile

    def dma(self, eng, out, in_, reads=(), writes=(), **kw):
        return self.op(eng, lambda E, o=out, i=in_, kw=kw:
                       E.dma_start(out=o, in_=i, **kw),
                       reads=reads, writes=writes, dma=True)

    def wait_mile(self, eng, mile):
        self._wait(eng, mile[0], mile[1])

    def barrier(self, dummy):
        bar = self.ctr("bar")
        ms = []
        self.q["tensor"].append(lambda E, h=bar.h: E.nop().then_inc(h, 1))
        self.q["vector"].append(lambda E, h=bar.h, d=dummy:
                                E.memset(d[:, 0:2], 0.0).then_inc(h, 1))
        self.q["scalar"].append(lambda E, h=bar.h, d=dummy:
                                E.copy(out=d[:, 2:4], in_=d[:, 0:2]).then_inc(h, 1))
        self.q["gpsimd"].append(lambda E, h=bar.h, d=dummy:
                                E.memset(d[:, 4:6], 0.0).then_inc(h, 1))
        bar.n += 4
        tgt = bar.n
        dma_targets = [(n, self.sems[n].n) for n in ("q_sync", "q_gp", "q_scl")
                       if n in self.sems]
        for e in ENGS:
            self.q[e].append(lambda E, h=bar.h, v=tgt: E.wait_ge(h, v))
            self.watermark[e]["bar"] = tgt
            for n, v in dma_targets:
                self._wait(e, n, v)
        self.lastw = {}
        self.readers = {}

    def emit(self):
        blk = self.ctx.enter_context(self.nc.Block())
        for e in ENGS:
            fns = self.q[e]
            if fns:
                getattr(blk, e)(lambda E, fns=fns: [f(E) for f in fns])




def emit_pret(b, jobs, bounce, bkey):
    for src_, dst_, O_, I_, ready in jobs:
        for kc in range(I_ // 128):
            b.wait_mile("sync", ready)
            b.dma("sync", bounce[:, 0:O_], src_[0:O_, kc * 128:(kc + 1) * 128],
                  transpose=True, writes=(bkey,))
            b.dma("scalar", dst_[kc * 128:(kc + 1) * 128, 0:O_], bounce[:, 0:O_],
                  reads=(bkey,), writes=(("pret", id(dst_), kc),))



def emit_ln_phase(b, io, *, first, make_xn2, final):
    al = b.al
    mark = al.mark()
    G = 4
    xb = [al.tile("ln_x", [128, D], F32) for _ in range(G)]
    if not first:
        sb = [al.tile("ln_s", [128, D], F32) for _ in range(G)]
        ab = [al.tile("ln_a", [128, D], F32) for _ in range(G)]
        gb = [al.tile("ln_g", [128, D], F32) for _ in range(G)]
        x1 = [al.tile("ln_x1", [128, D], F32) for _ in range(G)]
    sq = [al.tile("ln_sq", [128, D], F32) for _ in range(G)]
    xn = [al.tile("ln_xn", [128, D], BF16) for _ in range(G)]
    co = [al.tile("ln_co", [128, 16], F32) for _ in range(G)]
    if make_xn2:
        xn2 = [al.tile("ln_xn2", [128, D], BF16) for _ in range(G)]
        t2 = [al.tile("ln_t2", [128, D], F32) for _ in range(G)]
        co2 = [al.tile("ln_co2", [128, 16], F32) for _ in range(G)]
        lnw = al.tile("lnw_rep", [128, D], F32)
        lnb = al.tile("lnb_rep", [128, D], F32)
        b.dma("sync", lnw, io["ln_w_rep"], writes=("lnw",))
        b.dma("sync", lnb, io["ln_b_rep"], writes=("lnb",))

    def ln_chain(tiles, src_of, src_key, out_of, out_key, cob, sqb, affine, pfx):
        K = lambda nm, i: (pfx + nm, i % G)
        for i in tiles:
            c = cob[i % G]
            b.op("vector", lambda E, s=src_of(i), c=c:
                 E.tensor_reduce(out=c[:, 0:1], in_=s, axis=AX.X, op=ALU.add),
                 reads=(src_key(i),), writes=(K("c0", i),))
        for i in tiles:
            c = cob[i % G]
            b.op("scalar", lambda E, c=c:
                 E.activation(out=c[:, 1:2], in_=c[:, 0:1], func=AF.Copy,
                              scale=1.0 / D),
                 reads=(K("c0", i),), writes=(K("mu", i),))
            b.op("scalar", lambda E, s=src_of(i), q=sqb[i % G]:
                 E.activation(out=q, in_=s, func=AF.Square),
                 reads=(src_key(i),), writes=(K("sq", i),))
        for i in tiles:
            c = cob[i % G]
            b.op("vector", lambda E, q=sqb[i % G], c=c:
                 E.tensor_reduce(out=c[:, 2:3], in_=q, axis=AX.X, op=ALU.add),
                 reads=(K("sq", i),), writes=(K("c2", i),))
        for i in tiles:
            c = cob[i % G]
            b.op("scalar", lambda E, c=c:
                 E.activation(out=c[:, 3:4], in_=c[:, 2:3], func=AF.Copy,
                              scale=1.0 / D),
                 reads=(K("c2", i),), writes=(K("e2", i),))
            b.op("scalar", lambda E, c=c:
                 E.activation(out=c[:, 4:5], in_=c[:, 1:2], func=AF.Square),
                 reads=(K("mu", i),), writes=(K("m2", i),))
        for i in tiles:
            c = cob[i % G]
            b.op("vector", lambda E, c=c:
                 E.tensor_tensor(out=c[:, 5:6], in0=c[:, 3:4], in1=c[:, 4:5],
                                 op=ALU.subtract),
                 reads=(K("e2", i), K("m2", i)), writes=(K("var", i),))
        for i in tiles:
            c = cob[i % G]
            b.op("scalar", lambda E, c=c:
                 E.activation(out=c[:, 7:8], in_=c[:, 5:6], func=AF.Copy,
                              bias=EPS),
                 reads=(K("var", i),), writes=(K("veb", i),))
        for i in tiles:
            c = cob[i % G]
            b.op("vector", lambda E, c=c:
                 E.reciprocal(out=c[:, 8:9], in_=c[:, 7:8]),
                 reads=(K("veb", i),), writes=(K("rcp", i),))
        for i in tiles:
            c = cob[i % G]
            b.op("scalar", lambda E, c=c:
                 E.activation(out=c[:, 9:10], in_=c[:, 8:9], func=AF.Sqrt),
                 reads=(K("rcp", i),), writes=(K("rstd", i),))
        for i in tiles:
            c = cob[i % G]
            if affine is None:
                b.op("vector", lambda E, s=src_of(i), o=out_of(i), c=c:
                     E.tensor_scalar(out=o, in0=s, scalar1=c[:, 1:2],
                                     scalar2=c[:, 9:10], op0=ALU.subtract,
                                     op1=ALU.mult),
                     reads=(K("mu", i), K("rstd", i), src_key(i)),
                     writes=(out_key(i),))
            else:
                g_r, b_r = affine
                b.op("vector", lambda E, s=src_of(i), c=c, t=t2[i % G]:
                     E.tensor_scalar(out=t, in0=s, scalar1=c[:, 1:2],
                                     scalar2=c[:, 9:10], op0=ALU.subtract,
                                     op1=ALU.mult),
                     reads=(K("mu", i), K("rstd", i), src_key(i)),
                     writes=(K("t2", i),))
                b.op("vector", lambda E, t=t2[i % G], q=sqb[i % G]:
                     E.tensor_tensor(out=q, in0=t, in1=g_r, op=ALU.mult),
                     reads=(K("t2", i), "lnw"), writes=(K("sq", i),))
                b.op("vector", lambda E, q=sqb[i % G], o=out_of(i):
                     E.tensor_tensor(out=o, in0=q, in1=b_r, op=ALU.add),
                     reads=(K("sq", i), "lnb"), writes=(out_key(i),))

    for g0 in range(0, NT, G):
        tiles = list(range(g0, min(g0 + G, NT)))
        for i in tiles:
            s = i % G
            b.dma("scalar", xb[s], io["x_in"][i * 128:(i + 1) * 128, :],
                  writes=(("xb", s),))
            if not first:
                b.dma("scalar", sb[s], io["s_in"][i * 128:(i + 1) * 128, :],
                      writes=(("sb", s),))
                b.dma("scalar", ab[s], io["acc_in"][i * 128:(i + 1) * 128, :],
                      writes=(("ab", s),))
        if not first:
            for i in tiles:
                s = i % G
                b.op("scalar", lambda E, s=s:
                     E.activation(out=gb[s], in_=ab[s], func=AF.Silu),
                     reads=(("ab", s),), writes=(("gb", s),))
            for i in tiles:
                s = i % G
                b.op("vector", lambda E, s=s:
                     E.tensor_tensor(out=sq[s], in0=sb[s], in1=gb[s], op=ALU.mult),
                     reads=(("sb", s), ("gb", s)), writes=(("Lsq", s),))
                b.op("vector", lambda E, s=s:
                     E.tensor_tensor(out=x1[s], in0=xb[s], in1=sq[s], op=ALU.add),
                     reads=(("Lsq", s), ("xb", s)), writes=(("x1", s),))
            for i in tiles:
                s = i % G
                b.dma("sync", io["x_out"][i * 128:(i + 1) * 128, :], x1[s],
                      reads=(("x1", s),))
            src_of = lambda i: x1[i % G]
            src_key = lambda i: ("x1", i % G)
        else:
            src_of = lambda i: xb[i % G]
            src_key = lambda i: ("xb", i % G)
        if not final:
            ln_chain(tiles, src_of, src_key, lambda i: xn[i % G],
                     lambda i: ("xn", i % G), co, sq, None, "A")
            for i in tiles:
                b.dma("sync", io["xn_out"][i * 128:(i + 1) * 128, :], xn[i % G],
                      reads=(("xn", i % G),))
            if make_xn2:
                ln_chain(tiles, lambda i: xn[i % G], lambda i: ("xn", i % G),
                         lambda i: xn2[i % G], lambda i: ("xn2", i % G),
                         co2, sq, (lnw, lnb), "B")
                for i in tiles:
                    b.dma("sync", io["xn2_out"][i * 128:(i + 1) * 128, :],
                          xn2[i % G], reads=(("xn2", i % G),))
    al.reset(mark)



def emit_ffn(b, io, *, layer):
    al = b.al
    mark = al.mark()
    E_ = 16 if layer == 0 else 64
    QT = 1024
    NQ = T // QT
    QTT = QT // 128
    NCH2 = QT // 512

    xnfm = al.tile("xnfm", [128, 8 * QT], BF16)
    acc = [al.tile("acc", [128, QTT * D], F32) for _ in range(2)]
    hsb = al.tile("hsb", [128, 8 * QT], BF16)
    sgb = al.tile("sgb", [128, 512], BF16)
    uub = al.tile("uub", [128, 512], BF16)
    tmpd = al.tile("tmpd", [128, 512], F32)
    comb = al.tile("comb", [128, QTT * E_], F32)
    combb = al.tile("combb", [128, QTT * E_], F32)
    rsc = al.tile("rsc", [128, 5 * E_ + 96], F32)
    gbr = al.tile("gbr", [128, E_], F32)
    gwt = al.tile("gwt", [128, 8 * E_], BF16)
    dummy = al.tile("ffdum", [128, 16], F32)
    if layer == 0:
        wg = [al.tile("w13t", [128, 8 * 1024], BF16) for _ in range(2)]
        wd = [al.tile("w2t", [128, 4 * 1024], BF16) for _ in range(2)]
    else:
        weg = [al.tile("egt", [128, 8 * 512], BF16) for _ in range(2)]
        weu = [al.tile("eut", [128, 8 * 512], BF16) for _ in range(2)]
        wed = [al.tile("edt", [128, 4 * 1024], BF16) for _ in range(2)]

    scr = io["scr"]
    pm = io["prep_marks"]

    b.dma("sync", gbr, io["gate_b_rep"], writes=("gbr",))
    b.wait_mile("sync", pm["gate_w"])
    for kc in range(8):
        b.dma("sync", gwt[:, kc * E_:(kc + 1) * E_],
              scr["gate_w"][0:E_, kc * 128:(kc + 1) * 128], transpose=True,
              writes=("gwt",))

    def load_shared():
        if layer == 0:
            b.wait_mile("sync", pret_marks["w13_s"])
            b.dma("sync", wg[0].rearrange("p (kc m) -> p kc m", m=1024),
                  scrT["w13_s"].rearrange("(kc p) m -> p kc m", p=128),
                  writes=("wslot0",))
            b.wait_mile("sync", pret_marks["w2_s"])
            b.dma("sync", wd[0].rearrange("p (kc m) -> p kc m", m=1024),
                  scrT["w2_s"].rearrange("(kc p) m -> p kc m", p=128),
                  writes=("wslot0",))
        else:
            pass

    scrT = io["scrT"]
    pret_marks = {}

    def emit_pret_all():
        jobs = []
        if layer == 0:
            jobs.append(("w13_s", scr["w13_s"], scrT["w13_s"], 1024, 1024,
                         pm["w13_s"]))
            jobs.append(("w2_s", scr["w2_s"], scrT["w2_s"], 1024, 512,
                         pm["w2_s"]))
            for e in range(16):
                jobs.append((("w13", e), scr["w13"][e], scrT["w13"][e], 1024,
                             1024, pm["w13"][e]))
                jobs.append((("w2", e), scr["w2"][e], scrT["w2"][e], 1024, 512,
                             pm["w2"][e]))
        else:
            jobs.append(("sg", scr["sg"], scrT["sg"], 1024, 1024, pm["sg"]))
            jobs.append(("su", scr["su"], scrT["su"], 1024, 1024, pm["su"]))
            jobs.append(("sd", scr["sd"], scrT["sd"], 1024, 1024, pm["sd"]))
            for e in range(E_):
                jobs.append((("eg", e), scr["eg"][e], scrT["eg"][e], 512, 1024,
                             pm["eg"][e]))
                jobs.append((("eu", e), scr["eu"][e], scrT["eu"][e], 512, 1024,
                             pm["eu"][e]))
                jobs.append((("ed", e), scr["ed"][e], scrT["ed"][e], 1024, 512,
                             pm["ed"][e]))
        for name, src_, dst_, O_, I_, ready in jobs:
            for kc in range(I_ // 128):
                bb = "xnfmA" if kc % 2 == 0 else "xnfmB"
                off = 0 if kc % 2 == 0 else O_
                b.wait_mile("sync", ready)
                b.dma("sync", xnfm[:, off:off + O_],
                      src_[0:O_, kc * 128:(kc + 1) * 128],
                      transpose=True, writes=(bb,))
                b.dma("scalar", dst_[kc * 128:(kc + 1) * 128, 0:O_],
                      xnfm[:, off:off + O_], reads=(bb,),
                      writes=(("pret", name) if kc == I_ // 128 - 1 else
                              ("pretmid", name),))
            pret_marks[name] = ("q_scl", b.sems["q_scl"].n)

    emit_pret_all()

    def _big(dst, src2d, kcs, width, w):
        b.dma("sync", dst,
              src2d.rearrange("(kc p) m -> p (kc m)", p=128),
              writes=(w,))

    def load_shared1(half, slot):
        w = f"wslot{slot}"
        i0 = half * 512
        b.wait_mile("sync", pret_marks["sg"])
        b.dma("sync", weg[slot].rearrange("p (kc m) -> p kc m", m=512),
              scrT["sg"][:, i0:i0 + 512].rearrange("(kc p) m -> p kc m", p=128),
              writes=(w,))
        b.wait_mile("sync", pret_marks["su"])
        b.dma("sync", weu[slot].rearrange("p (kc m) -> p kc m", m=512),
              scrT["su"][:, i0:i0 + 512].rearrange("(kc p) m -> p kc m", p=128),
              writes=(w,))
        b.wait_mile("sync", pret_marks["sd"])
        b.dma("sync", wed[slot].rearrange("p (kc m) -> p kc m", m=1024),
              scrT["sd"][i0:i0 + 512, :].rearrange("(kc p) m -> p kc m", p=128),
              writes=(w,))

    def load_expert_slot(e, slot):
        w = f"wslot{slot}"
        if layer == 0:
            b.wait_mile("sync", pret_marks[("w13", e)])
            b.dma("sync", wg[slot].rearrange("p (kc m) -> p kc m", m=1024),
                  scrT["w13"][e].rearrange("(kc p) m -> p kc m", p=128),
                  writes=(w,))
            b.wait_mile("sync", pret_marks[("w2", e)])
            b.dma("sync", wd[slot].rearrange("p (kc m) -> p kc m", m=1024),
                  scrT["w2"][e].rearrange("(kc p) m -> p kc m", p=128),
                  writes=(w,))
        else:
            b.wait_mile("sync", pret_marks[("eg", e)])
            b.dma("sync", weg[slot].rearrange("p (kc m) -> p kc m", m=512),
                  scrT["eg"][e].rearrange("(kc p) m -> p kc m", p=128),
                  writes=(w,))
            b.wait_mile("sync", pret_marks[("eu", e)])
            b.dma("sync", weu[slot].rearrange("p (kc m) -> p kc m", m=512),
                  scrT["eu"][e].rearrange("(kc p) m -> p kc m", p=128),
                  writes=(w,))
            b.wait_mile("sync", pret_marks[("ed", e)])
            b.dma("sync", wed[slot].rearrange("p (kc m) -> p kc m", m=1024),
                  scrT["ed"][e].rearrange("(kc p) m -> p kc m", p=128),
                  writes=(w,))

    def routing(q):
        for tt in range(QTT):
            bank = b.psum[tt % 2]
            bres = f"pbank{tt % 2}"
            mm = b.op("tensor", lambda E, bank=bank, tt=tt: [
                E.matmul(bank[:, 0:E_],
                         xnfm[:, kc * QT + tt * 128: kc * QT + (tt + 1) * 128],
                         gwt[:, kc * E_:(kc + 1) * E_],
                         start=(kc == 0), stop=(kc == 7))
                for kc in range(8)][-1],
                reads=("xnfm", "gwt"), writes=(bres,))
            sc_t = rsc[:, 0:E_]
            b.op("scalar", lambda E, bank=bank, o=sc_t:
                 E.activation(out=o, in_=bank[:, 0:E_], func=AF.Sigmoid),
                 reads=(bres,), writes=(("sc",),))
            bi = rsc[:, E_:2 * E_]
            b.op("vector", lambda E, o=bi, a=sc_t:
                 E.tensor_tensor(out=o, in0=a, in1=gbr, op=ALU.add),
                 reads=(("sc",), "gbr", bres), writes=(("bi",),))
            bi2 = rsc[:, 2 * E_:3 * E_]
            b.op("scalar", lambda E, o=bi2, a=bi: E.copy(out=o, in_=a),
                 reads=(("bi",),), writes=(("bi2",),))
            x8 = rsc[:, 4 * E_:4 * E_ + 8]
            if layer == 0:
                b.op("vector", lambda E, a=bi2, o=x8: E.max(out=o, in_=a),
                     reads=(("bi2",),), writes=(("x8",),))
                th = rsc[:, 4 * E_ + 8:4 * E_ + 9]
                b.op("scalar", lambda E, o=th, a=x8: E.copy(out=o, in_=a[:, 3:4]),
                     reads=(("x8",),), writes=(("th",),))
                sel = rsc[:, 3 * E_:4 * E_]
                b.op("vector", lambda E, o=sel, a=bi2, t=th:
                     E.tensor_scalar(out=o, in0=a, scalar1=t, scalar2=None,
                                     op0=ALU.is_ge),
                     reads=(("bi2",), ("th",)), writes=(("sel",),))
                selb = rsc[:, 4 * E_ + 80:5 * E_ + 80]
                b.op("scalar", lambda E, o=selb, a=sel: E.copy(out=o, in_=a),
                     reads=(("sel",),), writes=(("selb",),))
                w_ = rsc[:, 4 * E_ + 16:5 * E_ + 16]
                b.op("vector", lambda E, o=w_, a=sc_t, s=selb:
                     E.tensor_tensor(out=o, in0=a, in1=s, op=ALU.mult),
                     reads=(("selb",), ("sc",)), writes=(("w",),))
                w2_ = rsc[:, 5 * E_ + 16:6 * E_ + 16]
                b.op("scalar", lambda E, o=w2_, a=w_: E.copy(out=o, in_=a),
                     reads=(("w",),), writes=(("w2",),))
                sm = rsc[:, 4 * E_ + 9:4 * E_ + 10]
                b.op("vector", lambda E, o=sm, a=w2_:
                     E.tensor_reduce(out=o, in_=a, axis=AX.X, op=ALU.add),
                     reads=(("w2",),), writes=(("sm",),))
                sm2 = rsc[:, 4 * E_ + 10:4 * E_ + 11]
                b.op("scalar", lambda E, o=sm2, a=sm: E.copy(out=o, in_=a),
                     reads=(("sm",),), writes=(("sm2",),))
                rc = rsc[:, 4 * E_ + 11:4 * E_ + 12]
                b.op("vector", lambda E, o=rc, a=sm2: E.reciprocal(out=o, in_=a),
                     reads=(("sm2",),), writes=(("rc",),))
                rc2 = rsc[:, 4 * E_ + 12:4 * E_ + 13]
                b.op("scalar", lambda E, o=rc2, a=rc:
                     E.activation(out=o, in_=a, func=AF.Copy, scale=2.5),
                     reads=(("rc",),), writes=(("rc2",),))
                b.op("vector", lambda E, o=comb[:, tt * E_:(tt + 1) * E_], a=w_,
                     r=rc2:
                     E.tensor_scalar(out=o, in0=a, scalar1=r, scalar2=None,
                                     op0=ALU.mult),
                     reads=(("w",), ("rc2",)), writes=(("comb",),))
                if tt == 0 and q == 0 and "dbg_out" in io:
                    b.dma("sync", io["dbg_out"], rsc,
                          reads=(("comb",), ("sc",), ("bi",), ("bi2",),
                                 ("sel",), ("x8",), ("w",), ("w2",)))
            else:
                gm = rsc[:, 4 * E_ + 16:4 * E_ + 24]
                b.op("vector", lambda E, a=bi2, o=gm:
                     E.tensor_reduce(out=o,
                                     in_=a.rearrange("p (g e) -> p g e", e=8),
                                     axis=AX.X, op=ALU.max),
                     reads=(("bi2",),), writes=(("gm",),))
                gm2 = rsc[:, 4 * E_ + 24:4 * E_ + 32]
                b.op("scalar", lambda E, o=gm2, a=gm: E.copy(out=o, in_=a),
                     reads=(("gm",),), writes=(("gm2",),))
                b.op("vector", lambda E, a=gm2, o=x8: E.max(out=o, in_=a),
                     reads=(("gm2",),), writes=(("x8",),))
                thg = rsc[:, 4 * E_ + 8:4 * E_ + 9]
                b.op("scalar", lambda E, o=thg, a=x8: E.copy(out=o, in_=a[:, 3:4]),
                     reads=(("x8",),), writes=(("thg",),))
                gmask = rsc[:, 4 * E_ + 32:4 * E_ + 40]
                b.op("vector", lambda E, o=gmask, a=gm2, t=thg:
                     E.tensor_scalar(out=o, in0=a, scalar1=t, scalar2=None,
                                     op0=ALU.is_ge),
                     reads=(("gm2",), ("thg",)), writes=(("gmk",),))
                gmask2 = rsc[:, 4 * E_ + 40:4 * E_ + 48]
                b.op("scalar", lambda E, o=gmask2, a=gmask: E.copy(out=o, in_=a),
                     reads=(("gmk",),), writes=(("gmk2",),))
                masked = rsc[:, 3 * E_:4 * E_]
                b.op("vector", lambda E, o=masked, a=bi2, s=gmask2: [
                     E.tensor_scalar(out=o[:, g * 8:(g + 1) * 8],
                                     in0=a[:, g * 8:(g + 1) * 8],
                                     scalar1=s[:, g:g + 1], scalar2=None,
                                     op0=ALU.mult) for g in range(8)][-1],
                     reads=(("bi2",), ("gmk2",)), writes=(("msk",),))
                masked2 = comb[:, tt * E_:(tt + 1) * E_]
                b.op("scalar", lambda E, o=masked2, a=masked: E.copy(out=o, in_=a),
                     reads=(("msk",),), writes=(("msk2",),))
                b.op("vector", lambda E, a=masked2, o=x8: E.max(out=o, in_=a),
                     reads=(("msk2",),), writes=(("x8",),))
                th6 = rsc[:, 4 * E_ + 9:4 * E_ + 10]
                b.op("scalar", lambda E, o=th6, a=x8: E.copy(out=o, in_=a[:, 5:6]),
                     reads=(("x8",),), writes=(("th6",),))
                sel = rsc[:, 0:E_]
                b.op("vector", lambda E, o=sel, a=masked2, t=th6:
                     E.tensor_scalar(out=o, in0=a, scalar1=t, scalar2=None,
                                     op0=ALU.is_ge),
                     reads=(("msk2",), ("th6",)), writes=(("sel",),))
                selb = rsc[:, 4 * E_ + 80:5 * E_ + 80]
                b.op("scalar", lambda E, o=selb, a=sel: E.copy(out=o, in_=a),
                     reads=(("sel",),), writes=(("selb",),))
                w_ = rsc[:, E_:2 * E_]
                b.op("vector", lambda E, o=w_, a=masked2, s=selb:
                     E.tensor_tensor(out=o, in0=a, in1=s, op=ALU.mult),
                     reads=(("selb",), ("msk2",)), writes=(("w",),))
                w2_ = rsc[:, 2 * E_:3 * E_]
                b.op("scalar", lambda E, o=w2_, a=w_: E.copy(out=o, in_=a),
                     reads=(("w",),), writes=(("w2",),))
                sm = rsc[:, 4 * E_ + 10:4 * E_ + 11]
                b.op("vector", lambda E, o=sm, a=w2_:
                     E.tensor_reduce(out=o, in_=a, axis=AX.X, op=ALU.add),
                     reads=(("w2",),), writes=(("sm",),))
                sm2 = rsc[:, 4 * E_ + 11:4 * E_ + 12]
                b.op("scalar", lambda E, o=sm2, a=sm:
                     E.activation(out=o, in_=a, func=AF.Copy, bias=1e-8),
                     reads=(("sm",),), writes=(("sm2",),))
                rc = rsc[:, 4 * E_ + 12:4 * E_ + 13]
                b.op("vector", lambda E, o=rc, a=sm2: E.reciprocal(out=o, in_=a),
                     reads=(("sm2",),), writes=(("rc",),))
                rc2 = rsc[:, 4 * E_ + 13:4 * E_ + 14]
                b.op("scalar", lambda E, o=rc2, a=rc: E.copy(out=o, in_=a),
                     reads=(("rc",),), writes=(("rc2",),))
                b.op("vector", lambda E, o=combb[:, tt * E_:(tt + 1) * E_],
                     a=w_, r=rc2:
                     E.tensor_scalar(out=o, in0=a, scalar1=r, scalar2=None,
                                     op0=ALU.mult),
                     reads=(("w",), ("rc2",)), writes=(("combv",),))
        src_, dst_ = (comb, combb) if layer == 0 else (combb, comb)
        b.op("scalar", lambda E, s=src_, d=dst_:
             E.copy(out=d[:, 0:QTT * E_], in_=s[:, 0:QTT * E_]),
             reads=(("comb",) if layer == 0 else ("combv",),),
             writes=("combf",))
        return dst_

    def expert_compute(kind, slot, acc_src, acc_dst, first_contrib, e_idx,
                       combf):
        if kind in ("exp0", "sh0"):
            n_inter = 512
            gm_ = lambda kc, msl: wg[slot][:, kc * 1024 + msl.start:
                                           kc * 1024 + msl.stop]
            um_ = lambda kc, msl: wg[slot][:, kc * 1024 + 512 + msl.start:
                                           kc * 1024 + 512 + msl.stop]
            dm_ = lambda kc2, nc_: wd[slot][:, kc2 * 1024 + nc_ * 512:
                                            kc2 * 1024 + (nc_ + 1) * 512]
            wres = (f"wslot{slot}",)
        elif kind == "exp1":
            n_inter = 512
            gm_ = lambda kc, msl: weg[slot][:, kc * 512 + msl.start:
                                            kc * 512 + msl.stop]
            um_ = lambda kc, msl: weu[slot][:, kc * 512 + msl.start:
                                            kc * 512 + msl.stop]
            dm_ = lambda kc2, nc_: wed[slot][:, kc2 * 1024 + nc_ * 512:
                                             kc2 * 1024 + (nc_ + 1) * 512]
            wres = (f"wslot{slot}",)
        NMT = n_inter // 128
        for mt in range(NMT):
            msl = slice(mt * 128, (mt + 1) * 128)
            for nch in range(NCH2):
                gb_ = b.psum[(2 * (mt * NCH2 + nch)) % 4]
                ub_ = b.psum[(2 * (mt * NCH2 + nch) + 1) % 4]
                gbr_ = f"pbank{(2 * (mt * NCH2 + nch)) % 4}"
                ubr_ = f"pbank{(2 * (mt * NCH2 + nch) + 1) % 4}"
                b.op("tensor", lambda E, gb_=gb_, msl=msl, nch=nch, gm_=gm_: [
                     E.matmul(gb_[:, 0:512], gm_(kc, msl),
                              xnfm[:, kc * QT + nch * 512:kc * QT + (nch + 1) * 512],
                              start=(kc == 0), stop=(kc == 7))
                     for kc in range(8)][-1],
                     reads=wres + ("xnfm",), writes=(gbr_,))
                b.op("tensor", lambda E, ub_=ub_, msl=msl, nch=nch, um_=um_: [
                     E.matmul(ub_[:, 0:512], um_(kc, msl),
                              xnfm[:, kc * QT + nch * 512:kc * QT + (nch + 1) * 512],
                              start=(kc == 0), stop=(kc == 7))
                     for kc in range(8)][-1],
                     reads=wres + ("xnfm",), writes=(ubr_,))
                b.op("scalar", lambda E, gb_=gb_:
                     E.activation(out=sgb, in_=gb_[:, 0:512], func=AF.Silu),
                     reads=(gbr_,), writes=("sgb",))
                b.op("vector", lambda E, ub_=ub_:
                     E.tensor_copy(out=uub, in_=ub_[:, 0:512]),
                     reads=(ubr_,), writes=("uub",))
                hsl = slice(mt * QT + nch * 512, mt * QT + (nch + 1) * 512)
                b.op("vector", lambda E, hsl=hsl:
                     E.tensor_tensor(out=hsb[:, hsl], in0=sgb, in1=uub,
                                     op=ALU.mult),
                     reads=("sgb", "uub"), writes=(("h", mt, nch),))
        NK2 = n_inter // 128
        hreads = tuple(("h", mt, nch) for mt in range(NMT) for nch in range(NCH2))
        for tt in range(QTT):
            for nc_ in range(2):
                bank = b.psum[4 + (tt * 2 + nc_) % 4]
                bres = f"pbank{4 + (tt * 2 + nc_) % 4}"
                b.op("tensor", lambda E, bank=bank, tt=tt, nc_=nc_, NK2=NK2,
                     dm_=dm_: [
                     E.matmul(bank[:, 0:512],
                              hsb[:, kc2 * QT + tt * 128:kc2 * QT + (tt + 1) * 128],
                              dm_(kc2, nc_),
                              start=(kc2 == 0), stop=(kc2 == NK2 - 1))
                     for kc2 in range(NK2)][-1],
                     reads=hreads + wres, writes=(bres,))
                dsl = slice(tt * D + nc_ * 512, tt * D + (nc_ + 1) * 512)
                if first_contrib:
                    b.op("vector", lambda E, bank=bank, dsl=dsl, a=acc_dst:
                         E.tensor_copy(out=a[:, dsl], in_=bank[:, 0:512]),
                         reads=(bres,), writes=(("acc", id(acc_dst), tt, nc_),))
                elif e_idx is None:
                    b.op("vector", lambda E, bank=bank, dsl=dsl, s=acc_src,
                         a=acc_dst:
                         E.tensor_tensor(out=a[:, dsl], in0=s[:, dsl],
                                         in1=bank[:, 0:512], op=ALU.add),
                         reads=(bres, ("acc", id(acc_src), tt, nc_)),
                         writes=(("acc", id(acc_dst), tt, nc_),))
                else:
                    ccol = combf[:, tt * E_ + e_idx:tt * E_ + e_idx + 1]
                    b.op("vector", lambda E, bank=bank, c=ccol:
                         E.tensor_scalar(out=tmpd, in0=bank[:, 0:512], scalar1=c,
                                         scalar2=None, op0=ALU.mult),
                         reads=(bres, "combf"), writes=("tmpd",))
                    b.op("vector", lambda E, dsl=dsl, s=acc_src, a=acc_dst:
                         E.tensor_tensor(out=a[:, dsl], in0=s[:, dsl], in1=tmpd,
                                         op=ALU.add),
                         reads=("tmpd", ("acc", id(acc_src), tt, nc_)),
                         writes=(("acc", id(acc_dst), tt, nc_),))

    if layer == 0:
        load_shared()
    for q in range(NQ):
        for kc in range(8):
            b.dma("sync", xnfm[:, kc * QT:(kc + 1) * QT],
                  io["xn_in"][q * QT:(q + 1) * QT, kc * 128:(kc + 1) * 128],
                  transpose=True, writes=("xnfm", "xnfmA", "xnfmB"))
        combf = routing(q)
        if "comb_out" in io:
            for tt in range(QTT):
                b.dma("sync", io["comb_out"][(q * QTT + tt) * 128:
                                             (q * QTT + tt + 1) * 128, :],
                      combf[:, tt * E_:(tt + 1) * E_], reads=("combf",))
        nexp = io.get("nexp", E_)
        if layer == 0:
            expert_compute("sh0", 0, None, acc[0], True, None, combf)
            nsh = 0
        else:
            load_shared1(0, 0)
            expert_compute("exp1", 0, None, acc[0], True, None, combf)
            load_shared1(1, 1)
            expert_compute("exp1", 1, acc[0], acc[1], False, None, combf)
            nsh = 1
        for e in range(nexp):
            slot = (e + nsh + 1) % 2
            load_expert_slot(e, slot)
            expert_compute("exp0" if layer == 0 else "exp1", slot,
                           acc[(e + nsh) % 2], acc[(e + nsh + 1) % 2], False, e,
                           combf)
        final_acc = acc[(nexp + nsh) % 2]
        for tt in range(QTT):
            b.dma("sync", io["acc_out"][(q * QTT + tt) * 128:
                                        (q * QTT + tt + 1) * 128, :],
                  final_acc[:, tt * D:(tt + 1) * D],
                  reads=(("acc", id(final_acc), tt, 0),
                         ("acc", id(final_acc), tt, 1)))
        if layer == 0:
            if q < NQ - 1:
                load_shared()
    al.reset(mark)



def emit_attn1(b, io):
    al = b.al
    mark = al.mark()
    C2 = 512
    NC2 = T // C2
    CT = C2 // 128

    projT = al.tile("projT", [128, 8 * 1024], BF16)
    f1sl = [al.tile("f1sl", [128, 8 * 128], BF16) for _ in range(2)]
    f2sl = al.tile("f2sl", [128, 32 * 256], BF16)
    g1r = al.tile("g1r", [128, D], BF16)
    g2r = al.tile("g2r", [128, D], BF16)
    pbr = al.tile("pbr", [128, D], BF16)
    f2br = al.tile("f2br", [128, D], BF16)
    l1g = al.tile("l1g", [128, D], BF16)
    l1b = al.tile("l1b", [128, D], BF16)
    l2g = al.tile("l2g", [128, D], BF16)
    l2b = al.tile("l2b", [128, D], BF16)
    lhg = al.tile("lhg", [128, 32], F32)
    lhb = al.tile("lhb", [128, 32], F32)
    f1b = al.tile("f1b", [128, 32], F32)
    ones1 = al.tile("ones1", [1, 128], BF16)
    onecol = al.tile("onecol", [128, 1], BF16)
    xnt = [al.tile("at_xn", [128, D], BF16) for _ in range(4)]
    y1t = [al.tile("at_y1", [128, D], BF16) for _ in range(2)]
    t1t = [al.tile("at_t1", [128, D], F32) for _ in range(4)]
    zt = [al.tile("at_z", [128, D], BF16) for _ in range(2)]
    s1t = [al.tile("at_s1", [128, D], F32) for _ in range(4)]
    co = [al.tile("at_co", [128, 16], F32) for _ in range(4)]
    co2 = [al.tile("at_co2", [128, 16], F32) for _ in range(4)]
    sq = [al.tile("at_sq", [128, D], F32) for _ in range(2)]
    t2a = [al.tile("at_t2", [128, D], F32) for _ in range(2)]
    y1fm = al.tile("y1fm", [128, 8 * C2], BF16)
    zfm = al.tile("zfm", [128, 8 * C2], BF16)
    hfm = al.tile("hfm", [128, 32 * C2], BF16)
    hsqt = [al.tile("hsqt", [128, C2], BF16) for _ in range(2)]
    rowb = al.tile("rowb", [1, 3 * C2], F32)
    rowbb = al.tile("rowbb", [1, 2 * C2], BF16)
    d1t = [al.tile("at_d1", [128, C2], F32) for _ in range(2)]
    d2t = [al.tile("at_d2", [128, C2], F32) for _ in range(2)]
    tmpv = al.tile("at_tmpv", [128, 512], F32)
    tmpw = al.tile("at_tmpw", [128, 512], F32)

    scr = io["scr"]
    pm = io["prep_marks"]

    for dst, nm in ((lhg, "lnh_g_t"), (lhb, "lnh_b_t"), (f1b, "fc1_b_t")):
        b.dma("sync", dst, io[nm], writes=("consts",))
    b.dma("sync", ones1, io["ones_row"], writes=("consts",))
    b.dma("sync", onecol, io["ones_col"], writes=("consts",))
    for dst, nm in ((g1r, "g1_rep"), (g2r, "g2_rep"), (pbr, "pb_rep"),
                    (f2br, "f2b_rep"), (l1g, "l1g_rep"), (l1b, "l1b_rep"),
                    (l2g, "l2g_rep"), (l2b, "l2b_rep")):
        b.dma("gpsimd", dst, io[nm], writes=("consts",))

    scrT = io["scrT"]
    for name, src_, dst_, O_, I_, ready in (
            ("proj_w", scr["proj_w"], scrT["proj_w"], 1024, 1024, pm["proj_w"]),
            ("fc1_w", scr["fc1_w"], scrT["fc1_w"], 4096, 1024, pm["fc1_w"]),
            ("fc2_w", scr["fc2_w"], scrT["fc2_w"], 1024, 4096, pm["fc2_w"])):
        for kc in range(I_ // 128):
            b.wait_mile("sync", ready)
            b.dma("sync", y1fm[:, 0:O_], src_[0:O_, kc * 128:(kc + 1) * 128],
                  transpose=True, writes=("y1fm",))
            b.dma("scalar", dst_[kc * 128:(kc + 1) * 128, 0:O_], y1fm[:, 0:O_],
                  reads=("y1fm",), writes=(("pret", name, kc),))
    pret_m = ("q_scl", b.sems["q_scl"].n)
    b.wait_mile("sync", pret_m)
    b.dma("sync", projT.rearrange("p (kc m) -> p kc m", m=1024),
          scrT["proj_w"].rearrange("(kc p) m -> p kc m", p=128),
          writes=("projT",))

    def ln_tile(i, src, src_key, out, out_key, cob, gg, bb, pfx):
        c = cob
        K = lambda nm: (pfx + nm, i)
        sqb, t2b = sq[i % 2], t2a[i % 2]
        b.op("vector", lambda E: E.tensor_reduce(out=c[:, 0:1], in_=src,
                                                 axis=AX.X, op=ALU.add),
             reads=(src_key,), writes=(K("c0"),))
        b.op("scalar", lambda E: E.activation(out=c[:, 1:2], in_=c[:, 0:1],
                                              func=AF.Copy, scale=1.0 / D),
             reads=(K("c0"),), writes=(K("mu"),))
        b.op("scalar", lambda E: E.activation(out=sqb, in_=src, func=AF.Square),
             reads=(src_key,), writes=(("SQb", i % 2),))
        b.op("vector", lambda E: E.tensor_reduce(out=c[:, 2:3], in_=sqb,
                                                 axis=AX.X, op=ALU.add),
             reads=(("SQb", i % 2),), writes=(K("c2"),))
        b.op("scalar", lambda E: E.activation(out=c[:, 3:4], in_=c[:, 2:3],
                                              func=AF.Copy, scale=1.0 / D),
             reads=(K("c2"),), writes=(K("e2"),))
        b.op("scalar", lambda E: E.activation(out=c[:, 4:5], in_=c[:, 1:2],
                                              func=AF.Square),
             reads=(K("mu"),), writes=(K("m2"),))
        b.op("vector", lambda E: E.tensor_tensor(out=c[:, 5:6], in0=c[:, 3:4],
                                                 in1=c[:, 4:5], op=ALU.subtract),
             reads=(K("e2"), K("m2")), writes=(K("var"),))
        b.op("scalar", lambda E: E.activation(out=c[:, 7:8], in_=c[:, 5:6],
                                              func=AF.Copy, bias=EPS),
             reads=(K("var"),), writes=(K("veb"),))
        b.op("vector", lambda E: E.reciprocal(out=c[:, 8:9], in_=c[:, 7:8]),
             reads=(K("veb"),), writes=(K("rcp"),))
        b.op("scalar", lambda E: E.activation(out=c[:, 9:10], in_=c[:, 8:9],
                                              func=AF.Sqrt),
             reads=(K("rcp"),), writes=(K("rstd"),))
        b.op("vector", lambda E: E.tensor_scalar(out=t2b, in0=src,
                                                 scalar1=c[:, 1:2],
                                                 scalar2=c[:, 9:10],
                                                 op0=ALU.subtract, op1=ALU.mult),
             reads=(K("mu"), K("rstd"), src_key), writes=(("T2b", i % 2),))
        b.op("vector", lambda E: E.tensor_tensor(out=sqb, in0=t2b, in1=gg,
                                                 op=ALU.mult),
             reads=(("T2b", i % 2), "consts"), writes=(("SQb", i % 2),))
        b.op("vector", lambda E: E.tensor_tensor(out=out, in0=sqb, in1=bb,
                                                 op=ALU.add),
             reads=(("SQb", i % 2), "consts"), writes=(out_key,))

    for ch in range(NC2):
        t0 = ch * C2
        for i in range(CT):
            b.dma("scalar", xnt[i],
                  io["xn_in"][t0 + i * 128:t0 + (i + 1) * 128, :],
                  writes=(("axn", i),))
        for i in range(CT):
            ln_tile(i, xnt[i], ("axn", i), y1t[i % 2], ("ay1", i % 2), co[i],
                    l1g, l1b, "p")
            b.dma("sync", io["y1tm"][t0 + i * 128:t0 + (i + 1) * 128, :],
                  y1t[i % 2], reads=(("ay1", i % 2),), writes=(("y1tm", i),))
        for kc in range(8):
            b.dma("sync", y1fm[:, kc * C2:(kc + 1) * C2],
                  io["y1tm"][t0:t0 + C2, kc * 128:(kc + 1) * 128], transpose=True,
                  reads=tuple(("y1tm", i) for i in range(CT)), writes=("y1fm",))
        for i in range(CT):
            for nc_ in range(2):
                bank = b.psum[(i * 2 + nc_) % 4]
                bres = f"pk{(i * 2 + nc_) % 4}"
                b.op("tensor", lambda E, bank=bank, i=i, nc_=nc_: [
                     E.matmul(bank[:, 0:512],
                              y1fm[:, kc * C2 + i * 128:kc * C2 + (i + 1) * 128],
                              projT[:, kc * 1024 + nc_ * 512:
                                    kc * 1024 + (nc_ + 1) * 512],
                              start=(kc == 0), stop=(kc == 7))
                     for kc in range(8)][-1],
                     reads=("y1fm", "projT"), writes=(bres,))
                dsl = slice(nc_ * 512, (nc_ + 1) * 512)
                b.op("vector", lambda E, bank=bank, dsl=dsl:
                     E.tensor_tensor(out=tmpv, in0=bank[:, 0:512],
                                     in1=pbr[:, dsl], op=ALU.add),
                     reads=(bres, "consts"), writes=(("tmpv",),))
                b.op("vector", lambda E, dsl=dsl:
                     E.tensor_tensor(out=tmpw, in0=tmpv, in1=g1r[:, dsl],
                                     op=ALU.mult),
                     reads=(("tmpv",), "consts"), writes=(("tmpw",),))
                b.op("vector", lambda E, i=i, dsl=dsl:
                     E.tensor_tensor(out=t1t[i][:, dsl], in0=tmpw,
                                     in1=xnt[i][:, dsl], op=ALU.add),
                     reads=(("tmpw",), ("axn", i)), writes=(("t1", i),))
        for i in range(CT):
            ln_tile(i, t1t[i], ("t1", i), zt[i % 2], ("az", i % 2), co2[i],
                    l2g, l2b, "q")
            b.dma("sync", io["ztm"][t0 + i * 128:t0 + (i + 1) * 128, :], zt[i % 2],
                  reads=(("az", i % 2),), writes=(("ztm", i),))
        for kc in range(8):
            b.dma("sync", zfm[:, kc * C2:(kc + 1) * C2],
                  io["ztm"][t0:t0 + C2, kc * 128:(kc + 1) * 128], transpose=True,
                  reads=tuple(("ztm", i) for i in range(CT)), writes=("zfm",))
        for mt in range(32):
            sl = mt % 2
            b.wait_mile("sync", pret_m)
            b.dma("sync", f1sl[sl].rearrange("p (kc m) -> p kc m", m=128),
                  scrT["fc1_w"][:, mt * 128:(mt + 1) * 128]
                  .rearrange("(kc p) m -> p kc m", p=128),
                  writes=((f"f1sl{sl}",),))
            bank = b.psum[mt % 4]
            bres = f"pk{mt % 4}"
            b.op("tensor", lambda E, bank=bank, sl=sl: [
                 E.matmul(bank[:, 0:C2], f1sl[sl][:, kc * 128:(kc + 1) * 128],
                          zfm[:, kc * C2:(kc + 1) * C2],
                          start=(kc == 0), stop=(kc == 7))
                 for kc in range(8)][-1],
                 reads=("zfm", (f"f1sl{sl}",)), writes=(bres,))
            b.op("scalar", lambda E, bank=bank, mt=mt:
                 E.activation(out=hfm[:, mt * C2:(mt + 1) * C2],
                              in_=bank[:, 0:C2], func=AF.Gelu,
                              bias=f1b[:, mt:mt + 1]),
                 reads=(bres, "consts"), writes=(("hf", mt),))
        hreads = tuple(("hf", mt) for mt in range(32))
        b.op("tensor", lambda E: [
             E.matmul(b.psum[4][0:1, 0:C2], onecol[:, 0:1],
                      hfm[:, mt * C2:(mt + 1) * C2],
                      start=(mt == 0), stop=(mt == 31))
             for mt in range(32)][-1],
             reads=hreads + ("consts",), writes=("pk4",))
        for mt in range(32):
            b.op("scalar", lambda E, mt=mt:
                 E.activation(out=hsqt[mt % 2], in_=hfm[:, mt * C2:(mt + 1) * C2],
                              func=AF.Square),
                 reads=(("hf", mt),), writes=((f"hsqt{mt % 2}",),))
            b.op("tensor", lambda E, mt=mt:
                 E.matmul(b.psum[5][0:1, 0:C2], onecol[:, 0:1], hsqt[mt % 2],
                          start=(mt == 0), stop=(mt == 31)),
                 reads=((f"hsqt{mt % 2}",), "consts"),
                 writes=("pk5",) if mt in (0, 31) else ())
        b.op("scalar", lambda E: E.activation(out=rowb[0:1, 0:C2],
                                              in_=b.psum[4][0:1, 0:C2],
                                              func=AF.Copy, scale=1.0 / 4096),
             reads=("pk4",), writes=("rmu",))
        b.op("vector", lambda E: E.tensor_copy(out=rowbb[0:1, 0:C2],
                                               in_=rowb[0:1, 0:C2]),
             reads=("rmu",), writes=("rbmu",))
        b.op("scalar", lambda E: E.activation(out=rowb[0:1, C2:2 * C2],
                                              in_=b.psum[5][0:1, 0:C2],
                                              func=AF.Copy, scale=1.0 / 4096),
             reads=("pk5",), writes=("re2",))
        b.op("scalar", lambda E: E.activation(out=rowb[0:1, 2 * C2:3 * C2],
                                              in_=rowb[0:1, 0:C2],
                                              func=AF.Square),
             reads=("rmu", "rbmu"), writes=("rm2",))
        b.op("vector", lambda E: E.tensor_tensor(out=rowb[0:1, 0:C2],
                                                 in0=rowb[0:1, C2:2 * C2],
                                                 in1=rowb[0:1, 2 * C2:3 * C2],
                                                 op=ALU.subtract),
             reads=("re2", "rm2", "rbmu"), writes=("rvar",))
        b.op("scalar", lambda E: E.activation(out=rowb[0:1, C2:2 * C2],
                                              in_=rowb[0:1, 0:C2],
                                              func=AF.Copy, bias=EPS),
             reads=("rvar",), writes=("rveb",))
        b.op("vector", lambda E: E.reciprocal(out=rowb[0:1, 0:C2],
                                              in_=rowb[0:1, C2:2 * C2]),
             reads=("rveb",), writes=("rrcp",))
        b.op("scalar", lambda E: E.activation(out=rowb[0:1, 2 * C2:3 * C2],
                                              in_=rowb[0:1, 0:C2],
                                              func=AF.Sqrt),
             reads=("rrcp",), writes=("rrstd",))
        b.op("vector", lambda E: E.tensor_copy(out=rowbb[0:1, C2:2 * C2],
                                               in_=rowb[0:1, 2 * C2:3 * C2]),
             reads=("rrstd",), writes=("rbrstd",))
        b.op("tensor", lambda E: E.matmul(b.psum[6][:, 0:C2], ones1[0:1, 0:128],
                                          rowbb[0:1, 0:C2], start=True,
                                          stop=True),
             reads=("rbmu", "consts"), writes=("pk6",))
        b.op("tensor", lambda E: E.matmul(b.psum[7][:, 0:C2], ones1[0:1, 0:128],
                                          rowbb[0:1, C2:2 * C2], start=True,
                                          stop=True),
             reads=("rbrstd", "consts"), writes=("pk7",))
        for mt in range(32):
            b.op("vector", lambda E, mt=mt:
                 E.tensor_tensor(out=d1t[mt % 2],
                                 in0=hfm[:, mt * C2:(mt + 1) * C2],
                                 in1=b.psum[6][:, 0:C2], op=ALU.subtract),
                 reads=(("hf", mt), "pk6"), writes=((f"d1t{mt % 2}",),))
            b.op("vector", lambda E, mt=mt:
                 E.tensor_tensor(out=d2t[mt % 2], in0=d1t[mt % 2],
                                 in1=b.psum[7][:, 0:C2], op=ALU.mult),
                 reads=((f"d1t{mt % 2}",), "pk7"), writes=((f"d2t{mt % 2}",),))
            b.op("scalar", lambda E, mt=mt:
                 E.activation(out=hfm[:, mt * C2:(mt + 1) * C2], in_=d2t[mt % 2],
                              func=AF.Identity, scale=lhg[:, mt:mt + 1],
                              bias=lhb[:, mt:mt + 1]),
                 reads=((f"d2t{mt % 2}",), "consts"), writes=(("hf", mt),))
        hreads2 = tuple(("hf", mt) for mt in range(32))
        for osl in range(4):
            b.wait_mile("sync", pret_m)
            for kq in range(4):
                b.dma("sync",
                      f2sl[:, kq * 8 * 256:(kq + 1) * 8 * 256]
                      .rearrange("p (kc m) -> p kc m", m=256),
                      scrT["fc2_w"][kq * 1024:(kq + 1) * 1024,
                                    osl * 256:(osl + 1) * 256]
                      .rearrange("(kc p) m -> p kc m", p=128),
                      writes=("f2sl",))
            osll = slice(osl * 256, (osl + 1) * 256)
            for tt in range(CT):
                bank = b.psum[tt % 4]
                bres = f"pk{tt % 4}"
                b.op("tensor", lambda E, bank=bank, tt=tt: [
                     E.matmul(bank[:, 0:256],
                              hfm[:, kc * C2 + tt * 128:kc * C2 + (tt + 1) * 128],
                              f2sl[:, kc * 256:(kc + 1) * 256],
                              start=(kc == 0), stop=(kc == 31))
                     for kc in range(32)][-1],
                     reads=hreads2 + ("f2sl",), writes=(bres,))
                b.op("vector", lambda E, bank=bank, osll=osll:
                     E.tensor_tensor(out=tmpv[:, 0:256], in0=bank[:, 0:256],
                                     in1=f2br[:, osll], op=ALU.add),
                     reads=(bres, "consts"), writes=(("tmpv",),))
                b.op("vector", lambda E, osll=osll:
                     E.tensor_tensor(out=tmpw[:, 0:256], in0=tmpv[:, 0:256],
                                     in1=g2r[:, osll], op=ALU.mult),
                     reads=(("tmpv",), "consts"), writes=(("tmpw",),))
                b.op("vector", lambda E, tt=tt, osll=osll:
                     E.tensor_tensor(out=s1t[tt][:, osll], in0=tmpw[:, 0:256],
                                     in1=t1t[tt][:, osll], op=ALU.add),
                     reads=(("tmpw",), ("t1", tt)), writes=(("s1", tt),))
        for i in range(CT):
            b.dma("sync", io["s_out"][t0 + i * 128:t0 + (i + 1) * 128, :], s1t[i],
                  reads=(("s1", i),))
    al.reset(mark)



MCH = 256
NMCH = T // MCH

def emit_mamba(b, io):
    al = b.al
    mark = al.mark()
    CHm = MCH
    CT2 = CHm // 128

    ipw = al.tile("ipw", [128, 8 * 4096], BF16)
    xpw = al.tile("xpw", [128, 16 * 96], BF16)
    dtw = al.tile("dtw", [64, DI], BF16)
    opwh = al.tile("opwh", [128, 16 * 512], BF16)
    xn2f = [al.tile("xn2f", [128, 8 * CHm], BF16)]
    A_sb = al.tile("A_sb", [128, 256], F32)
    A_tmp = al.tile("A_tmp", [128, 256], F32)
    cw = al.tile("cw", [128, 64], F32)
    cb = al.tile("cb", [128, 16], F32)
    dtb = al.tile("dtb", [128, 16], F32)
    dsk = al.tile("dsk", [128, 16], F32)
    selm = al.tile("selm", [16, 16 * 128], BF16)
    xi = al.tile("xi", [128, 16 * (CHm + 3)], BF16)
    zs = al.tile("zs", [128, 16 * CHm], BF16)
    xc = al.tile("xc", [128, 16 * CHm], BF16)
    dt = al.tile("dt", [128, 16 * CHm], F32)
    dug = al.tile("dug", [128, CHm], BF16)
    Bsb = al.tile("Bsb", [128, 16 * CHm], BF16)
    Csb = al.tile("Csb", [128, 16 * CHm], BF16)
    yg = al.tile("ygat", [128, 16 * CHm], BF16)
    hst = al.tile("hst", [128, 256], F32)
    hsl = [al.tile("hsl", [128, CHm], BF16) for _ in range(4)]
    asl = [al.tile("asl", [128, CHm], BF16) for _ in range(2)]
    bsl = [al.tile("bsl", [128, CHm], BF16) for _ in range(2)]
    ymb = [al.tile("ymb", [128, CHm], BF16) for _ in range(2)]
    ypp = [al.tile("ypp", [128, CHm], F32) for _ in range(2)]
    spt = [al.tile("spt", [128, CHm], F32) for _ in range(2)]
    dtr = al.tile("dtr", [64, CHm], BF16)
    bcbB = al.tile("bcbB", [16, CHm], BF16)
    bcbC = al.tile("bcbC", [16, CHm], BF16)
    xn0t = [al.tile("m_xn0", [128, D], BF16) for _ in range(2)]
    s0t = [al.tile("m_s0", [128, D], F32) for _ in range(2)]

    for kc in range(8):
        b.dma("gpsimd", ipw[:, kc * 4096:(kc + 1) * 4096],
              io["w_inproj"][kc * 128:(kc + 1) * 128, :], writes=("ipw",))
    for kc in range(16):
        b.dma("gpsimd", xpw[:, kc * 96:(kc + 1) * 96],
              io["w_xproj"][kc * 128:(kc + 1) * 128, :], writes=("xpw",))
    b.dma("gpsimd", dtw, io["w_dtproj"], writes=("dtw",))
    b.dma("gpsimd", selm, io["selmat"][0:16, 0:16 * 128], writes=("selm",))
    for dst, nm in ((A_sb, "alog_t"), (cw, "cw_t"), (cb, "cb_t"),
                    (dtb, "dtb_t"), (dsk, "dsk_t")):
        b.dma("sync", dst, io[nm], writes=("mconsts",))
    b.op("scalar", lambda E: E.activation(out=A_tmp, in_=A_sb, func=AF.Exp),
         reads=("mconsts",), writes=("A1",))
    b.op("vector", lambda E: E.tensor_scalar(out=A_sb, in0=A_tmp, scalar1=-1.0,
                                             scalar2=None, op0=ALU.mult),
         reads=("A1", "mconsts"), writes=("A2",))
    b.op("scalar", lambda E: E.copy(out=A_tmp, in_=A_sb),
         reads=("A2",), writes=("A_sb",))
    b.op("vector", lambda E: E.memset(hst[:, :], 0.0), writes=("hstz",))
    for g in range(16):
        b.op("vector", lambda E, g=g: E.memset(xi[:, g * (CHm + 3):
                                               g * (CHm + 3) + 3], 0.0),
             writes=(("xh", g),))

    for c in range(NMCH):
        t0 = c * CHm
        xs = 0
        for kc in range(8):
            b.dma("sync", xn2f[xs][:, kc * CHm:(kc + 1) * CHm],
                  io["xn2_in"][t0:t0 + CHm, kc * 128:(kc + 1) * 128],
                  transpose=True, writes=((f"xn2f{xs}",),))
        for mt in range(32):
            bank = b.psum[mt % 4]
            bres = f"pm{mt % 4}"
            b.op("tensor", lambda E, bank=bank, mt=mt, xs=xs: [
                 E.matmul(bank[:, 0:CHm],
                          ipw[:, kc * 4096 + mt * 128:kc * 4096 + (mt + 1) * 128],
                          xn2f[xs][:, kc * CHm:(kc + 1) * CHm],
                          start=(kc == 0), stop=(kc == 7))
                 for kc in range(8)][-1],
                 reads=("ipw", (f"xn2f{xs}",)), writes=(bres,))
            if mt < 16:
                g = mt
                b.op("vector", lambda E, bank=bank, g=g:
                     E.tensor_copy(out=xi[:, g * (CHm + 3) + 3:
                                          (g + 1) * (CHm + 3)],
                                   in_=bank[:, 0:CHm]),
                     reads=(bres,), writes=((f"xi", g),))
            else:
                g = mt - 16
                b.op("scalar", lambda E, bank=bank, g=g:
                     E.activation(out=zs[:, g * CHm:(g + 1) * CHm],
                                  in_=bank[:, 0:CHm], func=AF.Silu),
                     reads=(bres,), writes=((f"zs", g),))
        for g in range(16):
            x0 = g * (CHm + 3)
            for k in range(2):
                b.op("scalar", lambda E, g=g, k=k, x0=x0:
                     E.activation(out=spt[k], in_=xi[:, x0 + k:x0 + k + CHm],
                                  func=AF.Copy, scale=cw[:, g * 4 + k:g * 4 + k + 1]),
                     reads=((f"xi", g), ("xh", g), "mconsts"),
                     writes=((f"tap{k}",),))
            b.op("vector", lambda E:
                 E.tensor_tensor(out=ypp[0], in0=spt[0], in1=spt[1], op=ALU.add),
                 reads=(("tap0",), ("tap1",)), writes=(("cv01",),))
            for k in range(2, 4):
                b.op("scalar", lambda E, g=g, k=k, x0=x0:
                     E.activation(out=spt[k - 2], in_=xi[:, x0 + k:x0 + k + CHm],
                                  func=AF.Copy, scale=cw[:, g * 4 + k:g * 4 + k + 1]),
                     reads=((f"xi", g), ("xh", g), "mconsts"),
                     writes=((f"tap{k}",),))
            b.op("vector", lambda E:
                 E.tensor_tensor(out=ypp[1], in0=spt[0], in1=spt[1], op=ALU.add),
                 reads=(("tap2",), ("tap3",)), writes=(("cv23",),))
            b.op("vector", lambda E:
                 E.tensor_tensor(out=spt[0], in0=ypp[0], in1=ypp[1], op=ALU.add),
                 reads=(("cv01",), ("cv23",)), writes=(("cvs",),))
            b.op("scalar", lambda E, g=g:
                 E.activation(out=xc[:, g * CHm:(g + 1) * CHm], in_=spt[0],
                              func=AF.Silu, bias=cb[:, g:g + 1]),
                 reads=(("cvs",), "mconsts"), writes=((f"xc", g),))
            b.op("vector", lambda E, g=g, x0=x0:
                 E.tensor_copy(out=xi[:, x0:x0 + 3],
                               in_=xi[:, x0 + CHm:x0 + CHm + 3]),
                 reads=((f"xi", g),), writes=(("xh", g),))
        xc_reads = tuple((f"xc", g) for g in range(16))
        for gi, (c0, c1, M) in enumerate(((0, 64, 64), (64, 80, 16),
                                          (80, 96, 16))):
            bank = b.psum[4 + gi]
            bres = f"pm{4 + gi}"
            b.op("tensor", lambda E, bank=bank, c0=c0, c1=c1, M=M: [
                 E.matmul(bank[0:M, 0:CHm], xpw[:, kc * 96 + c0:kc * 96 + c1],
                          xc[:, kc * CHm:(kc + 1) * CHm],
                          start=(kc == 0), stop=(kc == 15))
                 for kc in range(16)][-1],
                 reads=("xpw",) + xc_reads, writes=(bres,))
        b.op("vector", lambda E: E.tensor_copy(out=dtr, in_=b.psum[4][0:64, 0:CHm]),
             reads=("pm4",), writes=("dtr",))
        b.op("vector", lambda E: E.tensor_copy(out=bcbB[0:16, :],
                                               in_=b.psum[5][0:16, 0:CHm]),
             reads=("pm5",), writes=("bcb",))
        b.op("vector", lambda E: E.tensor_copy(out=bcbC[0:16, :],
                                               in_=b.psum[6][0:16, 0:CHm]),
             reads=("pm6",), writes=("bcb2",))
        for g in range(16):
            bank = b.psum[4 + (g % 2)]
            bres = f"pm{4 + (g % 2)}"
            b.op("tensor", lambda E, bank=bank, g=g:
                 E.matmul(bank[:, 0:CHm], dtw[0:64, g * 128:(g + 1) * 128], dtr,
                          start=True, stop=True),
                 reads=("dtw", "dtr"), writes=(bres,))
            b.op("scalar", lambda E, bank=bank, g=g:
                 E.activation(out=spt[0], in_=bank[:, 0:CHm], func=AF.Identity,
                              bias=dtb[:, g:g + 1]),
                 reads=(bres, "mconsts"), writes=(("sp0",),))
            b.op("scalar", lambda E:
                 E.activation(out=spt[1], in_=spt[0], func=AF.Abs),
                 reads=(("sp0",),), writes=(("sp1",),))
            b.op("scalar", lambda E:
                 E.activation(out=ypp[0], in_=spt[1], func=AF.Exp, scale=-1.0),
                 reads=(("sp1",),), writes=(("sp2",),))
            b.op("scalar", lambda E:
                 E.activation(out=spt[1], in_=ypp[0], func=AF.Ln, bias=1.0),
                 reads=(("sp2",),), writes=(("sp1b",),))
            b.op("scalar", lambda E:
                 E.activation(out=ypp[1], in_=spt[0], func=AF.Relu),
                 reads=(("sp0",),), writes=(("sp3",),))
            b.op("vector", lambda E, g=g:
                 E.tensor_tensor(out=dt[:, g * CHm:(g + 1) * CHm], in0=ypp[1],
                                 in1=spt[1], op=ALU.add),
                 reads=(("sp3",), ("sp1b",)), writes=((f"dt", g),))
        for s in range(NS):
            bank = b.psum[6 + (s % 2)]
            bres = f"pm{6 + (s % 2)}"
            b.op("tensor", lambda E, bank=bank, s=s:
                 E.matmul(bank[:, 0:CHm], selm[0:16, s * 128:(s + 1) * 128],
                          bcbB[0:16, 0:CHm], start=True, stop=True),
                 reads=("selm", "bcb"), writes=(bres,))
            b.op("vector", lambda E, bank=bank, s=s:
                 E.tensor_copy(out=Bsb[:, s * CHm:(s + 1) * CHm],
                               in_=bank[:, 0:CHm]),
                 reads=(bres,), writes=((f"Bs", s),))
            bank2 = b.psum[6 + ((s + 1) % 2)]
            bres2 = f"pm{6 + ((s + 1) % 2)}"
            b.op("tensor", lambda E, bank=bank2, s=s:
                 E.matmul(bank[:, 0:CHm], selm[0:16, s * 128:(s + 1) * 128],
                          bcbC[0:16, 0:CHm], start=True, stop=True),
                 reads=("selm", "bcb2"), writes=(bres2,))
            b.op("vector", lambda E, bank=bank2, s=s:
                 E.tensor_copy(out=Csb[:, s * CHm:(s + 1) * CHm],
                               in_=bank[:, 0:CHm]),
                 reads=(bres2,), writes=((f"Cs", s),))
        for g in range(16):
            b.op("vector", lambda E, g=g:
                 E.tensor_tensor(out=dug, in0=dt[:, g * CHm:(g + 1) * CHm],
                                 in1=xc[:, g * CHm:(g + 1) * CHm], op=ALU.mult),
                 reads=((f"dt", g), (f"xc", g)), writes=(("dug",),))
            b.op("vector", lambda E: E.memset(ypp[0][:, :], 0.0),
                 writes=(("yp", 0),))
            def sched(s, g=g):
                if s < NS:
                    b.op("gpsimd", lambda E, s=s:
                         E.tensor_tensor(out=bsl[s % 2], in0=dug,
                                         in1=Bsb[:, s * CHm:(s + 1) * CHm],
                                         op=ALU.mult),
                         reads=(("dug",), (f"Bs", s)), writes=((f"bsl", s % 2),))
                    b.op("scalar", lambda E, s=s, g=g:
                         E.activation(out=asl[s % 2],
                                      in_=dt[:, g * CHm:(g + 1) * CHm],
                                      func=AF.Exp,
                                      scale=A_sb[:, g * NS + s:g * NS + s + 1]),
                         reads=((f"dt", g), "A2"), writes=((f"asl", s % 2),))
                    b.op("vector", lambda E, s=s, g=g:
                         E.tensor_tensor_scan(out=hsl[s % 4], data0=asl[s % 2],
                                              data1=bsl[s % 2],
                                              initial=hst[:, g * NS + s:
                                                          g * NS + s + 1],
                                              op0=ALU.mult, op1=ALU.add),
                         reads=((f"asl", s % 2), (f"bsl", s % 2),
                                ("hst", g, s)),
                         writes=((f"hsl", s % 4),))
                sp = s - 2
                if 0 <= sp < NS:
                    b.op("gpsimd", lambda E, sp=sp:
                         E.tensor_tensor(out=ymb[sp % 2], in0=hsl[sp % 4],
                                         in1=Csb[:, sp * CHm:(sp + 1) * CHm],
                                         op=ALU.mult),
                         reads=((f"hsl", sp % 4), (f"Cs", sp)),
                         writes=((f"ymb{sp % 2}",),))
                    b.op("vector", lambda E, sp=sp:
                         E.tensor_tensor(out=ypp[(sp + 1) % 2], in0=ypp[sp % 2],
                                         in1=ymb[sp % 2], op=ALU.add),
                         reads=((f"ymb{sp % 2}",), ("yp", sp % 2)),
                         writes=(("yp", (sp + 1) % 2),))
                    b.op("vector", lambda E, sp=sp, g=g:
                         E.tensor_copy(out=hst[:, g * NS + sp:g * NS + sp + 1],
                                       in_=hsl[sp % 4][:, CHm - 1:CHm]),
                         reads=((f"hsl", sp % 4),), writes=(("hst", g, sp),))
            for s in range(NS + 2):
                sched(s)
            b.op("vector", lambda E, g=g:
                 E.tensor_scalar(out=spt[0], in0=xc[:, g * CHm:(g + 1) * CHm],
                                 scalar1=dsk[:, g:g + 1], scalar2=None,
                                 op0=ALU.mult),
                 reads=((f"xc", g), "mconsts"), writes=(("sp0",),))
            b.op("vector", lambda E, g=g:
                 E.tensor_tensor(out=spt[1], in0=ypp[0], in1=spt[0], op=ALU.add),
                 reads=(("yp", 0), ("sp0",)), writes=(("sp1",),))
            b.op("vector", lambda E, g=g:
                 E.tensor_tensor(out=yg[:, g * CHm:(g + 1) * CHm], in0=spt[1],
                                 in1=zs[:, g * CHm:(g + 1) * CHm], op=ALU.mult),
                 reads=(("sp1",), (f"zs", g)), writes=((f"yg", g),))
        yg_reads = tuple((f"yg", g) for g in range(16))
        for nc_ in range(2):
            for kc2 in range(16):
                b.dma("gpsimd", opwh[:, kc2 * 512:(kc2 + 1) * 512],
                      io["w_outproj"][kc2 * 128:(kc2 + 1) * 128,
                                      nc_ * 512:(nc_ + 1) * 512],
                      writes=("opwh",))
            for tt in range(CT2):
                if nc_ == 0:
                    b.dma("sync", xn0t[tt],
                          io["xn0_in"][t0 + tt * 128:t0 + (tt + 1) * 128, :],
                          writes=((f"xn0t", tt),))
                bank = b.psum[nc_ * 2 + tt]
                bres = f"pm{nc_ * 2 + tt}"
                b.op("tensor", lambda E, bank=bank, tt=tt: [
                     E.matmul(bank[:, 0:512],
                              yg[:, kc2 * CHm + tt * 128:
                                 kc2 * CHm + (tt + 1) * 128],
                              opwh[:, kc2 * 512:(kc2 + 1) * 512],
                              start=(kc2 == 0), stop=(kc2 == 15))
                     for kc2 in range(16)][-1],
                     reads=yg_reads + ("opwh",), writes=(bres,))
                b.op("vector", lambda E, bank=bank, tt=tt, nc_=nc_:
                     E.tensor_tensor(out=s0t[tt][:, nc_ * 512:(nc_ + 1) * 512],
                                     in0=bank[:, 0:512],
                                     in1=xn0t[tt][:, nc_ * 512:(nc_ + 1) * 512],
                                     op=ALU.add),
                     reads=(bres, (f"xn0t", tt)), writes=((f"s0t", tt, nc_),))
        for tt in range(CT2):
            b.dma("sync", io["s_out"][t0 + tt * 128:t0 + (tt + 1) * 128, :],
                  s0t[tt], reads=((f"s0t", tt, 0), (f"s0t", tt, 1)))
    al.reset(mark)



def _finalize(b):
    for cn in ("q_sync", "q_gp", "q_scl"):
        if cn in b.sems:
            b._wait("sync", cn, b.sems[cn].n)
    b.emit()


def _build_ln0():
    b = B()
    io = {
        "x_in": b.din("x_in", [T, D], F32),
        "xn_out": b.dout("xn0", [T, D], BF16),
        "xn2_out": b.dout("xn2", [T, D], BF16),
        "ln_w_rep": b.din("ln_w_rep", [128, D], F32),
        "ln_b_rep": b.din("ln_b_rep", [128, D], F32),
    }
    emit_ln_phase(b, io, first=True, make_xn2=True, final=False)
    _finalize(b)
    return b


def _build_upd(layer):
    b = B()
    io = {
        "x_in": b.din("x_in", [T, D], F32),
        "s_in": b.din("s_in", [T, D], F32),
        "acc_in": b.din("acc_in", [T, D], F32),
        "x_out": b.dout("x_out", [T, D], F32),
    }
    if layer == 0:
        io["xn_out"] = b.dout("xn1", [T, D], BF16)
        emit_ln_phase(b, io, first=False, make_xn2=False, final=False)
    else:
        emit_ln_phase(b, io, first=False, make_xn2=False, final=True)
    _finalize(b)
    return b


def _build_mamba():
    b = B()
    io = {
        "xn2_in": b.din("xn2_in", [T, D], BF16),
        "xn0_in": b.din("xn0_in", [T, D], BF16),
        "s_out": b.dout("s_out", [T, D], F32),
        "w_inproj": b.din("w_inproj", [D, 2 * DI], F32),
        "w_xproj": b.din("w_xproj", [DI, 96], F32),
        "w_dtproj": b.din("w_dtproj", [DTR, DI], F32),
        "w_outproj": b.din("w_outproj", [DI, D], F32),
        "alog_t": b.din("alog_t", [128, 256], F32),
        "cw_t": b.din("cw_t", [128, 64], F32),
        "cb_t": b.din("cb_t", [128, 16], F32),
        "dtb_t": b.din("dtb_t", [128, 16], F32),
        "dsk_t": b.din("dsk_t", [128, 16], F32),
        "selmat": b.din("selmat", [32, 32 * 128], BF16),
    }
    emit_mamba(b, io)
    _finalize(b)
    return b


def _build_ffn(layer):
    b = B()
    io = {
        "xn_in": b.din("xn_in", [T, D], BF16),
        "acc_out": b.dout("acc_out", [T, D], F32),
    }
    scr = {}
    pm = {}
    E_ = 16 if layer == 0 else 64
    io["gate_b_rep"] = b.din("gate_b_rep", [128, E_], F32)
    if layer == 0:
        w13 = b.din("w13", [16, 1024, 1024], F32)
        w2 = b.din("w2", [16, 1024, 512], F32)
        w13s = b.din("w13s", [1024, 1024], F32)
        w2s = b.din("w2s", [1024, 512], F32)
        gw = b.din("gw", [16, 1024], F32)
        scr["w13"] = b.dint("scr_w13", [16, 1024, 1024], BF16)
        scr["w2"] = b.dint("scr_w2", [16, 1024, 512], BF16)
        scr["w13_s"] = b.dint("scr_w13s", [1024, 1024], BF16)
        scr["w2_s"] = b.dint("scr_w2s", [1024, 512], BF16)
        scr["gate_w"] = b.dint("scr_gw", [16, 1024], BF16)
        io["scrT"] = {"w13": b.dint("scrT_w13", [16, 1024, 1024], BF16),
                      "w2": b.dint("scrT_w2", [16, 512, 1024], BF16),
                      "w13_s": b.dint("scrT_w13s", [1024, 1024], BF16),
                      "w2_s": b.dint("scrT_w2s", [512, 1024], BF16)}
        pm["gate_w"] = b.dma("gpsimd", scr["gate_w"], gw)
        pm["w13_s"] = b.dma("gpsimd", scr["w13_s"], w13s)
        pm["w2_s"] = b.dma("gpsimd", scr["w2_s"], w2s)
        pm["w13"] = [b.dma("gpsimd", scr["w13"][e], w13[e]) for e in range(16)]
        pm["w2"] = [b.dma("gpsimd", scr["w2"][e], w2[e]) for e in range(16)]
    else:
        eg = b.din("eg", [64, 512, 1024], F32)
        eu = b.din("eu", [64, 512, 1024], F32)
        ed = b.din("ed", [64, 1024, 512], F32)
        sg = b.din("sg", [1024, 1024], F32)
        su = b.din("su", [1024, 1024], F32)
        sd = b.din("sd", [1024, 1024], F32)
        gw = b.din("gw", [64, 1024], F32)
        scr["eg"] = b.dint("scr_eg", [64, 512, 1024], BF16)
        scr["eu"] = b.dint("scr_eu", [64, 512, 1024], BF16)
        scr["ed"] = b.dint("scr_ed", [64, 1024, 512], BF16)
        scr["sg"] = b.dint("scr_sg", [1024, 1024], BF16)
        scr["su"] = b.dint("scr_su", [1024, 1024], BF16)
        scr["sd"] = b.dint("scr_sd", [1024, 1024], BF16)
        scr["gate_w"] = b.dint("scr_gw64", [64, 1024], BF16)
        io["scrT"] = {"eg": b.dint("scrT_eg", [64, 1024, 512], BF16),
                      "eu": b.dint("scrT_eu", [64, 1024, 512], BF16),
                      "ed": b.dint("scrT_ed", [64, 512, 1024], BF16),
                      "sg": b.dint("scrT_sg", [1024, 1024], BF16),
                      "su": b.dint("scrT_su", [1024, 1024], BF16),
                      "sd": b.dint("scrT_sd", [1024, 1024], BF16)}
        pm["gate_w"] = b.dma("gpsimd", scr["gate_w"], gw)
        pm["sg"] = b.dma("gpsimd", scr["sg"], sg)
        pm["su"] = b.dma("gpsimd", scr["su"], su)
        pm["sd"] = b.dma("gpsimd", scr["sd"], sd)
        pm["eg"] = [b.dma("gpsimd", scr["eg"][e], eg[e]) for e in range(64)]
        pm["eu"] = [b.dma("gpsimd", scr["eu"][e], eu[e]) for e in range(64)]
        pm["ed"] = [b.dma("gpsimd", scr["ed"][e], ed[e]) for e in range(64)]
    io["scr"] = scr
    io["prep_marks"] = pm
    emit_ffn(b, io, layer=layer)
    _finalize(b)
    return b


def _build_attn1():
    b = B()
    io = {
        "xn_in": b.din("xn_in", [T, D], BF16),
        "s_out": b.dout("s_out", [T, D], F32),
        "g1_rep": b.din("g1_rep", [128, D], F32),
        "g2_rep": b.din("g2_rep", [128, D], F32),
        "l1g_rep": b.din("l1g_rep", [128, D], F32),
        "l1b_rep": b.din("l1b_rep", [128, D], F32),
        "l2g_rep": b.din("l2g_rep", [128, D], F32),
        "l2b_rep": b.din("l2b_rep", [128, D], F32),
        "lnh_g_t": b.din("lnh_g_t", [128, 32], F32),
        "lnh_b_t": b.din("lnh_b_t", [128, 32], F32),
        "fc1_b_t": b.din("fc1_b_t", [128, 32], F32),
        "pb_rep": b.din("pb_rep", [128, D], F32),
        "f2b_rep": b.din("f2b_rep", [128, D], F32),
        "ones_row": b.din("ones_row", [1, 128], BF16),
        "ones_col": b.din("ones_col", [128, 1], BF16),
        "y1tm": b.dint("y1tm", [T, D], BF16),
        "ztm": b.dint("ztm", [T, D], BF16),
    }
    pw = b.din("proj_w", [1024, 1024], F32)
    f1w = b.din("fc1_w", [4096, 1024], F32)
    f2w = b.din("fc2_w", [1024, 4096], F32)
    scr = {
        "proj_w": b.dint("scr_proj", [1024, 1024], BF16),
        "fc1_w": b.dint("scr_fc1", [4096, 1024], BF16),
        "fc2_w": b.dint("scr_fc2", [1024, 4096], BF16),
    }
    io["scrT"] = {"proj_w": b.dint("scrT_proj", [1024, 1024], BF16),
                  "fc1_w": b.dint("scrT_fc1", [1024, 4096], BF16),
                  "fc2_w": b.dint("scrT_fc2", [4096, 1024], BF16)}
    pm = {
        "proj_w": b.dma("gpsimd", scr["proj_w"], pw),
        "fc1_w": b.dma("gpsimd", scr["fc1_w"], f1w),
        "fc2_w": b.dma("gpsimd", scr["fc2_w"], f2w),
    }
    io["scr"] = scr
    io["prep_marks"] = pm
    emit_attn1(b, io)
    _finalize(b)
    return b



def _rep(v):
    v = np.ascontiguousarray(np.asarray(v, np.float32))
    return np.ascontiguousarray(np.broadcast_to(v[None, :], (128, v.shape[0])))


def _t16(v, k):
    return np.ascontiguousarray(np.asarray(v, np.float32).reshape(k, 128).T)


def _run(b, in_maps):
    res = run_bass_kernel_spmd(b.nc, in_maps, core_ids=list(range(len(in_maps))))
    return res.results


def _kernel_multi(inp):
    NCORES = 8
    x = np.ascontiguousarray(np.asarray(inp["L0_attn_x"], np.float32))

    lnw_rep = _rep(inp["L0_attn_ln_w"])
    lnb_rep = _rep(inp["L0_attn_ln_b"])

    alog_t = np.ascontiguousarray(
        np.asarray(inp["L0_attn_A_log"], np.float32)
        .reshape(16, 128, 16).transpose(1, 0, 2).reshape(128, 256))
    cw_t = np.ascontiguousarray(
        np.asarray(inp["L0_attn_conv_w"], np.float32)
        .reshape(16, 128, 4).transpose(1, 0, 2).reshape(128, 64))
    cb_t = _t16(inp["L0_attn_conv_b"], 16)
    dtb_t = _t16(inp["L0_attn_dt_proj_b"], 16)
    dsk_t = _t16(inp["L0_attn_D_skip"], 16)
    selmat = np.zeros((32, 32 * 128), dtype=BF)
    for j in range(32):
        selmat[j, j * 128:(j + 1) * 128] = 1

    b0 = _build_ln0()
    r0 = _run(b0, [{"x_in": x[i], "ln_w_rep": lnw_rep, "ln_b_rep": lnb_rep}
                   for i in range(NCORES)])
    xn0 = [r["xn0"] for r in r0]
    xn2 = [r["xn2"] for r in r0]

    b1 = _build_mamba()
    r1 = _run(b1, [{
        "xn2_in": xn2[i], "xn0_in": xn0[i],
        "w_inproj": inp["L0_attn_in_proj_w"],
        "w_xproj": inp["L0_attn_x_proj_w"],
        "w_dtproj": inp["L0_attn_dt_proj_w"],
        "w_outproj": inp["L0_attn_out_proj_w"],
        "alog_t": alog_t, "cw_t": cw_t, "cb_t": cb_t, "dtb_t": dtb_t,
        "dsk_t": dsk_t, "selmat": selmat,
    } for i in range(NCORES)])
    s0 = [r["s_out"] for r in r1]

    b2 = _build_ffn(0)
    r2 = _run(b2, [{
        "xn_in": xn0[i],
        "gate_b_rep": _rep(inp["L0_ffn_gate_b"]),
        "w13": inp["L0_ffn_w13"], "w2": inp["L0_ffn_w2"],
        "w13s": inp["L0_ffn_w13_s"], "w2s": inp["L0_ffn_w2_s"],
        "gw": inp["L0_ffn_gate_w"],
    } for i in range(NCORES)])
    acc0 = [r["acc_out"] for r in r2]

    b3 = _build_upd(0)
    r3 = _run(b3, [{"x_in": x[i], "s_in": s0[i], "acc_in": acc0[i]}
                   for i in range(NCORES)])
    x1 = [r["x_out"] for r in r3]
    xn1 = [r["xn1"] for r in r3]

    b4 = _build_attn1()
    r4 = _run(b4, [{
        "xn_in": xn1[i],
        "g1_rep": _rep(inp["L1_attn_gamma1"]), "g2_rep": _rep(inp["L1_attn_gamma2"]),
        "l1g_rep": _rep(inp["L1_attn_ln1_g"]), "l1b_rep": _rep(inp["L1_attn_ln1_b"]),
        "l2g_rep": _rep(inp["L1_attn_ln2_g"]), "l2b_rep": _rep(inp["L1_attn_ln2_b"]),
        "lnh_g_t": _t16(inp["L1_attn_lnh_g"], 32),
        "lnh_b_t": _t16(inp["L1_attn_lnh_b"], 32),
        "fc1_b_t": _t16(inp["L1_attn_fc1_b"], 32),
        "pb_rep": _rep(inp["L1_attn_proj_b"]),
        "f2b_rep": _rep(inp["L1_attn_fc2_b"]),
        "ones_row": np.ones((1, 128), dtype=BF),
        "ones_col": np.ones((128, 1), dtype=BF),
        "proj_w": inp["L1_attn_proj_w"],
        "fc1_w": inp["L1_attn_fc1_w"],
        "fc2_w": inp["L1_attn_fc2_w"],
    } for i in range(NCORES)])
    s1 = [r["s_out"] for r in r4]

    b5 = _build_ffn(1)
    r5 = _run(b5, [{
        "xn_in": xn1[i],
        "gate_b_rep": _rep(inp["L1_ffn_gate_bias"]),
        "eg": inp["L1_ffn_eg_w"], "eu": inp["L1_ffn_eu_w"],
        "ed": inp["L1_ffn_ed_w"],
        "sg": inp["L1_ffn_sg_w"], "su": inp["L1_ffn_su_w"],
        "sd": inp["L1_ffn_sd_w"],
        "gw": inp["L1_ffn_gate_w"],
    } for i in range(NCORES)])
    acc1 = [r["acc_out"] for r in r5]

    b6 = _build_upd(1)
    r6 = _run(b6, [{"x_in": x1[i], "s_in": s1[i], "acc_in": acc1[i]}
                   for i in range(NCORES)])
    out = np.stack([np.asarray(r["x_out"], np.float32) for r in r6], axis=0)
    return out



def _build_fused():
    b = B()
    al = b.al
    dummy = al.tile("bar_dummy", [128, 16], F32)

    x_in = b.din("x_in", [T, D], F32)
    out = b.dout("out", [T, D], F32)
    xn0 = b.dint("xn0", [T, D], BF16)
    xn2 = b.dint("xn2", [T, D], BF16)
    s0 = b.dint("s0", [T, D], F32)
    acc0 = b.dint("acc0", [T, D], F32)
    x1 = b.dint("x1", [T, D], F32)
    xn1 = b.dint("xn1", [T, D], BF16)
    s1 = b.dint("s1", [T, D], F32)
    acc1 = b.dint("acc1", [T, D], F32)

    io = {"x_in": x_in, "xn_out": xn0, "xn2_out": xn2,
          "ln_w_rep": b.din("ln_w_rep", [128, D], F32),
          "ln_b_rep": b.din("ln_b_rep", [128, D], F32)}
    emit_ln_phase(b, io, first=True, make_xn2=True, final=False)
    b.barrier(dummy)

    io = {"xn2_in": xn2, "xn0_in": xn0, "s_out": s0,
          "w_inproj": b.din("w_inproj", [D, 2 * DI], F32),
          "w_xproj": b.din("w_xproj", [DI, 96], F32),
          "w_dtproj": b.din("w_dtproj", [DTR, DI], F32),
          "w_outproj": b.din("w_outproj", [DI, D], F32),
          "alog_t": b.din("alog_t", [128, 256], F32),
          "cw_t": b.din("cw_t", [128, 64], F32),
          "cb_t": b.din("cb_t", [128, 16], F32),
          "dtb_t": b.din("dtb_t", [128, 16], F32),
          "dsk_t": b.din("dsk_t", [128, 16], F32),
          "selmat": b.din("selmat", [32, 32 * 128], BF16)}
    emit_mamba(b, io)
    b.barrier(dummy)

    io = {"xn_in": xn0, "acc_out": acc0,
          "gate_b_rep": b.din("gate_b_rep0", [128, 16], F32)}
    scr = {"w13": b.dint("scr_w13", [16, 1024, 1024], BF16),
           "w2": b.dint("scr_w2", [16, 1024, 512], BF16),
           "w13_s": b.dint("scr_w13s", [1024, 1024], BF16),
           "w2_s": b.dint("scr_w2s", [1024, 512], BF16),
           "gate_w": b.dint("scr_gw0", [16, 1024], BF16)}
    io["scrT"] = {"w13": b.dint("scrT_w13", [16, 1024, 1024], BF16),
                  "w2": b.dint("scrT_w2", [16, 512, 1024], BF16),
                  "w13_s": b.dint("scrT_w13s", [1024, 1024], BF16),
                  "w2_s": b.dint("scrT_w2s", [512, 1024], BF16)}
    w13 = b.din("w13", [16, 1024, 1024], F32)
    w2 = b.din("w2", [16, 1024, 512], F32)
    pm = {"gate_w": b.dma("gpsimd", scr["gate_w"], b.din("gw0", [16, 1024], F32)),
          "w13_s": b.dma("gpsimd", scr["w13_s"], b.din("w13s", [1024, 1024], F32)),
          "w2_s": b.dma("gpsimd", scr["w2_s"], b.din("w2s", [1024, 512], F32)),
          "w13": [b.dma("gpsimd", scr["w13"][e], w13[e]) for e in range(16)],
          "w2": [b.dma("gpsimd", scr["w2"][e], w2[e]) for e in range(16)]}
    io["scr"] = scr
    io["prep_marks"] = pm
    emit_ffn(b, io, layer=0)
    b.barrier(dummy)

    io = {"x_in": x_in, "s_in": s0, "acc_in": acc0, "x_out": x1,
          "xn_out": xn1}
    emit_ln_phase(b, io, first=False, make_xn2=False, final=False)
    b.barrier(dummy)

    io = {"xn_in": xn1, "s_out": s1,
          "g1_rep": b.din("g1_rep", [128, D], F32),
          "g2_rep": b.din("g2_rep", [128, D], F32),
          "l1g_rep": b.din("l1g_rep", [128, D], F32),
          "l1b_rep": b.din("l1b_rep", [128, D], F32),
          "l2g_rep": b.din("l2g_rep", [128, D], F32),
          "l2b_rep": b.din("l2b_rep", [128, D], F32),
          "lnh_g_t": b.din("lnh_g_t", [128, 32], F32),
          "lnh_b_t": b.din("lnh_b_t", [128, 32], F32),
          "fc1_b_t": b.din("fc1_b_t", [128, 32], F32),
          "pb_rep": b.din("pb_rep", [128, D], F32),
          "f2b_rep": b.din("f2b_rep", [128, D], F32),
          "ones_row": b.din("ones_row", [1, 128], BF16),
          "ones_col": b.din("ones_col", [128, 1], BF16),
          "y1tm": b.dint("y1tm", [T, D], BF16),
          "ztm": b.dint("ztm", [T, D], BF16)}
    scr = {"proj_w": b.dint("scr_proj", [1024, 1024], BF16),
           "fc1_w": b.dint("scr_fc1", [4096, 1024], BF16),
           "fc2_w": b.dint("scr_fc2", [1024, 4096], BF16)}
    io["scrT"] = {"proj_w": b.dint("scrT_proj", [1024, 1024], BF16),
                  "fc1_w": b.dint("scrT_fc1", [1024, 4096], BF16),
                  "fc2_w": b.dint("scrT_fc2", [4096, 1024], BF16)}
    pm = {"proj_w": b.dma("gpsimd", scr["proj_w"], b.din("proj_w", [1024, 1024], F32)),
          "fc1_w": b.dma("gpsimd", scr["fc1_w"], b.din("fc1_w", [4096, 1024], F32)),
          "fc2_w": b.dma("gpsimd", scr["fc2_w"], b.din("fc2_w", [1024, 4096], F32))}
    io["scr"] = scr
    io["prep_marks"] = pm
    emit_attn1(b, io)
    b.barrier(dummy)

    io = {"xn_in": xn1, "acc_out": acc1,
          "gate_b_rep": b.din("gate_b_rep1", [128, 64], F32)}
    scr = {"eg": b.dint("scr_eg", [64, 512, 1024], BF16),
           "eu": b.dint("scr_eu", [64, 512, 1024], BF16),
           "ed": b.dint("scr_ed", [64, 1024, 512], BF16),
           "sg": b.dint("scr_sg", [1024, 1024], BF16),
           "su": b.dint("scr_su", [1024, 1024], BF16),
           "sd": b.dint("scr_sd", [1024, 1024], BF16),
           "gate_w": b.dint("scr_gw1", [64, 1024], BF16)}
    io["scrT"] = {"eg": b.dint("scrT_eg", [64, 1024, 512], BF16),
                  "eu": b.dint("scrT_eu", [64, 1024, 512], BF16),
                  "ed": b.dint("scrT_ed", [64, 512, 1024], BF16),
                  "sg": b.dint("scrT_sg", [1024, 1024], BF16),
                  "su": b.dint("scrT_su", [1024, 1024], BF16),
                  "sd": b.dint("scrT_sd", [1024, 1024], BF16)}
    eg = b.din("eg", [64, 512, 1024], F32)
    eu = b.din("eu", [64, 512, 1024], F32)
    ed = b.din("ed", [64, 1024, 512], F32)
    pm = {"gate_w": b.dma("gpsimd", scr["gate_w"], b.din("gw1", [64, 1024], F32)),
          "sg": b.dma("gpsimd", scr["sg"], b.din("sg", [1024, 1024], F32)),
          "su": b.dma("gpsimd", scr["su"], b.din("su", [1024, 1024], F32)),
          "sd": b.dma("gpsimd", scr["sd"], b.din("sd", [1024, 1024], F32)),
          "eg": [b.dma("gpsimd", scr["eg"][e], eg[e]) for e in range(64)],
          "eu": [b.dma("gpsimd", scr["eu"][e], eu[e]) for e in range(64)],
          "ed": [b.dma("gpsimd", scr["ed"][e], ed[e]) for e in range(64)]}
    io["scr"] = scr
    io["prep_marks"] = pm
    emit_ffn(b, io, layer=1)
    b.barrier(dummy)

    io = {"x_in": x1, "s_in": s1, "acc_in": acc1, "x_out": out}
    emit_ln_phase(b, io, first=False, make_xn2=False, final=True)
    _finalize(b)
    return b


def _kernel_fused(inp):
    x = np.ascontiguousarray(np.asarray(inp["L0_attn_x"], np.float32))
    alog_t = np.ascontiguousarray(
        np.asarray(inp["L0_attn_A_log"], np.float32)
        .reshape(16, 128, 16).transpose(1, 0, 2).reshape(128, 256))
    cw_t = np.ascontiguousarray(
        np.asarray(inp["L0_attn_conv_w"], np.float32)
        .reshape(16, 128, 4).transpose(1, 0, 2).reshape(128, 64))
    selmat = np.zeros((32, 32 * 128), dtype=BF)
    for j in range(32):
        selmat[j, j * 128:(j + 1) * 128] = 1
    common = {
        "ln_w_rep": _rep(inp["L0_attn_ln_w"]), "ln_b_rep": _rep(inp["L0_attn_ln_b"]),
        "w_inproj": inp["L0_attn_in_proj_w"], "w_xproj": inp["L0_attn_x_proj_w"],
        "w_dtproj": inp["L0_attn_dt_proj_w"], "w_outproj": inp["L0_attn_out_proj_w"],
        "alog_t": alog_t, "cw_t": cw_t, "cb_t": _t16(inp["L0_attn_conv_b"], 16),
        "dtb_t": _t16(inp["L0_attn_dt_proj_b"], 16),
        "dsk_t": _t16(inp["L0_attn_D_skip"], 16), "selmat": selmat,
        "gate_b_rep0": _rep(inp["L0_ffn_gate_b"]),
        "w13": inp["L0_ffn_w13"], "w2": inp["L0_ffn_w2"],
        "w13s": inp["L0_ffn_w13_s"], "w2s": inp["L0_ffn_w2_s"],
        "gw0": inp["L0_ffn_gate_w"],
        "g1_rep": _rep(inp["L1_attn_gamma1"]), "g2_rep": _rep(inp["L1_attn_gamma2"]),
        "l1g_rep": _rep(inp["L1_attn_ln1_g"]), "l1b_rep": _rep(inp["L1_attn_ln1_b"]),
        "l2g_rep": _rep(inp["L1_attn_ln2_g"]), "l2b_rep": _rep(inp["L1_attn_ln2_b"]),
        "lnh_g_t": _t16(inp["L1_attn_lnh_g"], 32),
        "lnh_b_t": _t16(inp["L1_attn_lnh_b"], 32),
        "fc1_b_t": _t16(inp["L1_attn_fc1_b"], 32),
        "pb_rep": _rep(inp["L1_attn_proj_b"]), "f2b_rep": _rep(inp["L1_attn_fc2_b"]),
        "ones_row": np.ones((1, 128), dtype=BF),
        "ones_col": np.ones((128, 1), dtype=BF),
        "proj_w": inp["L1_attn_proj_w"], "fc1_w": inp["L1_attn_fc1_w"],
        "fc2_w": inp["L1_attn_fc2_w"],
        "gate_b_rep1": _rep(inp["L1_ffn_gate_bias"]),
        "eg": inp["L1_ffn_eg_w"], "eu": inp["L1_ffn_eu_w"], "ed": inp["L1_ffn_ed_w"],
        "sg": inp["L1_ffn_sg_w"], "su": inp["L1_ffn_su_w"], "sd": inp["L1_ffn_sd_w"],
        "gw1": inp["L1_ffn_gate_w"],
    }
    b = _build_fused()
    res = _run(b, [dict(common, x_in=x[i]) for i in range(8)])
    return np.stack([np.asarray(r["out"], np.float32) for r in res], axis=0)


KERNEL_MODE = "fused"


def kernel(**inp):
    if KERNEL_MODE == "fused":
        return _kernel_fused(inp)
    return _kernel_multi(inp)


# version 04
# speedup vs baseline: 1.1615x; 1.1382x over previous
import numpy as np
import ml_dtypes
import concourse.bass as bass
import concourse.mybir as mybir
from concourse.bass_utils import run_bass_kernel_spmd
from contextlib import ExitStack

F32 = mybir.dt.float32
BF16 = mybir.dt.bfloat16
AF = mybir.ActivationFunctionType
ALU = mybir.AluOpType
AX = mybir.AxisListType
BF = ml_dtypes.bfloat16

T = 4096
D = 1024
DI = 2048
NS = 16
DTR = 64
NG = DI // 128
CH = 512
NCHK = T // CH
EPS = 1e-5
NT = T // 128
ENGS = ("sync", "tensor", "vector", "scalar", "gpsimd")
COMPUTE_CTR = {"sync": "q_sync", "tensor": "c_te", "vector": "c_ve",
               "scalar": "c_se", "gpsimd": "q_gp"}


class Ctr:
    def __init__(self, handle):
        self.h = handle
        self.n = 0


ARENA_BYTES = 196608


class Alloc:
    def __init__(self, nc, limit=None):
        arena = nc.alloc_sbuf_tensor("manual_arena", [128, ARENA_BYTES],
                                     mybir.dt.uint8)
        base = nc.lookup_mloc(arena).addr
        self.nc, self.cur, self.uid = nc, base, 0
        self.limit = base + ARENA_BYTES

    def tile(self, name, shape, dtype, align=64):
        self.cur = (self.cur + align - 1) // align * align
        nb = int(np.prod(shape[1:])) * mybir.dt.size(dtype)
        t = self.nc.alloc_sbuf_tensor_at(f"{name}_{self.uid}", list(shape),
                                         dtype, offset=self.cur)
        self.uid += 1
        self.cur += nb
        assert self.cur <= self.limit, f"SBUF overflow at {name}: {self.cur}"
        return t.ap()

    def mark(self):
        return self.cur

    def reset(self, pos):
        self.cur = pos


class B:
    def __init__(self):
        self.nc = bass.Bass("TRN2", target_bir_lowering=False)
        self.ctx = ExitStack()
        self.al = Alloc(self.nc)
        self.sems = {}
        self.psum = [self.nc.alloc_psum_tensor(f"pb{i}", [128, 512], F32).ap()
                     for i in range(8)]
        self.q = {e: [] for e in ENGS}
        self.lastw = {}
        self.readers = {}
        self.watermark = {e: {} for e in ENGS}

    def ctr(self, name):
        if name not in self.sems:
            self.sems[name] = Ctr(self.ctx.enter_context(self.nc.semaphore(name)))
        return self.sems[name]

    def din(self, name, shape, dtype):
        return self.nc.dram_tensor(name, list(shape), dtype,
                                   kind="ExternalInput").ap()

    def dout(self, name, shape, dtype):
        return self.nc.dram_tensor(name, list(shape), dtype,
                                   kind="ExternalOutput").ap()

    def dint(self, name, shape, dtype):
        return self.nc.dram_tensor(name, list(shape), dtype,
                                   kind="Internal").ap()

    def raw(self, eng, fn):
        self.q[eng].append(fn)

    def _wait(self, eng, ctrname, val):
        if val <= 0:
            return
        wm = self.watermark[eng]
        if wm.get(ctrname, 0) >= val:
            return
        wm[ctrname] = val
        h = self.ctr(ctrname).h
        self.q[eng].append(lambda E, h=h, v=val: E.wait_ge(h, v))

    def op(self, eng, fn, reads=(), writes=(), dma=False):
        need = {}
        def _need(c, v):
            if need.get(c, 0) < v:
                need[c] = v
        for r in reads:
            lw = self.lastw.get(r)
            if lw:
                _need(lw[0], lw[1])
        for w in writes:
            lw = self.lastw.get(w)
            if lw:
                _need(lw[0], lw[1])
            for rc, rv in self.readers.get(w, ()):
                _need(rc, rv)
        for c, v in need.items():
            self._wait(eng, c, v)
        cname = COMPUTE_CTR[eng] if not dma else {
            "sync": "q_sync", "gpsimd": "q_gp", "scalar": "q_scl"}[eng]
        c = self.ctr(cname)
        k = 16 if dma else 1
        c.n += k
        self.q[eng].append(lambda E, fn=fn, h=c.h, k=k: fn(E).then_inc(h, k))
        mile = (cname, c.n)
        if not dma and eng in ("vector", "tensor"):
            self.watermark[eng][cname] = c.n
        for r in reads:
            lst = self.readers.setdefault(r, [])
            lst[:] = [m for m in lst if m[0] != cname]
            lst.append(mile)
        for w in writes:
            self.lastw[w] = mile
            self.readers[w] = []
        return mile

    def dma(self, eng, out, in_, reads=(), writes=(), **kw):
        return self.op(eng, lambda E, o=out, i=in_, kw=kw:
                       E.dma_start(out=o, in_=i, **kw),
                       reads=reads, writes=writes, dma=True)

    def wait_mile(self, eng, mile):
        self._wait(eng, mile[0], mile[1])

    def barrier(self, dummy):
        bar = self.ctr("bar")
        ms = []
        self.q["tensor"].append(lambda E, h=bar.h: E.nop().then_inc(h, 1))
        self.q["vector"].append(lambda E, h=bar.h, d=dummy:
                                E.memset(d[:, 0:2], 0.0).then_inc(h, 1))
        self.q["scalar"].append(lambda E, h=bar.h, d=dummy:
                                E.copy(out=d[:, 2:4], in_=d[:, 0:2]).then_inc(h, 1))
        self.q["gpsimd"].append(lambda E, h=bar.h, d=dummy:
                                E.memset(d[:, 4:6], 0.0).then_inc(h, 1))
        bar.n += 4
        tgt = bar.n
        dma_targets = [(n, self.sems[n].n) for n in ("q_sync", "q_scl")
                       if n in self.sems]
        for e in ENGS:
            self.q[e].append(lambda E, h=bar.h, v=tgt: E.wait_ge(h, v))
            self.watermark[e]["bar"] = tgt
            for n, v in dma_targets:
                self._wait(e, n, v)
        self.lastw = {k: v for k, v in self.lastw.items()
                      if isinstance(k, tuple) and len(k) > 0 and k[0] == "scr"}
        self.readers = {}

    def emit(self):
        blk = self.ctx.enter_context(self.nc.Block())
        for e in ENGS:
            fns = self.q[e]
            if fns:
                getattr(blk, e)(lambda E, fns=fns: [f(E) for f in fns])



def emit_ln_phase(b, io, *, first, make_xn2, final):
    al = b.al
    mark = al.mark()
    G = 4
    xb = [al.tile("ln_x", [128, D], F32) for _ in range(G)]
    if not first:
        sb = [al.tile("ln_s", [128, D], F32) for _ in range(G)]
        ab = [al.tile("ln_a", [128, D], F32) for _ in range(G)]
        gb = [al.tile("ln_g", [128, D], F32) for _ in range(G)]
        x1 = [al.tile("ln_x1", [128, D], F32) for _ in range(G)]
    sq = [al.tile("ln_sq", [128, D], F32) for _ in range(G)]
    xn = [al.tile("ln_xn", [128, D], BF16) for _ in range(G)]
    co = [al.tile("ln_co", [128, 16], F32) for _ in range(G)]
    if make_xn2:
        xn2 = [al.tile("ln_xn2", [128, D], BF16) for _ in range(G)]
        t2 = [al.tile("ln_t2", [128, D], F32) for _ in range(G)]
        co2 = [al.tile("ln_co2", [128, 16], F32) for _ in range(G)]
        lnw = al.tile("lnw_rep", [128, D], F32)
        lnb = al.tile("lnb_rep", [128, D], F32)
        b.dma("sync", lnw, io["ln_w_rep"], writes=("lnw",))
        b.dma("sync", lnb, io["ln_b_rep"], writes=("lnb",))

    def ln_chain(tiles, src_of, src_key, out_of, out_key, cob, sqb, affine, pfx):
        K = lambda nm, i: (pfx + nm, i % G)
        for i in tiles:
            c = cob[i % G]
            b.op("vector", lambda E, s=src_of(i), c=c:
                 E.tensor_reduce(out=c[:, 0:1], in_=s, axis=AX.X, op=ALU.add),
                 reads=(src_key(i),), writes=(K("c0", i),))
        for i in tiles:
            c = cob[i % G]
            b.op("scalar", lambda E, c=c:
                 E.activation(out=c[:, 1:2], in_=c[:, 0:1], func=AF.Copy,
                              scale=1.0 / D),
                 reads=(K("c0", i),), writes=(K("mu", i),))
            b.op("scalar", lambda E, s=src_of(i), q=sqb[i % G]:
                 E.activation(out=q, in_=s, func=AF.Square),
                 reads=(src_key(i),), writes=(K("sq", i),))
        for i in tiles:
            c = cob[i % G]
            b.op("vector", lambda E, q=sqb[i % G], c=c:
                 E.tensor_reduce(out=c[:, 2:3], in_=q, axis=AX.X, op=ALU.add),
                 reads=(K("sq", i),), writes=(K("c2", i),))
        for i in tiles:
            c = cob[i % G]
            b.op("scalar", lambda E, c=c:
                 E.activation(out=c[:, 3:4], in_=c[:, 2:3], func=AF.Copy,
                              scale=1.0 / D),
                 reads=(K("c2", i),), writes=(K("e2", i),))
            b.op("scalar", lambda E, c=c:
                 E.activation(out=c[:, 4:5], in_=c[:, 1:2], func=AF.Square),
                 reads=(K("mu", i),), writes=(K("m2", i),))
        for i in tiles:
            c = cob[i % G]
            b.op("vector", lambda E, c=c:
                 E.tensor_tensor(out=c[:, 5:6], in0=c[:, 3:4], in1=c[:, 4:5],
                                 op=ALU.subtract),
                 reads=(K("e2", i), K("m2", i)), writes=(K("var", i),))
        for i in tiles:
            c = cob[i % G]
            b.op("scalar", lambda E, c=c:
                 E.activation(out=c[:, 7:8], in_=c[:, 5:6], func=AF.Copy,
                              bias=EPS),
                 reads=(K("var", i),), writes=(K("veb", i),))
        for i in tiles:
            c = cob[i % G]
            b.op("vector", lambda E, c=c:
                 E.reciprocal(out=c[:, 8:9], in_=c[:, 7:8]),
                 reads=(K("veb", i),), writes=(K("rcp", i),))
        for i in tiles:
            c = cob[i % G]
            b.op("scalar", lambda E, c=c:
                 E.activation(out=c[:, 9:10], in_=c[:, 8:9], func=AF.Sqrt),
                 reads=(K("rcp", i),), writes=(K("rstd", i),))
        for i in tiles:
            c = cob[i % G]
            if affine is None:
                b.op("vector", lambda E, s=src_of(i), o=out_of(i), c=c:
                     E.tensor_scalar(out=o, in0=s, scalar1=c[:, 1:2],
                                     scalar2=c[:, 9:10], op0=ALU.subtract,
                                     op1=ALU.mult),
                     reads=(K("mu", i), K("rstd", i), src_key(i)),
                     writes=(out_key(i),))
            else:
                g_r, b_r = affine
                b.op("vector", lambda E, s=src_of(i), c=c, t=t2[i % G]:
                     E.tensor_scalar(out=t, in0=s, scalar1=c[:, 1:2],
                                     scalar2=c[:, 9:10], op0=ALU.subtract,
                                     op1=ALU.mult),
                     reads=(K("mu", i), K("rstd", i), src_key(i)),
                     writes=(K("t2", i),))
                b.op("vector", lambda E, t=t2[i % G], q=sqb[i % G]:
                     E.tensor_tensor(out=q, in0=t, in1=g_r, op=ALU.mult),
                     reads=(K("t2", i), "lnw"), writes=(K("sq", i),))
                b.op("vector", lambda E, q=sqb[i % G], o=out_of(i):
                     E.tensor_tensor(out=o, in0=q, in1=b_r, op=ALU.add),
                     reads=(K("sq", i), "lnb"), writes=(out_key(i),))

    for g0 in range(0, NT, G):
        tiles = list(range(g0, min(g0 + G, NT)))
        for i in tiles:
            s = i % G
            b.dma("sync", xb[s], io["x_in"][i * 128:(i + 1) * 128, :],
                  writes=(("xb", s),))
            if not first:
                b.dma("sync", sb[s], io["s_in"][i * 128:(i + 1) * 128, :],
                      writes=(("sb", s),))
                b.dma("sync", ab[s], io["acc_in"][i * 128:(i + 1) * 128, :],
                      writes=(("ab", s),))
        if not first:
            for i in tiles:
                s = i % G
                b.op("scalar", lambda E, s=s:
                     E.activation(out=gb[s], in_=ab[s], func=AF.Silu),
                     reads=(("ab", s),), writes=(("gb", s),))
            for i in tiles:
                s = i % G
                b.op("vector", lambda E, s=s:
                     E.tensor_tensor(out=sq[s], in0=sb[s], in1=gb[s], op=ALU.mult),
                     reads=(("sb", s), ("gb", s)), writes=(("Lsq", s),))
                b.op("vector", lambda E, s=s:
                     E.tensor_tensor(out=x1[s], in0=xb[s], in1=sq[s], op=ALU.add),
                     reads=(("Lsq", s), ("xb", s)), writes=(("x1", s),))
            for i in tiles:
                s = i % G
                b.dma("sync", io["x_out"][i * 128:(i + 1) * 128, :], x1[s],
                      reads=(("x1", s),))
            src_of = lambda i: x1[i % G]
            src_key = lambda i: ("x1", i % G)
        else:
            src_of = lambda i: xb[i % G]
            src_key = lambda i: ("xb", i % G)
        if not final:
            ln_chain(tiles, src_of, src_key, lambda i: xn[i % G],
                     lambda i: ("xn", i % G), co, sq, None, "A")
            for i in tiles:
                b.dma("sync", io["xn_out"][i * 128:(i + 1) * 128, :], xn[i % G],
                      reads=(("xn", i % G),))
            if make_xn2:
                ln_chain(tiles, lambda i: xn[i % G], lambda i: ("xn", i % G),
                         lambda i: xn2[i % G], lambda i: ("xn2", i % G),
                         co2, sq, (lnw, lnb), "B")
                for i in tiles:
                    b.dma("sync", io["xn2_out"][i * 128:(i + 1) * 128, :],
                          xn2[i % G], reads=(("xn2", i % G),))
    al.reset(mark)



def emit_ffn(b, io, *, layer):
    al = b.al
    mark = al.mark()
    E_ = 16 if layer == 0 else 64
    QT = 1024
    NQ = T // QT
    QTT = QT // 128
    NCH2 = QT // 512

    xnfm = al.tile("xnfm", [128, 8 * QT], BF16)
    acc = [al.tile("acc", [128, QTT * D], F32) for _ in range(2)]
    hsb = al.tile("hsb", [128, 8 * QT], BF16)
    sgb = al.tile("sgb", [128, 512], BF16)
    uub = al.tile("uub", [128, 512], BF16)
    tmpd = al.tile("tmpd", [128, 512], F32)
    comb = al.tile("comb", [128, QTT * E_], F32)
    combb = al.tile("combb", [128, QTT * E_], F32)
    rsc = al.tile("rsc", [128, 5 * E_ + 96], F32)
    gbr = al.tile("gbr", [128, E_], F32)
    gwt = al.tile("gwt", [128, 8 * E_], BF16)
    dummy = al.tile("ffdum", [128, 16], F32)
    if layer == 0:
        wg = [al.tile("w13t", [128, 8 * 1024], BF16) for _ in range(2)]
        wd = [al.tile("w2t", [128, 4 * 1024], BF16) for _ in range(2)]
    else:
        weg = [al.tile("egt", [128, 8 * 512], BF16) for _ in range(2)]
        weu = [al.tile("eut", [128, 8 * 512], BF16) for _ in range(2)]
        wed = [al.tile("edt", [128, 4 * 1024], BF16) for _ in range(2)]

    scrT = io["scrT"]

    b.dma("sync", gbr, io["gate_b_rep"], writes=("gbr",))
    for kc in range(8):
        b.dma("sync", gwt[:, kc * E_:(kc + 1) * E_],
              scrT["gate_w"][kc * 128:(kc + 1) * 128, :],
              reads=(("scr", f"gw{layer}"),), writes=("gwt",))

    def load_shared():
        if layer == 0:
            for kc in range(8):
                b.dma("sync", wg[0][:, kc * 1024:(kc + 1) * 1024],
                      scrT["w13_s"][kc * 128:(kc + 1) * 128, :],
                      reads=(("scr", "w13_s"),), writes=("wslot0g",))
            for kc in range(4):
                b.dma("scalar", wd[0][:, kc * 1024:(kc + 1) * 1024],
                      scrT["w2_s"][kc * 128:(kc + 1) * 128, :],
                      reads=(("scr", "w2_s"),), writes=("wslot0d",))
        else:
            pass

    def load_shared1(half, slot):
        i0 = half * 512
        for kc in range(8):
            b.dma("sync", weg[slot][:, kc * 512:(kc + 1) * 512],
                  scrT["sg"][kc * 128:(kc + 1) * 128, i0:i0 + 512],
                  reads=(("scr", "sg"),), writes=(f"wslot{slot}g",))
        for kc in range(8):
            b.dma("scalar", weu[slot][:, kc * 512:(kc + 1) * 512],
                  scrT["su"][kc * 128:(kc + 1) * 128, i0:i0 + 512],
                  reads=(("scr", "su"),), writes=(f"wslot{slot}u",))
        for kc in range(4):
            b.dma("scalar", wed[slot][:, kc * 1024:(kc + 1) * 1024],
                  scrT["sd"][i0 + kc * 128:i0 + (kc + 1) * 128, :],
                  reads=(("scr", "sd"),), writes=(f"wslot{slot}d",))

    def load_expert_slot(e, slot):
        if layer == 0:
            for kc in range(8):
                b.dma("sync", wg[slot][:, kc * 1024:(kc + 1) * 1024],
                      scrT["w13"][e, kc * 128:(kc + 1) * 128, :],
                      reads=(("scr", "w13", e),), writes=(f"wslot{slot}g",))
            for kc in range(4):
                b.dma("scalar", wd[slot][:, kc * 1024:(kc + 1) * 1024],
                      scrT["w2"][e, kc * 128:(kc + 1) * 128, :],
                      reads=(("scr", "w2", e),), writes=(f"wslot{slot}d",))
        else:
            for kc in range(8):
                b.dma("sync", weg[slot][:, kc * 512:(kc + 1) * 512],
                      scrT["eg"][e, kc * 128:(kc + 1) * 128, :],
                      reads=(("scr", "eg", e),), writes=(f"wslot{slot}g",))
            for kc in range(8):
                b.dma("scalar", weu[slot][:, kc * 512:(kc + 1) * 512],
                      scrT["eu"][e, kc * 128:(kc + 1) * 128, :],
                      reads=(("scr", "eu", e),), writes=(f"wslot{slot}u",))
            for kc in range(4):
                b.dma("scalar", wed[slot][:, kc * 1024:(kc + 1) * 1024],
                      scrT["ed"][e, kc * 128:(kc + 1) * 128, :],
                      reads=(("scr", "ed", e),), writes=(f"wslot{slot}d",))

    def routing(q):
        for tt in range(QTT):
            bank = b.psum[tt % 2]
            bres = f"pbank{tt % 2}"
            mm = b.op("tensor", lambda E, bank=bank, tt=tt: [
                E.matmul(bank[:, 0:E_],
                         xnfm[:, kc * QT + tt * 128: kc * QT + (tt + 1) * 128],
                         gwt[:, kc * E_:(kc + 1) * E_],
                         start=(kc == 0), stop=(kc == 7))
                for kc in range(8)][-1],
                reads=("xnfm", "gwt"), writes=(bres,))
            sc_t = rsc[:, 0:E_]
            b.op("scalar", lambda E, bank=bank, o=sc_t:
                 E.activation(out=o, in_=bank[:, 0:E_], func=AF.Sigmoid),
                 reads=(bres,), writes=(("sc",),))
            bi = rsc[:, E_:2 * E_]
            b.op("vector", lambda E, o=bi, a=sc_t:
                 E.tensor_tensor(out=o, in0=a, in1=gbr, op=ALU.add),
                 reads=(("sc",), "gbr", bres), writes=(("bi",),))
            bi2 = rsc[:, 2 * E_:3 * E_]
            b.op("scalar", lambda E, o=bi2, a=bi: E.copy(out=o, in_=a),
                 reads=(("bi",),), writes=(("bi2",),))
            x8 = rsc[:, 4 * E_:4 * E_ + 8]
            if layer == 0:
                b.op("vector", lambda E, a=bi2, o=x8: E.max(out=o, in_=a),
                     reads=(("bi2",),), writes=(("x8",),))
                th = rsc[:, 4 * E_ + 8:4 * E_ + 9]
                b.op("scalar", lambda E, o=th, a=x8: E.copy(out=o, in_=a[:, 3:4]),
                     reads=(("x8",),), writes=(("th",),))
                sel = rsc[:, 3 * E_:4 * E_]
                b.op("vector", lambda E, o=sel, a=bi2, t=th:
                     E.tensor_scalar(out=o, in0=a, scalar1=t, scalar2=None,
                                     op0=ALU.is_ge),
                     reads=(("bi2",), ("th",)), writes=(("sel",),))
                selb = rsc[:, 4 * E_ + 80:5 * E_ + 80]
                b.op("scalar", lambda E, o=selb, a=sel: E.copy(out=o, in_=a),
                     reads=(("sel",),), writes=(("selb",),))
                w_ = rsc[:, 4 * E_ + 16:5 * E_ + 16]
                b.op("vector", lambda E, o=w_, a=sc_t, s=selb:
                     E.tensor_tensor(out=o, in0=a, in1=s, op=ALU.mult),
                     reads=(("selb",), ("sc",)), writes=(("w",),))
                w2_ = rsc[:, 5 * E_ + 16:6 * E_ + 16]
                b.op("scalar", lambda E, o=w2_, a=w_: E.copy(out=o, in_=a),
                     reads=(("w",),), writes=(("w2",),))
                sm = rsc[:, 4 * E_ + 9:4 * E_ + 10]
                b.op("vector", lambda E, o=sm, a=w2_:
                     E.tensor_reduce(out=o, in_=a, axis=AX.X, op=ALU.add),
                     reads=(("w2",),), writes=(("sm",),))
                sm2 = rsc[:, 4 * E_ + 10:4 * E_ + 11]
                b.op("scalar", lambda E, o=sm2, a=sm: E.copy(out=o, in_=a),
                     reads=(("sm",),), writes=(("sm2",),))
                rc = rsc[:, 4 * E_ + 11:4 * E_ + 12]
                b.op("vector", lambda E, o=rc, a=sm2: E.reciprocal(out=o, in_=a),
                     reads=(("sm2",),), writes=(("rc",),))
                rc2 = rsc[:, 4 * E_ + 12:4 * E_ + 13]
                b.op("scalar", lambda E, o=rc2, a=rc:
                     E.activation(out=o, in_=a, func=AF.Copy, scale=2.5),
                     reads=(("rc",),), writes=(("rc2",),))
                b.op("vector", lambda E, o=comb[:, tt * E_:(tt + 1) * E_], a=w_,
                     r=rc2:
                     E.tensor_scalar(out=o, in0=a, scalar1=r, scalar2=None,
                                     op0=ALU.mult),
                     reads=(("w",), ("rc2",)), writes=(("comb",),))
                if tt == 0 and q == 0 and "dbg_out" in io:
                    b.dma("sync", io["dbg_out"], rsc,
                          reads=(("comb",), ("sc",), ("bi",), ("bi2",),
                                 ("sel",), ("x8",), ("w",), ("w2",)))
            else:
                gm = rsc[:, 4 * E_ + 16:4 * E_ + 24]
                b.op("vector", lambda E, a=bi2, o=gm:
                     E.tensor_reduce(out=o,
                                     in_=a.rearrange("p (g e) -> p g e", e=8),
                                     axis=AX.X, op=ALU.max),
                     reads=(("bi2",),), writes=(("gm",),))
                gm2 = rsc[:, 4 * E_ + 24:4 * E_ + 32]
                b.op("scalar", lambda E, o=gm2, a=gm: E.copy(out=o, in_=a),
                     reads=(("gm",),), writes=(("gm2",),))
                b.op("vector", lambda E, a=gm2, o=x8: E.max(out=o, in_=a),
                     reads=(("gm2",),), writes=(("x8",),))
                thg = rsc[:, 4 * E_ + 8:4 * E_ + 9]
                b.op("scalar", lambda E, o=thg, a=x8: E.copy(out=o, in_=a[:, 3:4]),
                     reads=(("x8",),), writes=(("thg",),))
                gmask = rsc[:, 4 * E_ + 32:4 * E_ + 40]
                b.op("vector", lambda E, o=gmask, a=gm2, t=thg:
                     E.tensor_scalar(out=o, in0=a, scalar1=t, scalar2=None,
                                     op0=ALU.is_ge),
                     reads=(("gm2",), ("thg",)), writes=(("gmk",),))
                gmask2 = rsc[:, 4 * E_ + 40:4 * E_ + 48]
                b.op("scalar", lambda E, o=gmask2, a=gmask: E.copy(out=o, in_=a),
                     reads=(("gmk",),), writes=(("gmk2",),))
                masked = rsc[:, 3 * E_:4 * E_]
                b.op("vector", lambda E, o=masked, a=bi2, s=gmask2: [
                     E.tensor_scalar(out=o[:, g * 8:(g + 1) * 8],
                                     in0=a[:, g * 8:(g + 1) * 8],
                                     scalar1=s[:, g:g + 1], scalar2=None,
                                     op0=ALU.mult) for g in range(8)][-1],
                     reads=(("bi2",), ("gmk2",)), writes=(("msk",),))
                masked2 = comb[:, tt * E_:(tt + 1) * E_]
                b.op("scalar", lambda E, o=masked2, a=masked: E.copy(out=o, in_=a),
                     reads=(("msk",),), writes=(("msk2",),))
                b.op("vector", lambda E, a=masked2, o=x8: E.max(out=o, in_=a),
                     reads=(("msk2",),), writes=(("x8",),))
                th6 = rsc[:, 4 * E_ + 9:4 * E_ + 10]
                b.op("scalar", lambda E, o=th6, a=x8: E.copy(out=o, in_=a[:, 5:6]),
                     reads=(("x8",),), writes=(("th6",),))
                sel = rsc[:, 0:E_]
                b.op("vector", lambda E, o=sel, a=masked2, t=th6:
                     E.tensor_scalar(out=o, in0=a, scalar1=t, scalar2=None,
                                     op0=ALU.is_ge),
                     reads=(("msk2",), ("th6",)), writes=(("sel",),))
                selb = rsc[:, 4 * E_ + 80:5 * E_ + 80]
                b.op("scalar", lambda E, o=selb, a=sel: E.copy(out=o, in_=a),
                     reads=(("sel",),), writes=(("selb",),))
                w_ = rsc[:, E_:2 * E_]
                b.op("vector", lambda E, o=w_, a=masked2, s=selb:
                     E.tensor_tensor(out=o, in0=a, in1=s, op=ALU.mult),
                     reads=(("selb",), ("msk2",)), writes=(("w",),))
                w2_ = rsc[:, 2 * E_:3 * E_]
                b.op("scalar", lambda E, o=w2_, a=w_: E.copy(out=o, in_=a),
                     reads=(("w",),), writes=(("w2",),))
                sm = rsc[:, 4 * E_ + 10:4 * E_ + 11]
                b.op("vector", lambda E, o=sm, a=w2_:
                     E.tensor_reduce(out=o, in_=a, axis=AX.X, op=ALU.add),
                     reads=(("w2",),), writes=(("sm",),))
                sm2 = rsc[:, 4 * E_ + 11:4 * E_ + 12]
                b.op("scalar", lambda E, o=sm2, a=sm:
                     E.activation(out=o, in_=a, func=AF.Copy, bias=1e-8),
                     reads=(("sm",),), writes=(("sm2",),))
                rc = rsc[:, 4 * E_ + 12:4 * E_ + 13]
                b.op("vector", lambda E, o=rc, a=sm2: E.reciprocal(out=o, in_=a),
                     reads=(("sm2",),), writes=(("rc",),))
                rc2 = rsc[:, 4 * E_ + 13:4 * E_ + 14]
                b.op("scalar", lambda E, o=rc2, a=rc: E.copy(out=o, in_=a),
                     reads=(("rc",),), writes=(("rc2",),))
                b.op("vector", lambda E, o=combb[:, tt * E_:(tt + 1) * E_],
                     a=w_, r=rc2:
                     E.tensor_scalar(out=o, in0=a, scalar1=r, scalar2=None,
                                     op0=ALU.mult),
                     reads=(("w",), ("rc2",)), writes=(("combv",),))
        src_, dst_ = (comb, combb) if layer == 0 else (combb, comb)
        b.op("scalar", lambda E, s=src_, d=dst_:
             E.copy(out=d[:, 0:QTT * E_], in_=s[:, 0:QTT * E_]),
             reads=(("comb",) if layer == 0 else ("combv",),),
             writes=("combf",))
        return dst_

    def expert_compute(kind, slot, acc_src, acc_dst, first_contrib, e_idx,
                       combf):
        if kind in ("exp0", "sh0"):
            n_inter = 512
            gm_ = lambda kc, msl: wg[slot][:, kc * 1024 + msl.start:
                                           kc * 1024 + msl.stop]
            um_ = lambda kc, msl: wg[slot][:, kc * 1024 + 512 + msl.start:
                                           kc * 1024 + 512 + msl.stop]
            dm_ = lambda kc2, nc_: wd[slot][:, kc2 * 1024 + nc_ * 512:
                                            kc2 * 1024 + (nc_ + 1) * 512]
            wres_g = wres_u = (f"wslot{slot}g",)
            wres_d = (f"wslot{slot}d",)
        elif kind == "exp1":
            n_inter = 512
            gm_ = lambda kc, msl: weg[slot][:, kc * 512 + msl.start:
                                            kc * 512 + msl.stop]
            um_ = lambda kc, msl: weu[slot][:, kc * 512 + msl.start:
                                            kc * 512 + msl.stop]
            dm_ = lambda kc2, nc_: wed[slot][:, kc2 * 1024 + nc_ * 512:
                                             kc2 * 1024 + (nc_ + 1) * 512]
            wres_g = (f"wslot{slot}g",)
            wres_u = (f"wslot{slot}u",)
            wres_d = (f"wslot{slot}d",)
        NMT = n_inter // 128
        for mt in range(NMT):
            msl = slice(mt * 128, (mt + 1) * 128)
            for nch in range(NCH2):
                gb_ = b.psum[(2 * (mt * NCH2 + nch)) % 4]
                ub_ = b.psum[(2 * (mt * NCH2 + nch) + 1) % 4]
                gbr_ = f"pbank{(2 * (mt * NCH2 + nch)) % 4}"
                ubr_ = f"pbank{(2 * (mt * NCH2 + nch) + 1) % 4}"
                b.op("tensor", lambda E, gb_=gb_, msl=msl, nch=nch, gm_=gm_: [
                     E.matmul(gb_[:, 0:512], gm_(kc, msl),
                              xnfm[:, kc * QT + nch * 512:kc * QT + (nch + 1) * 512],
                              start=(kc == 0), stop=(kc == 7))
                     for kc in range(8)][-1],
                     reads=wres_g + ("xnfm",), writes=(gbr_,))
                b.op("tensor", lambda E, ub_=ub_, msl=msl, nch=nch, um_=um_: [
                     E.matmul(ub_[:, 0:512], um_(kc, msl),
                              xnfm[:, kc * QT + nch * 512:kc * QT + (nch + 1) * 512],
                              start=(kc == 0), stop=(kc == 7))
                     for kc in range(8)][-1],
                     reads=wres_u + ("xnfm",), writes=(ubr_,))
                b.op("scalar", lambda E, gb_=gb_:
                     E.activation(out=sgb, in_=gb_[:, 0:512], func=AF.Silu),
                     reads=(gbr_,), writes=("sgb",))
                b.op("vector", lambda E, ub_=ub_:
                     E.tensor_copy(out=uub, in_=ub_[:, 0:512]),
                     reads=(ubr_,), writes=("uub",))
                hsl = slice(mt * QT + nch * 512, mt * QT + (nch + 1) * 512)
                b.op("vector", lambda E, hsl=hsl:
                     E.tensor_tensor(out=hsb[:, hsl], in0=sgb, in1=uub,
                                     op=ALU.mult),
                     reads=("sgb", "uub"), writes=(("h", mt, nch),))
        NK2 = n_inter // 128
        hreads = tuple(("h", mt, nch) for mt in range(NMT) for nch in range(NCH2))
        for tt in range(QTT):
            for nc_ in range(2):
                bank = b.psum[4 + (tt * 2 + nc_) % 4]
                bres = f"pbank{4 + (tt * 2 + nc_) % 4}"
                b.op("tensor", lambda E, bank=bank, tt=tt, nc_=nc_, NK2=NK2,
                     dm_=dm_: [
                     E.matmul(bank[:, 0:512],
                              hsb[:, kc2 * QT + tt * 128:kc2 * QT + (tt + 1) * 128],
                              dm_(kc2, nc_),
                              start=(kc2 == 0), stop=(kc2 == NK2 - 1))
                     for kc2 in range(NK2)][-1],
                     reads=hreads + wres_d, writes=(bres,))
                dsl = slice(tt * D + nc_ * 512, tt * D + (nc_ + 1) * 512)
                if first_contrib:
                    b.op("vector", lambda E, bank=bank, dsl=dsl, a=acc_dst:
                         E.tensor_copy(out=a[:, dsl], in_=bank[:, 0:512]),
                         reads=(bres,), writes=(("acc", id(acc_dst), tt, nc_),))
                elif e_idx is None:
                    b.op("vector", lambda E, bank=bank, dsl=dsl, s=acc_src,
                         a=acc_dst:
                         E.tensor_tensor(out=a[:, dsl], in0=s[:, dsl],
                                         in1=bank[:, 0:512], op=ALU.add),
                         reads=(bres, ("acc", id(acc_src), tt, nc_)),
                         writes=(("acc", id(acc_dst), tt, nc_),))
                else:
                    ccol = combf[:, tt * E_ + e_idx:tt * E_ + e_idx + 1]
                    b.op("vector", lambda E, bank=bank, c=ccol:
                         E.tensor_scalar(out=tmpd, in0=bank[:, 0:512], scalar1=c,
                                         scalar2=None, op0=ALU.mult),
                         reads=(bres, "combf"), writes=("tmpd",))
                    b.op("vector", lambda E, dsl=dsl, s=acc_src, a=acc_dst:
                         E.tensor_tensor(out=a[:, dsl], in0=s[:, dsl], in1=tmpd,
                                         op=ALU.add),
                         reads=("tmpd", ("acc", id(acc_src), tt, nc_)),
                         writes=(("acc", id(acc_dst), tt, nc_),))

    if layer == 0:
        load_shared()
    for q in range(NQ):
        for kc in range(8):
            b.dma("sync", xnfm[:, kc * QT:(kc + 1) * QT],
                  io["xn_in"][q * QT:(q + 1) * QT, kc * 128:(kc + 1) * 128],
                  transpose=True, writes=("xnfm",))
        combf = routing(q)
        if "comb_out" in io:
            for tt in range(QTT):
                b.dma("sync", io["comb_out"][(q * QTT + tt) * 128:
                                             (q * QTT + tt + 1) * 128, :],
                      combf[:, tt * E_:(tt + 1) * E_], reads=("combf",))
        nexp = io.get("nexp", E_)
        if layer == 0:
            expert_compute("sh0", 0, None, acc[0], True, None, combf)
            nsh = 0
        else:
            load_shared1(0, 0)
            expert_compute("exp1", 0, None, acc[0], True, None, combf)
            load_shared1(1, 1)
            expert_compute("exp1", 1, acc[0], acc[1], False, None, combf)
            nsh = 1
        for e in range(nexp):
            slot = (e + nsh + 1) % 2
            load_expert_slot(e, slot)
            expert_compute("exp0" if layer == 0 else "exp1", slot,
                           acc[(e + nsh) % 2], acc[(e + nsh + 1) % 2], False, e,
                           combf)
        final_acc = acc[(nexp + nsh) % 2]
        for tt in range(QTT):
            b.dma("sync", io["acc_out"][(q * QTT + tt) * 128:
                                        (q * QTT + tt + 1) * 128, :],
                  final_acc[:, tt * D:(tt + 1) * D],
                  reads=(("acc", id(final_acc), tt, 0),
                         ("acc", id(final_acc), tt, 1)))
        if layer == 0:
            if q < NQ - 1:
                load_shared()
    al.reset(mark)



def emit_attn1(b, io):
    al = b.al
    mark = al.mark()
    C2 = 512
    NC2 = T // C2
    CT = C2 // 128

    projT = al.tile("projT", [128, 8 * 1024], BF16)
    f1sl = [al.tile("f1sl", [128, 8 * 128], BF16) for _ in range(2)]
    f2sl = al.tile("f2sl", [128, 32 * 256], BF16)
    g1r = al.tile("g1r", [128, D], BF16)
    g2r = al.tile("g2r", [128, D], BF16)
    pbr = al.tile("pbr", [128, D], BF16)
    f2br = al.tile("f2br", [128, D], BF16)
    l1g = al.tile("l1g", [128, D], BF16)
    l1b = al.tile("l1b", [128, D], BF16)
    l2g = al.tile("l2g", [128, D], BF16)
    l2b = al.tile("l2b", [128, D], BF16)
    lhg = al.tile("lhg", [128, 32], F32)
    lhb = al.tile("lhb", [128, 32], F32)
    f1b = al.tile("f1b", [128, 32], F32)
    ones1 = al.tile("ones1", [1, 128], BF16)
    onecol = al.tile("onecol", [128, 1], BF16)
    xnt = [al.tile("at_xn", [128, D], BF16) for _ in range(4)]
    y1t = [al.tile("at_y1", [128, D], BF16) for _ in range(2)]
    t1t = [al.tile("at_t1", [128, D], F32) for _ in range(4)]
    zt = [al.tile("at_z", [128, D], BF16) for _ in range(2)]
    s1t = [al.tile("at_s1", [128, D], F32) for _ in range(4)]
    co = [al.tile("at_co", [128, 16], F32) for _ in range(4)]
    co2 = [al.tile("at_co2", [128, 16], F32) for _ in range(4)]
    sq = [al.tile("at_sq", [128, D], F32) for _ in range(2)]
    t2a = [al.tile("at_t2", [128, D], F32) for _ in range(2)]
    y1fm = al.tile("y1fm", [128, 8 * C2], BF16)
    zfm = al.tile("zfm", [128, 8 * C2], BF16)
    hfm = al.tile("hfm", [128, 32 * C2], BF16)
    hsqt = al.tile("hsqt", [128, C2], BF16)
    rowb = al.tile("rowb", [1, 3 * C2], F32)
    rowbb = al.tile("rowbb", [1, 2 * C2], BF16)
    d1t = al.tile("at_d1", [128, C2], F32)
    d2t = al.tile("at_d2", [128, C2], F32)
    tmpv = al.tile("at_tmpv", [128, 512], F32)
    tmpw = al.tile("at_tmpw", [128, 512], F32)

    scrT = io["scrT"]

    for dst, nm in ((lhg, "lnh_g_t"), (lhb, "lnh_b_t"), (f1b, "fc1_b_t")):
        b.dma("sync", dst, io[nm], writes=("consts",))
    b.dma("sync", ones1, io["ones_row"], writes=("consts",))
    b.dma("sync", onecol, io["ones_col"], writes=("consts",))
    for dst, nm in ((g1r, "g1_rep"), (g2r, "g2_rep"), (pbr, "pb_rep"),
                    (f2br, "f2b_rep"), (l1g, "l1g_rep"), (l1b, "l1b_rep"),
                    (l2g, "l2g_rep"), (l2b, "l2b_rep")):
        b.dma("gpsimd", dst, io[nm], writes=("consts",))

    for kc in range(8):
        b.dma("sync", projT[:, kc * 1024:(kc + 1) * 1024],
              scrT["proj"][kc * 128:(kc + 1) * 128, :],
              reads=(("scr", "proj"),), writes=("projT",))

    def ln_tile(i, src, src_key, out, out_key, cob, gg, bb, pfx):
        c = cob
        K = lambda nm: (pfx + nm, i)
        sqb, t2b = sq[i % 2], t2a[i % 2]
        b.op("vector", lambda E: E.tensor_reduce(out=c[:, 0:1], in_=src,
                                                 axis=AX.X, op=ALU.add),
             reads=(src_key,), writes=(K("c0"),))
        b.op("scalar", lambda E: E.activation(out=c[:, 1:2], in_=c[:, 0:1],
                                              func=AF.Copy, scale=1.0 / D),
             reads=(K("c0"),), writes=(K("mu"),))
        b.op("scalar", lambda E: E.activation(out=sqb, in_=src, func=AF.Square),
             reads=(src_key,), writes=(("SQb", i % 2),))
        b.op("vector", lambda E: E.tensor_reduce(out=c[:, 2:3], in_=sqb,
                                                 axis=AX.X, op=ALU.add),
             reads=(("SQb", i % 2),), writes=(K("c2"),))
        b.op("scalar", lambda E: E.activation(out=c[:, 3:4], in_=c[:, 2:3],
                                              func=AF.Copy, scale=1.0 / D),
             reads=(K("c2"),), writes=(K("e2"),))
        b.op("scalar", lambda E: E.activation(out=c[:, 4:5], in_=c[:, 1:2],
                                              func=AF.Square),
             reads=(K("mu"),), writes=(K("m2"),))
        b.op("vector", lambda E: E.tensor_tensor(out=c[:, 5:6], in0=c[:, 3:4],
                                                 in1=c[:, 4:5], op=ALU.subtract),
             reads=(K("e2"), K("m2")), writes=(K("var"),))
        b.op("scalar", lambda E: E.activation(out=c[:, 7:8], in_=c[:, 5:6],
                                              func=AF.Copy, bias=EPS),
             reads=(K("var"),), writes=(K("veb"),))
        b.op("vector", lambda E: E.reciprocal(out=c[:, 8:9], in_=c[:, 7:8]),
             reads=(K("veb"),), writes=(K("rcp"),))
        b.op("scalar", lambda E: E.activation(out=c[:, 9:10], in_=c[:, 8:9],
                                              func=AF.Sqrt),
             reads=(K("rcp"),), writes=(K("rstd"),))
        b.op("vector", lambda E: E.tensor_scalar(out=t2b, in0=src,
                                                 scalar1=c[:, 1:2],
                                                 scalar2=c[:, 9:10],
                                                 op0=ALU.subtract, op1=ALU.mult),
             reads=(K("mu"), K("rstd"), src_key), writes=(("T2b", i % 2),))
        b.op("vector", lambda E: E.tensor_tensor(out=sqb, in0=t2b, in1=gg,
                                                 op=ALU.mult),
             reads=(("T2b", i % 2), "consts"), writes=(("SQb", i % 2),))
        b.op("vector", lambda E: E.tensor_tensor(out=out, in0=sqb, in1=bb,
                                                 op=ALU.add),
             reads=(("SQb", i % 2), "consts"), writes=(out_key,))

    for ch in range(NC2):
        t0 = ch * C2
        for i in range(CT):
            b.dma("sync", xnt[i], io["xn_in"][t0 + i * 128:t0 + (i + 1) * 128, :],
                  writes=(("axn", i),))
        for i in range(CT):
            ln_tile(i, xnt[i], ("axn", i), y1t[i % 2], ("ay1", i % 2), co[i],
                    l1g, l1b, "p")
            b.dma("sync", io["y1tm"][t0 + i * 128:t0 + (i + 1) * 128, :],
                  y1t[i % 2], reads=(("ay1", i % 2),), writes=(("y1tm", i),))
        for kc in range(8):
            b.dma("sync", y1fm[:, kc * C2:(kc + 1) * C2],
                  io["y1tm"][t0:t0 + C2, kc * 128:(kc + 1) * 128], transpose=True,
                  reads=tuple(("y1tm", i) for i in range(CT)), writes=("y1fm",))
        for i in range(CT):
            for nc_ in range(2):
                bank = b.psum[(i * 2 + nc_) % 4]
                bres = f"pk{(i * 2 + nc_) % 4}"
                b.op("tensor", lambda E, bank=bank, i=i, nc_=nc_: [
                     E.matmul(bank[:, 0:512],
                              y1fm[:, kc * C2 + i * 128:kc * C2 + (i + 1) * 128],
                              projT[:, kc * 1024 + nc_ * 512:
                                    kc * 1024 + (nc_ + 1) * 512],
                              start=(kc == 0), stop=(kc == 7))
                     for kc in range(8)][-1],
                     reads=("y1fm", "projT"), writes=(bres,))
                dsl = slice(nc_ * 512, (nc_ + 1) * 512)
                b.op("vector", lambda E, bank=bank, dsl=dsl:
                     E.tensor_tensor(out=tmpv, in0=bank[:, 0:512],
                                     in1=pbr[:, dsl], op=ALU.add),
                     reads=(bres, "consts"), writes=(("tmpv",),))
                b.op("vector", lambda E, dsl=dsl:
                     E.tensor_tensor(out=tmpw, in0=tmpv, in1=g1r[:, dsl],
                                     op=ALU.mult),
                     reads=(("tmpv",), "consts"), writes=(("tmpw",),))
                b.op("vector", lambda E, i=i, dsl=dsl:
                     E.tensor_tensor(out=t1t[i][:, dsl], in0=tmpw,
                                     in1=xnt[i][:, dsl], op=ALU.add),
                     reads=(("tmpw",), ("axn", i)), writes=(("t1", i),))
        for i in range(CT):
            ln_tile(i, t1t[i], ("t1", i), zt[i % 2], ("az", i % 2), co2[i],
                    l2g, l2b, "q")
            b.dma("sync", io["ztm"][t0 + i * 128:t0 + (i + 1) * 128, :], zt[i % 2],
                  reads=(("az", i % 2),), writes=(("ztm", i),))
        for kc in range(8):
            b.dma("sync", zfm[:, kc * C2:(kc + 1) * C2],
                  io["ztm"][t0:t0 + C2, kc * 128:(kc + 1) * 128], transpose=True,
                  reads=tuple(("ztm", i) for i in range(CT)), writes=("zfm",))
        for mt in range(32):
            sl = mt % 2
            b.dma("sync" if mt % 2 == 0 else "scalar",
                  f1sl[sl].rearrange("p (kc m) -> p kc m", m=128),
                  scrT["fc1"][:, mt * 128:(mt + 1) * 128]
                  .rearrange("(kc p) m -> p kc m", p=128),
                  reads=(("scr", "fc1"),), writes=((f"f1sl{sl}",),))
            bank = b.psum[mt % 4]
            bres = f"pk{mt % 4}"
            b.op("tensor", lambda E, bank=bank, sl=sl: [
                 E.matmul(bank[:, 0:C2], f1sl[sl][:, kc * 128:(kc + 1) * 128],
                          zfm[:, kc * C2:(kc + 1) * C2],
                          start=(kc == 0), stop=(kc == 7))
                 for kc in range(8)][-1],
                 reads=("zfm", (f"f1sl{sl}",)), writes=(bres,))
            b.op("scalar", lambda E, bank=bank, mt=mt:
                 E.activation(out=hfm[:, mt * C2:(mt + 1) * C2],
                              in_=bank[:, 0:C2], func=AF.Gelu,
                              bias=f1b[:, mt:mt + 1]),
                 reads=(bres, "consts"), writes=(("hf", mt),))
        hreads = tuple(("hf", mt) for mt in range(32))
        b.op("tensor", lambda E: [
             E.matmul(b.psum[4][0:1, 0:C2], onecol[:, 0:1],
                      hfm[:, mt * C2:(mt + 1) * C2],
                      start=(mt == 0), stop=(mt == 31))
             for mt in range(32)][-1],
             reads=hreads + ("consts",), writes=("pk4",))
        for mt in range(32):
            b.op("scalar", lambda E, mt=mt:
                 E.activation(out=hsqt, in_=hfm[:, mt * C2:(mt + 1) * C2],
                              func=AF.Square),
                 reads=(("hf", mt),), writes=("hsqt",))
            b.op("tensor", lambda E, mt=mt:
                 E.matmul(b.psum[5][0:1, 0:C2], onecol[:, 0:1], hsqt,
                          start=(mt == 0), stop=(mt == 31)),
                 reads=("hsqt", "consts"),
                 writes=("pk5",) if mt in (0, 31) else ())
        b.op("scalar", lambda E: E.activation(out=rowb[0:1, 0:C2],
                                              in_=b.psum[4][0:1, 0:C2],
                                              func=AF.Copy, scale=1.0 / 4096),
             reads=("pk4",), writes=("rmu",))
        b.op("vector", lambda E: E.tensor_copy(out=rowbb[0:1, 0:C2],
                                               in_=rowb[0:1, 0:C2]),
             reads=("rmu",), writes=("rbmu",))
        b.op("scalar", lambda E: E.activation(out=rowb[0:1, C2:2 * C2],
                                              in_=b.psum[5][0:1, 0:C2],
                                              func=AF.Copy, scale=1.0 / 4096),
             reads=("pk5",), writes=("re2",))
        b.op("scalar", lambda E: E.activation(out=rowb[0:1, 2 * C2:3 * C2],
                                              in_=rowb[0:1, 0:C2],
                                              func=AF.Square),
             reads=("rmu", "rbmu"), writes=("rm2",))
        b.op("vector", lambda E: E.tensor_tensor(out=rowb[0:1, 0:C2],
                                                 in0=rowb[0:1, C2:2 * C2],
                                                 in1=rowb[0:1, 2 * C2:3 * C2],
                                                 op=ALU.subtract),
             reads=("re2", "rm2", "rbmu"), writes=("rvar",))
        b.op("scalar", lambda E: E.activation(out=rowb[0:1, C2:2 * C2],
                                              in_=rowb[0:1, 0:C2],
                                              func=AF.Copy, bias=EPS),
             reads=("rvar",), writes=("rveb",))
        b.op("vector", lambda E: E.reciprocal(out=rowb[0:1, 0:C2],
                                              in_=rowb[0:1, C2:2 * C2]),
             reads=("rveb",), writes=("rrcp",))
        b.op("scalar", lambda E: E.activation(out=rowb[0:1, 2 * C2:3 * C2],
                                              in_=rowb[0:1, 0:C2],
                                              func=AF.Sqrt),
             reads=("rrcp",), writes=("rrstd",))
        b.op("vector", lambda E: E.tensor_copy(out=rowbb[0:1, C2:2 * C2],
                                               in_=rowb[0:1, 2 * C2:3 * C2]),
             reads=("rrstd",), writes=("rbrstd",))
        b.op("tensor", lambda E: E.matmul(b.psum[6][:, 0:C2], ones1[0:1, 0:128],
                                          rowbb[0:1, 0:C2], start=True,
                                          stop=True),
             reads=("rbmu", "consts"), writes=("pk6",))
        b.op("tensor", lambda E: E.matmul(b.psum[7][:, 0:C2], ones1[0:1, 0:128],
                                          rowbb[0:1, C2:2 * C2], start=True,
                                          stop=True),
             reads=("rbrstd", "consts"), writes=("pk7",))
        for mt in range(32):
            b.op("vector", lambda E, mt=mt:
                 E.tensor_tensor(out=d1t, in0=hfm[:, mt * C2:(mt + 1) * C2],
                                 in1=b.psum[6][:, 0:C2], op=ALU.subtract),
                 reads=(("hf", mt), "pk6"), writes=("d1t",))
            b.op("vector", lambda E:
                 E.tensor_tensor(out=d2t, in0=d1t, in1=b.psum[7][:, 0:C2],
                                 op=ALU.mult),
                 reads=("d1t", "pk7"), writes=("d2t",))
            b.op("scalar", lambda E, mt=mt:
                 E.activation(out=hfm[:, mt * C2:(mt + 1) * C2], in_=d2t,
                              func=AF.Identity, scale=lhg[:, mt:mt + 1],
                              bias=lhb[:, mt:mt + 1]),
                 reads=("d2t", "consts"), writes=(("hf", mt),))
        hreads2 = tuple(("hf", mt) for mt in range(32))
        for osl in range(4):
            b.dma("scalar", f2sl.rearrange("p (kc m) -> p kc m", m=256),
                  scrT["fc2"][:, osl * 256:(osl + 1) * 256]
                  .rearrange("(kc p) m -> p kc m", p=128),
                  reads=(("scr", "fc2"),), writes=("f2sl",))
            osll = slice(osl * 256, (osl + 1) * 256)
            for tt in range(CT):
                bank = b.psum[tt % 4]
                bres = f"pk{tt % 4}"
                b.op("tensor", lambda E, bank=bank, tt=tt: [
                     E.matmul(bank[:, 0:256],
                              hfm[:, kc * C2 + tt * 128:kc * C2 + (tt + 1) * 128],
                              f2sl[:, kc * 256:(kc + 1) * 256],
                              start=(kc == 0), stop=(kc == 31))
                     for kc in range(32)][-1],
                     reads=hreads2 + ("f2sl",), writes=(bres,))
                b.op("vector", lambda E, bank=bank, osll=osll:
                     E.tensor_tensor(out=tmpv[:, 0:256], in0=bank[:, 0:256],
                                     in1=f2br[:, osll], op=ALU.add),
                     reads=(bres, "consts"), writes=(("tmpv",),))
                b.op("vector", lambda E, osll=osll:
                     E.tensor_tensor(out=tmpw[:, 0:256], in0=tmpv[:, 0:256],
                                     in1=g2r[:, osll], op=ALU.mult),
                     reads=(("tmpv",), "consts"), writes=(("tmpw",),))
                b.op("vector", lambda E, tt=tt, osll=osll:
                     E.tensor_tensor(out=s1t[tt][:, osll], in0=tmpw[:, 0:256],
                                     in1=t1t[tt][:, osll], op=ALU.add),
                     reads=(("tmpw",), ("t1", tt)), writes=(("s1", tt),))
        for i in range(CT):
            b.dma("sync", io["s_out"][t0 + i * 128:t0 + (i + 1) * 128, :], s1t[i],
                  reads=(("s1", i),))
    al.reset(mark)



MCH = 256
NMCH = T // MCH

def emit_mamba(b, io):
    al = b.al
    mark = al.mark()
    CHm = MCH
    CT2 = CHm // 128

    ipw = al.tile("ipw", [128, 8 * 4096], BF16)
    xpw = al.tile("xpw", [128, 16 * 96], BF16)
    dtw = al.tile("dtw", [64, DI], BF16)
    opwh = al.tile("opwh", [128, 16 * 512], BF16)
    xn2f = [al.tile("xn2f", [128, 8 * CHm], BF16)]
    A_sb = al.tile("A_sb", [128, 256], F32)
    A_tmp = al.tile("A_tmp", [128, 256], F32)
    cw = al.tile("cw", [128, 64], F32)
    cb = al.tile("cb", [128, 16], F32)
    dtb = al.tile("dtb", [128, 16], F32)
    dsk = al.tile("dsk", [128, 16], F32)
    selm = al.tile("selm", [16, 16 * 128], BF16)
    xi = al.tile("xi", [128, 16 * (CHm + 3)], BF16)
    zs = al.tile("zs", [128, 16 * CHm], BF16)
    xc = al.tile("xc", [128, 16 * CHm], BF16)
    dt = al.tile("dt", [128, 16 * CHm], F32)
    dug = al.tile("dug", [128, CHm], BF16)
    Bsb = al.tile("Bsb", [128, 16 * CHm], BF16)
    Csb = al.tile("Csb", [128, 16 * CHm], BF16)
    yg = al.tile("ygat", [128, 16 * CHm], BF16)
    hst = al.tile("hst", [128, 256], F32)
    hsl = [al.tile("hsl", [128, CHm], BF16) for _ in range(4)]
    asl = [al.tile("asl", [128, CHm], BF16) for _ in range(2)]
    bsl = [al.tile("bsl", [128, CHm], BF16) for _ in range(2)]
    ymb = al.tile("ymb", [128, CHm], BF16)
    ypp = [al.tile("ypp", [128, CHm], F32) for _ in range(2)]
    spt = [al.tile("spt", [128, CHm], F32) for _ in range(2)]
    dtr = al.tile("dtr", [64, CHm], BF16)
    bcbB = al.tile("bcbB", [16, CHm], BF16)
    bcbC = al.tile("bcbC", [16, CHm], BF16)
    xn0t = [al.tile("m_xn0", [128, D], BF16) for _ in range(2)]
    s0t = [al.tile("m_s0", [128, D], F32) for _ in range(2)]

    scrm = io["scr"]
    for kc in range(8):
        b.dma("sync", ipw[:, kc * 4096:(kc + 1) * 4096],
              scrm["ipw"][kc * 128:(kc + 1) * 128, :],
              reads=(("scr", "ipw"),), writes=("ipw",))
    for kc in range(16):
        b.dma("sync", xpw[:, kc * 96:(kc + 1) * 96],
              scrm["xpw"][kc * 128:(kc + 1) * 128, :],
              reads=(("scr", "xpw"),), writes=("xpw",))
    b.dma("sync", dtw, scrm["dtw"], reads=(("scr", "dtw"),), writes=("dtw",))
    b.dma("sync", selm, io["selmat"][0:16, 0:16 * 128], writes=("selm",))
    for dst, nm in ((A_sb, "alog_t"), (cw, "cw_t"), (cb, "cb_t"),
                    (dtb, "dtb_t"), (dsk, "dsk_t")):
        b.dma("sync", dst, io[nm], writes=("mconsts",))
    b.op("scalar", lambda E: E.activation(out=A_tmp, in_=A_sb, func=AF.Exp),
         reads=("mconsts",), writes=("A1",))
    b.op("vector", lambda E: E.tensor_scalar(out=A_sb, in0=A_tmp, scalar1=-1.0,
                                             scalar2=None, op0=ALU.mult),
         reads=("A1", "mconsts"), writes=("A2",))
    b.op("scalar", lambda E: E.copy(out=A_tmp, in_=A_sb),
         reads=("A2",), writes=("A_sb",))
    b.op("vector", lambda E: E.memset(hst[:, :], 0.0), writes=("hstz",))
    for g in range(16):
        b.op("vector", lambda E, g=g: E.memset(xi[:, g * (CHm + 3):
                                               g * (CHm + 3) + 3], 0.0),
             writes=(("xh", g),))

    for c in range(NMCH):
        t0 = c * CHm
        xs = 0
        for kc in range(8):
            b.dma("sync", xn2f[xs][:, kc * CHm:(kc + 1) * CHm],
                  io["xn2_in"][t0:t0 + CHm, kc * 128:(kc + 1) * 128],
                  transpose=True, writes=((f"xn2f{xs}",),))
        for mt in range(32):
            bank = b.psum[mt % 4]
            bres = f"pm{mt % 4}"
            b.op("tensor", lambda E, bank=bank, mt=mt, xs=xs: [
                 E.matmul(bank[:, 0:CHm],
                          ipw[:, kc * 4096 + mt * 128:kc * 4096 + (mt + 1) * 128],
                          xn2f[xs][:, kc * CHm:(kc + 1) * CHm],
                          start=(kc == 0), stop=(kc == 7))
                 for kc in range(8)][-1],
                 reads=("ipw", (f"xn2f{xs}",)), writes=(bres,))
            if mt < 16:
                g = mt
                b.op("vector", lambda E, bank=bank, g=g:
                     E.tensor_copy(out=xi[:, g * (CHm + 3) + 3:
                                          (g + 1) * (CHm + 3)],
                                   in_=bank[:, 0:CHm]),
                     reads=(bres,), writes=((f"xi", g),))
            else:
                g = mt - 16
                b.op("scalar", lambda E, bank=bank, g=g:
                     E.activation(out=zs[:, g * CHm:(g + 1) * CHm],
                                  in_=bank[:, 0:CHm], func=AF.Silu),
                     reads=(bres,), writes=((f"zs", g),))
        for g in range(16):
            x0 = g * (CHm + 3)
            for k in range(2):
                b.op("scalar", lambda E, g=g, k=k, x0=x0:
                     E.activation(out=spt[k], in_=xi[:, x0 + k:x0 + k + CHm],
                                  func=AF.Copy, scale=cw[:, g * 4 + k:g * 4 + k + 1]),
                     reads=((f"xi", g), ("xh", g), "mconsts"),
                     writes=((f"tap{k}",),))
            b.op("vector", lambda E:
                 E.tensor_tensor(out=ypp[0], in0=spt[0], in1=spt[1], op=ALU.add),
                 reads=(("tap0",), ("tap1",)), writes=(("cv01",),))
            for k in range(2, 4):
                b.op("scalar", lambda E, g=g, k=k, x0=x0:
                     E.activation(out=spt[k - 2], in_=xi[:, x0 + k:x0 + k + CHm],
                                  func=AF.Copy, scale=cw[:, g * 4 + k:g * 4 + k + 1]),
                     reads=((f"xi", g), ("xh", g), "mconsts"),
                     writes=((f"tap{k}",),))
            b.op("vector", lambda E:
                 E.tensor_tensor(out=ypp[1], in0=spt[0], in1=spt[1], op=ALU.add),
                 reads=(("tap2",), ("tap3",)), writes=(("cv23",),))
            b.op("vector", lambda E:
                 E.tensor_tensor(out=spt[0], in0=ypp[0], in1=ypp[1], op=ALU.add),
                 reads=(("cv01",), ("cv23",)), writes=(("cvs",),))
            b.op("scalar", lambda E, g=g:
                 E.activation(out=xc[:, g * CHm:(g + 1) * CHm], in_=spt[0],
                              func=AF.Silu, bias=cb[:, g:g + 1]),
                 reads=(("cvs",), "mconsts"), writes=((f"xc", g),))
            b.op("vector", lambda E, g=g, x0=x0:
                 E.tensor_copy(out=xi[:, x0:x0 + 3],
                               in_=xi[:, x0 + CHm:x0 + CHm + 3]),
                 reads=((f"xi", g),), writes=(("xh", g),))
        xc_reads = tuple((f"xc", g) for g in range(16))
        for gi, (c0, c1, M) in enumerate(((0, 64, 64), (64, 80, 16),
                                          (80, 96, 16))):
            bank = b.psum[4 + gi]
            bres = f"pm{4 + gi}"
            b.op("tensor", lambda E, bank=bank, c0=c0, c1=c1, M=M: [
                 E.matmul(bank[0:M, 0:CHm], xpw[:, kc * 96 + c0:kc * 96 + c1],
                          xc[:, kc * CHm:(kc + 1) * CHm],
                          start=(kc == 0), stop=(kc == 15))
                 for kc in range(16)][-1],
                 reads=("xpw",) + xc_reads, writes=(bres,))
        b.op("vector", lambda E: E.tensor_copy(out=dtr, in_=b.psum[4][0:64, 0:CHm]),
             reads=("pm4",), writes=("dtr",))
        b.op("vector", lambda E: E.tensor_copy(out=bcbB[0:16, :],
                                               in_=b.psum[5][0:16, 0:CHm]),
             reads=("pm5",), writes=("bcb",))
        b.op("vector", lambda E: E.tensor_copy(out=bcbC[0:16, :],
                                               in_=b.psum[6][0:16, 0:CHm]),
             reads=("pm6",), writes=("bcb2",))
        for g in range(16):
            bank = b.psum[4 + (g % 2)]
            bres = f"pm{4 + (g % 2)}"
            b.op("tensor", lambda E, bank=bank, g=g:
                 E.matmul(bank[:, 0:CHm], dtw[0:64, g * 128:(g + 1) * 128], dtr,
                          start=True, stop=True),
                 reads=("dtw", "dtr"), writes=(bres,))
            b.op("scalar", lambda E, bank=bank, g=g:
                 E.activation(out=spt[0], in_=bank[:, 0:CHm], func=AF.Identity,
                              bias=dtb[:, g:g + 1]),
                 reads=(bres, "mconsts"), writes=(("sp0",),))
            b.op("scalar", lambda E:
                 E.activation(out=spt[1], in_=spt[0], func=AF.Abs),
                 reads=(("sp0",),), writes=(("sp1",),))
            b.op("scalar", lambda E:
                 E.activation(out=ypp[0], in_=spt[1], func=AF.Exp, scale=-1.0),
                 reads=(("sp1",),), writes=(("sp2",),))
            b.op("scalar", lambda E:
                 E.activation(out=spt[1], in_=ypp[0], func=AF.Ln, bias=1.0),
                 reads=(("sp2",),), writes=(("sp1b",),))
            b.op("scalar", lambda E:
                 E.activation(out=ypp[1], in_=spt[0], func=AF.Relu),
                 reads=(("sp0",),), writes=(("sp3",),))
            b.op("vector", lambda E, g=g:
                 E.tensor_tensor(out=dt[:, g * CHm:(g + 1) * CHm], in0=ypp[1],
                                 in1=spt[1], op=ALU.add),
                 reads=(("sp3",), ("sp1b",)), writes=((f"dt", g),))
        for s in range(NS):
            bank = b.psum[6 + (s % 2)]
            bres = f"pm{6 + (s % 2)}"
            b.op("tensor", lambda E, bank=bank, s=s:
                 E.matmul(bank[:, 0:CHm], selm[0:16, s * 128:(s + 1) * 128],
                          bcbB[0:16, 0:CHm], start=True, stop=True),
                 reads=("selm", "bcb"), writes=(bres,))
            b.op("vector", lambda E, bank=bank, s=s:
                 E.tensor_copy(out=Bsb[:, s * CHm:(s + 1) * CHm],
                               in_=bank[:, 0:CHm]),
                 reads=(bres,), writes=((f"Bs", s),))
            bank2 = b.psum[6 + ((s + 1) % 2)]
            bres2 = f"pm{6 + ((s + 1) % 2)}"
            b.op("tensor", lambda E, bank=bank2, s=s:
                 E.matmul(bank[:, 0:CHm], selm[0:16, s * 128:(s + 1) * 128],
                          bcbC[0:16, 0:CHm], start=True, stop=True),
                 reads=("selm", "bcb2"), writes=(bres2,))
            b.op("vector", lambda E, bank=bank2, s=s:
                 E.tensor_copy(out=Csb[:, s * CHm:(s + 1) * CHm],
                               in_=bank[:, 0:CHm]),
                 reads=(bres2,), writes=((f"Cs", s),))
        for g in range(16):
            b.op("vector", lambda E, g=g:
                 E.tensor_tensor(out=dug, in0=dt[:, g * CHm:(g + 1) * CHm],
                                 in1=xc[:, g * CHm:(g + 1) * CHm], op=ALU.mult),
                 reads=((f"dt", g), (f"xc", g)), writes=(("dug",),))
            b.op("vector", lambda E: E.memset(ypp[0][:, :], 0.0),
                 writes=(("yp", 0),))
            def sched(s, g=g):
                if s < NS:
                    b.op("gpsimd", lambda E, s=s:
                         E.tensor_tensor(out=bsl[s % 2], in0=dug,
                                         in1=Bsb[:, s * CHm:(s + 1) * CHm],
                                         op=ALU.mult),
                         reads=(("dug",), (f"Bs", s)), writes=((f"bsl", s % 2),))
                    b.op("scalar", lambda E, s=s, g=g:
                         E.activation(out=asl[s % 2],
                                      in_=dt[:, g * CHm:(g + 1) * CHm],
                                      func=AF.Exp,
                                      scale=A_sb[:, g * NS + s:g * NS + s + 1]),
                         reads=((f"dt", g), "A2"), writes=((f"asl", s % 2),))
                    b.op("vector", lambda E, s=s, g=g:
                         E.tensor_tensor_scan(out=hsl[s % 4], data0=asl[s % 2],
                                              data1=bsl[s % 2],
                                              initial=hst[:, g * NS + s:
                                                          g * NS + s + 1],
                                              op0=ALU.mult, op1=ALU.add),
                         reads=((f"asl", s % 2), (f"bsl", s % 2),
                                ("hst", g, s)),
                         writes=((f"hsl", s % 4),))
                sp = s - 2
                if 0 <= sp < NS:
                    b.op("vector", lambda E, sp=sp:
                         E.tensor_tensor(out=ymb, in0=hsl[sp % 4],
                                         in1=Csb[:, sp * CHm:(sp + 1) * CHm],
                                         op=ALU.mult),
                         reads=((f"hsl", sp % 4), (f"Cs", sp)), writes=("ymb",))
                    b.op("vector", lambda E, sp=sp:
                         E.tensor_tensor(out=ypp[(sp + 1) % 2], in0=ypp[sp % 2],
                                         in1=ymb, op=ALU.add),
                         reads=("ymb", ("yp", sp % 2)),
                         writes=(("yp", (sp + 1) % 2),))
                    b.op("scalar", lambda E, sp=sp, g=g:
                         E.copy(out=hst[:, g * NS + sp:g * NS + sp + 1],
                                in_=hsl[sp % 4][:, CHm - 1:CHm]),
                         reads=((f"hsl", sp % 4),), writes=(("hst", g, sp),))
            for s in range(NS + 2):
                sched(s)
            b.op("vector", lambda E, g=g:
                 E.tensor_scalar(out=spt[0], in0=xc[:, g * CHm:(g + 1) * CHm],
                                 scalar1=dsk[:, g:g + 1], scalar2=None,
                                 op0=ALU.mult),
                 reads=((f"xc", g), "mconsts"), writes=(("sp0",),))
            b.op("vector", lambda E, g=g:
                 E.tensor_tensor(out=spt[1], in0=ypp[0], in1=spt[0], op=ALU.add),
                 reads=(("yp", 0), ("sp0",)), writes=(("sp1",),))
            b.op("vector", lambda E, g=g:
                 E.tensor_tensor(out=yg[:, g * CHm:(g + 1) * CHm], in0=spt[1],
                                 in1=zs[:, g * CHm:(g + 1) * CHm], op=ALU.mult),
                 reads=(("sp1",), (f"zs", g)), writes=((f"yg", g),))
        yg_reads = tuple((f"yg", g) for g in range(16))
        for nc_ in range(2):
            for kc2 in range(16):
                b.dma("scalar", opwh[:, kc2 * 512:(kc2 + 1) * 512],
                      scrm["opw"][kc2 * 128:(kc2 + 1) * 128,
                                  nc_ * 512:(nc_ + 1) * 512],
                      reads=(("scr", "opw"),), writes=("opwh",))
            for tt in range(CT2):
                if nc_ == 0:
                    b.dma("sync", xn0t[tt],
                          io["xn0_in"][t0 + tt * 128:t0 + (tt + 1) * 128, :],
                          writes=((f"xn0t", tt),))
                bank = b.psum[nc_ * 2 + tt]
                bres = f"pm{nc_ * 2 + tt}"
                b.op("tensor", lambda E, bank=bank, tt=tt: [
                     E.matmul(bank[:, 0:512],
                              yg[:, kc2 * CHm + tt * 128:
                                 kc2 * CHm + (tt + 1) * 128],
                              opwh[:, kc2 * 512:(kc2 + 1) * 512],
                              start=(kc2 == 0), stop=(kc2 == 15))
                     for kc2 in range(16)][-1],
                     reads=yg_reads + ("opwh",), writes=(bres,))
                b.op("vector", lambda E, bank=bank, tt=tt, nc_=nc_:
                     E.tensor_tensor(out=s0t[tt][:, nc_ * 512:(nc_ + 1) * 512],
                                     in0=bank[:, 0:512],
                                     in1=xn0t[tt][:, nc_ * 512:(nc_ + 1) * 512],
                                     op=ALU.add),
                     reads=(bres, (f"xn0t", tt)), writes=((f"s0t", tt, nc_),))
        for tt in range(CT2):
            b.dma("sync", io["s_out"][t0 + tt * 128:t0 + (tt + 1) * 128, :],
                  s0t[tt], reads=((f"s0t", tt, 0), (f"s0t", tt, 1)))
    al.reset(mark)





def _finalize(b):
    for cn in ("q_sync", "q_gp", "q_scl"):
        if cn in b.sems:
            b._wait("sync", cn, b.sems[cn].n)
    b.emit()


def _build_fused():
    b = B()
    al = b.al
    dummy = al.tile("bar_dummy", [128, 16], F32)

    x_in = b.din("x_in", [T, D], F32)
    out = b.dout("out", [T, D], F32)
    xn0 = b.dint("xn0", [T, D], BF16)
    xn2 = b.dint("xn2", [T, D], BF16)
    s0 = b.dint("s0", [T, D], F32)
    acc0 = b.dint("acc0", [T, D], F32)
    x1 = b.dint("x1", [T, D], F32)
    xn1 = b.dint("xn1", [T, D], BF16)
    s1 = b.dint("s1", [T, D], F32)
    acc1 = b.dint("acc1", [T, D], F32)

    scr_m = {"ipw": b.dint("scr_ipw", [D, 2 * DI], BF16),
             "xpw": b.dint("scr_xpw", [DI, 96], BF16),
             "dtw": b.dint("scr_dtw", [DTR, DI], BF16),
             "opw": b.dint("scr_opw", [DI, D], BF16)}
    scr_f0 = {"gate_w": b.dint("scrT_gw0", [1024, 16], BF16),
              "w13_s": b.dint("scrT_w13s", [1024, 1024], BF16),
              "w2_s": b.dint("scrT_w2s", [512, 1024], BF16),
              "w13": b.dint("scrT_w13", [16, 1024, 1024], BF16),
              "w2": b.dint("scrT_w2", [16, 512, 1024], BF16)}
    scr_a1 = {"proj": b.dint("scrT_proj", [1024, 1024], BF16),
              "fc1": b.dint("scrT_fc1", [1024, 4096], BF16),
              "fc2": b.dint("scrT_fc2", [4096, 1024], BF16)}
    scr_f1 = {"gate_w": b.dint("scrT_gw1", [1024, 64], BF16),
              "sg": b.dint("scrT_sg", [1024, 1024], BF16),
              "su": b.dint("scrT_su", [1024, 1024], BF16),
              "sd": b.dint("scrT_sd", [1024, 1024], BF16),
              "eg": b.dint("scrT_eg", [64, 1024, 512], BF16),
              "eu": b.dint("scrT_eu", [64, 1024, 512], BF16),
              "ed": b.dint("scrT_ed", [64, 512, 1024], BF16)}
    w13T = b.din("w13T", [16, 1024, 1024], F32)
    w2T = b.din("w2T", [16, 512, 1024], F32)
    egT = b.din("egT", [64, 1024, 512], F32)
    euT = b.din("euT", [64, 1024, 512], F32)
    edT = b.din("edT", [64, 512, 1024], F32)
    for nm, shp in (("ipw", [D, 2 * DI]), ("xpw", [DI, 96]),
                    ("dtw", [DTR, DI]), ("opw", [DI, D])):
        b.dma("gpsimd", scr_m[nm], b.din(f"m_{nm}", shp, F32),
              writes=(("scr", nm),))
    b.dma("gpsimd", scr_f0["gate_w"], b.din("f0_gate_w", [1024, 16], F32),
          writes=(("scr", "gw0"),))
    b.dma("gpsimd", scr_f0["w13_s"], b.din("f0_w13_s", [1024, 1024], F32),
          writes=(("scr", "w13_s"),))
    b.dma("gpsimd", scr_f0["w2_s"], b.din("f0_w2_s", [512, 1024], F32),
          writes=(("scr", "w2_s"),))
    for e in range(16):
        b.dma("gpsimd", scr_f0["w13"][e], w13T[e], writes=(("scr", "w13", e),))
        b.dma("gpsimd", scr_f0["w2"][e], w2T[e], writes=(("scr", "w2", e),))
    for nm, shp in (("proj", [1024, 1024]), ("fc1", [1024, 4096]),
                    ("fc2", [4096, 1024])):
        b.dma("gpsimd", scr_a1[nm], b.din(f"a1_{nm}", shp, F32),
              writes=(("scr", nm),))
    b.dma("gpsimd", scr_f1["gate_w"], b.din("f1_gate_w", [1024, 64], F32),
          writes=(("scr", "gw1"),))
    for nm in ("sg", "su", "sd"):
        b.dma("gpsimd", scr_f1[nm], b.din(f"f1_{nm}", [1024, 1024], F32),
              writes=(("scr", nm),))
    for e in range(64):
        b.dma("gpsimd", scr_f1["eg"][e], egT[e], writes=(("scr", "eg", e),))
        b.dma("gpsimd", scr_f1["eu"][e], euT[e], writes=(("scr", "eu", e),))
        b.dma("gpsimd", scr_f1["ed"][e], edT[e], writes=(("scr", "ed", e),))

    io = {"x_in": x_in, "xn_out": xn0, "xn2_out": xn2,
          "ln_w_rep": b.din("ln_w_rep", [128, D], F32),
          "ln_b_rep": b.din("ln_b_rep", [128, D], F32)}
    emit_ln_phase(b, io, first=True, make_xn2=True, final=False)
    b.barrier(dummy)

    io = {"xn2_in": xn2, "xn0_in": xn0, "s_out": s0, "scr": scr_m,
          "alog_t": b.din("alog_t", [128, 256], F32),
          "cw_t": b.din("cw_t", [128, 64], F32),
          "cb_t": b.din("cb_t", [128, 16], F32),
          "dtb_t": b.din("dtb_t", [128, 16], F32),
          "dsk_t": b.din("dsk_t", [128, 16], F32),
          "selmat": b.din("selmat", [32, 32 * 128], BF16)}
    emit_mamba(b, io)
    b.barrier(dummy)

    io = {"xn_in": xn0, "acc_out": acc0,
          "gate_b_rep": b.din("gate_b_rep0", [128, 16], F32),
          "scrT": scr_f0}
    emit_ffn(b, io, layer=0)
    b.barrier(dummy)

    io = {"x_in": x_in, "s_in": s0, "acc_in": acc0, "x_out": x1,
          "xn_out": xn1}
    emit_ln_phase(b, io, first=False, make_xn2=False, final=False)
    b.barrier(dummy)

    io = {"xn_in": xn1, "s_out": s1,
          "g1_rep": b.din("g1_rep", [128, D], F32),
          "g2_rep": b.din("g2_rep", [128, D], F32),
          "l1g_rep": b.din("l1g_rep", [128, D], F32),
          "l1b_rep": b.din("l1b_rep", [128, D], F32),
          "l2g_rep": b.din("l2g_rep", [128, D], F32),
          "l2b_rep": b.din("l2b_rep", [128, D], F32),
          "lnh_g_t": b.din("lnh_g_t", [128, 32], F32),
          "lnh_b_t": b.din("lnh_b_t", [128, 32], F32),
          "fc1_b_t": b.din("fc1_b_t", [128, 32], F32),
          "pb_rep": b.din("pb_rep", [128, D], F32),
          "f2b_rep": b.din("f2b_rep", [128, D], F32),
          "ones_row": b.din("ones_row", [1, 128], BF16),
          "ones_col": b.din("ones_col", [128, 1], BF16),
          "y1tm": b.dint("y1tm", [T, D], BF16),
          "ztm": b.dint("ztm", [T, D], BF16),
          "scrT": scr_a1}
    emit_attn1(b, io)
    b.barrier(dummy)

    io = {"xn_in": xn1, "acc_out": acc1,
          "gate_b_rep": b.din("gate_b_rep1", [128, 64], F32),
          "scrT": scr_f1}
    emit_ffn(b, io, layer=1)
    b.barrier(dummy)

    io = {"x_in": x1, "s_in": s1, "acc_in": acc1, "x_out": out}
    emit_ln_phase(b, io, first=False, make_xn2=False, final=True)
    _finalize(b)
    return b



def _rep(v):
    v = np.ascontiguousarray(np.asarray(v, np.float32))
    return np.ascontiguousarray(np.broadcast_to(v[None, :], (128, v.shape[0])))


def _t16(v, k):
    return np.ascontiguousarray(np.asarray(v, np.float32).reshape(k, 128).T)


def _T(a):
    return np.ascontiguousarray(np.asarray(a, np.float32).T)


def _T3(a):
    return np.ascontiguousarray(np.asarray(a, np.float32).transpose(0, 2, 1))


def _run(b, in_maps):
    res = run_bass_kernel_spmd(b.nc, in_maps, core_ids=list(range(len(in_maps))))
    return res.results


def _kernel_fused(inp):
    x = np.ascontiguousarray(np.asarray(inp["L0_attn_x"], np.float32))
    alog_t = np.ascontiguousarray(
        np.asarray(inp["L0_attn_A_log"], np.float32)
        .reshape(16, 128, 16).transpose(1, 0, 2).reshape(128, 256))
    cw_t = np.ascontiguousarray(
        np.asarray(inp["L0_attn_conv_w"], np.float32)
        .reshape(16, 128, 4).transpose(1, 0, 2).reshape(128, 64))
    selmat = np.zeros((32, 32 * 128), dtype=BF)
    for j in range(32):
        selmat[j, j * 128:(j + 1) * 128] = 1
    common = {
        "ln_w_rep": _rep(inp["L0_attn_ln_w"]), "ln_b_rep": _rep(inp["L0_attn_ln_b"]),
        "m_ipw": inp["L0_attn_in_proj_w"], "m_xpw": inp["L0_attn_x_proj_w"],
        "m_dtw": inp["L0_attn_dt_proj_w"], "m_opw": inp["L0_attn_out_proj_w"],
        "alog_t": alog_t, "cw_t": cw_t, "cb_t": _t16(inp["L0_attn_conv_b"], 16),
        "dtb_t": _t16(inp["L0_attn_dt_proj_b"], 16),
        "dsk_t": _t16(inp["L0_attn_D_skip"], 16), "selmat": selmat,
        "gate_b_rep0": _rep(inp["L0_ffn_gate_b"]),
        "f0_gate_w": _T(inp["L0_ffn_gate_w"]),
        "f0_w13_s": _T(inp["L0_ffn_w13_s"]), "f0_w2_s": _T(inp["L0_ffn_w2_s"]),
        "w13T": _T3(inp["L0_ffn_w13"]), "w2T": _T3(inp["L0_ffn_w2"]),
        "g1_rep": _rep(inp["L1_attn_gamma1"]), "g2_rep": _rep(inp["L1_attn_gamma2"]),
        "l1g_rep": _rep(inp["L1_attn_ln1_g"]), "l1b_rep": _rep(inp["L1_attn_ln1_b"]),
        "l2g_rep": _rep(inp["L1_attn_ln2_g"]), "l2b_rep": _rep(inp["L1_attn_ln2_b"]),
        "lnh_g_t": _t16(inp["L1_attn_lnh_g"], 32),
        "lnh_b_t": _t16(inp["L1_attn_lnh_b"], 32),
        "fc1_b_t": _t16(inp["L1_attn_fc1_b"], 32),
        "pb_rep": _rep(inp["L1_attn_proj_b"]), "f2b_rep": _rep(inp["L1_attn_fc2_b"]),
        "ones_row": np.ones((1, 128), dtype=BF),
        "ones_col": np.ones((128, 1), dtype=BF),
        "a1_proj": _T(inp["L1_attn_proj_w"]), "a1_fc1": _T(inp["L1_attn_fc1_w"]),
        "a1_fc2": _T(inp["L1_attn_fc2_w"]),
        "gate_b_rep1": _rep(inp["L1_ffn_gate_bias"]),
        "f1_gate_w": _T(inp["L1_ffn_gate_w"]),
        "f1_sg": _T(inp["L1_ffn_sg_w"]), "f1_su": _T(inp["L1_ffn_su_w"]),
        "f1_sd": _T(inp["L1_ffn_sd_w"]),
        "egT": _T3(inp["L1_ffn_eg_w"]), "euT": _T3(inp["L1_ffn_eu_w"]),
        "edT": _T3(inp["L1_ffn_ed_w"]),
    }
    b = _build_fused()
    res = _run(b, [dict(common, x_in=x[i]) for i in range(8)])
    return np.stack([np.asarray(r["out"], np.float32) for r in res], axis=0)


def kernel(**inp):
    return _kernel_fused(inp)
